# version 12
# baseline (speedup 1.0000x reference)
import contextlib
import numpy as np
import ml_dtypes
import concourse.bass as bass
import concourse.mybir as mybir
import bass_rust
from concourse import library_config
from concourse.bass_utils import run_bass_kernel_spmd

F32 = mybir.dt.float32
BF16 = mybir.dt.bfloat16
AF = mybir.ActivationFunctionType
ALU = mybir.AluOpType
AX = mybir.AxisListType
RED = bass_rust.ReduceOp

NB, NSEG, NT = 4, 4, 4
D, DC, H, DH = 1024, 8, 16, 64
FF, FC = 3072, 24
NTT, NG = 64, 16
MAGIC = float(np.float32(1.5 * 2 ** 23))
EPS_LN = 1e-5
EPS_RMS = 1.1920929e-07
CLIP = 1e12

ENGINES = ("sync", "tensor", "vector", "scalar", "gpsimd")


def host_constants():
    c = {}
    c["ident"] = np.eye(128, dtype=ml_dtypes.bfloat16)
    j = np.arange(128)[:, None]
    i = np.arange(128)[None, :]
    c["maskmul"] = (j <= i).astype(np.float32)
    pos = np.arange(2048, dtype=np.float64)
    inv = 1.0 / 10000.0 ** (np.arange(0, DH, 2, dtype=np.float64) / DH)
    ang = pos[:, None] * inv[None, :]
    c["costab"] = np.cos(ang).astype(ml_dtypes.bfloat16).reshape(16, 128, 32) \
        .transpose(1, 0, 2).reshape(128, 512).copy()
    c["sintab"] = np.sin(ang).astype(ml_dtypes.bfloat16).reshape(16, 128, 32) \
        .transpose(1, 0, 2).reshape(128, 512).copy()
    ind = np.zeros((16, 1024), dtype=ml_dtypes.bfloat16)
    for h in range(16):
        ind[h, (h // 2) * 128 + (h % 2) * 64: (h // 2) * 128 + (h % 2) * 64 + 64] = 1.0
    c["indsel"] = ind
    return c


def qk_row_perm():
    per_head = np.concatenate([np.arange(0, DH, 2), np.arange(1, DH, 2)])
    one = np.concatenate([h * DH + per_head for h in range(H)])
    return np.concatenate([one, 1024 + one, 2048 + np.arange(1024)])


def build_nc(n_cores=8, max_phase=7, stub_cc=False, p2opts='dmae'):
    nc = bass.Bass("TRN2", target_bir_lowering=False)

    seq_d = nc.dram_tensor("seq", [NB, 2048, D], F32, kind="ExternalInput").ap()
    wqkv_d = nc.dram_tensor("wqkv", [3 * D, D], F32, kind="ExternalInput").ap()
    wout_d = nc.dram_tensor("wout", [D, D], F32, kind="ExternalInput").ap()
    wup_d = nc.dram_tensor("wup", [FF, D], F32, kind="ExternalInput").ap()
    wgate_d = nc.dram_tensor("wgate", [FF, D], F32, kind="ExternalInput").ap()
    wdown_d = nc.dram_tensor("wdown", [D, FF], F32, kind="ExternalInput").ap()
    g_d = nc.dram_tensor("gvec", [128, DC], F32, kind="ExternalInput").ap()
    pmkf_d = nc.dram_tensor("pmkf", [128, 256], F32, kind="ExternalInput").ap()
    pmvst_d = nc.dram_tensor("pmvst", [16, 1040], F32, kind="ExternalInput").ap()
    ident_d = nc.dram_tensor("identc", [128, 128], BF16, kind="ExternalInput").ap()
    identf_d = nc.dram_tensor("identf", [128, 128], F32, kind="ExternalInput").ap()
    mask_d = nc.dram_tensor("maskmul", [128, 128], F32, kind="ExternalInput").ap()
    cos_d = nc.dram_tensor("costab", [128, 512], BF16, kind="ExternalInput").ap()
    sin_d = nc.dram_tensor("sintab", [128, 512], BF16, kind="ExternalInput").ap()
    ind_d = nc.dram_tensor("indsel", [16, 1024], BF16, kind="ExternalInput").ap()
    y_d = nc.dram_tensor("y", [NB, 2048, D], F32, kind="ExternalOutput").ap()

    x2_d = nc.dram_tensor("X2s", [NTT, 128, D], F32).ap()
    sig_d = nc.dram_tensor("SIGs", [NG, 128, 8], F32).ap()
    u_d = nc.dram_tensor("Usc", [FC, NG, 128, 512], BF16).ap()
    gg_d = nc.dram_tensor("Gsc", [FC, NG, 128, 512], BF16).ap()
    t_d = nc.dram_tensor("Tsc", [FC, NG, 128, 512], BF16).ap()
    o_d = nc.dram_tensor("Osc", [FC, NG, 128, 512], BF16).ap()
    cci = [nc.dram_tensor(f"cci{k}", [1, 8], F32).ap() for k in range(5)]
    cco = [nc.dram_tensor(f"cco{k}", [1, 8], F32).ap() for k in range(5)]
    rs_d = nc.dram_tensor("RSb", [16, 512], BF16).ap()

    prog = {e: [] for e in ENGINES}
    cnt = {}

    def W(eng, semname, val):
        if val and val > 0:
            prog[eng].append(("w", semname, val))

    def OP(eng, fn, inc=None, n=1):
        tot = None
        if inc is not None:
            cnt[inc] = cnt.get(inc, 0) + n
            tot = cnt[inc]
        prog[eng].append(("o", fn, inc, n))
        return tot

    def DR(eng):
        prog[eng].append(("d",))

    def CV(semname):
        return cnt.get(semname, 0)

    ctx = contextlib.ExitStack()
    sbt = lambda name, shape, dt: ctx.enter_context(nc.sbuf_tensor(name, shape, dt))
    pst = lambda name, shape, dt: ctx.enter_context(nc.psum_tensor(name, shape, dt))

    with ctx:
        ident = sbt("identS", [128, 128], BF16)
        identf = sbt("identfS", [128, 128], F32)
        onesrow = sbt("onesrowS", [1, 128], F32)
        mcol8 = sbt("mcol8S", [1, 8], F32)
        mg8 = sbt("mg8S", [1, 8], F32)
        mask = sbt("maskS", [128, 128], F32)
        indsel = sbt("indselS", [16, 1024], BF16)
        gsb = sbt("gsbS", [128, DC], F32)
        costab = sbt("costabS", [128, 512], BF16)
        sintab = sbt("sintabS", [128, 512], BF16)
        mx = sbt("mxS", [128, 1088], F32)
        MX_X, MX_U, MX_G, MX_T, MX_O = 0, 64, 448, 832, 928
        scl = sbt("sclS", [128, 16], F32)
        SXI, SX, SUI, SU, SGI, SG, SGQ, STI, STU, SOI, SO = range(11)
        cw = sbt("cwS", [128, 64], F32)
        st6 = sbt("st6S", [128, 12], F32)
        mvt = sbt("mvtS", [128, 2], F32)
        mvt4 = sbt("mvt4S", [128, 8], F32)
        sgst = sbt("sgstS", [128, 8], F32)
        cmag = sbt("cmagS", [128, 2], F32)

        A = [pst(f"psA{i}", [128, 512], F32) for i in range(6)]
        Bp = pst("psB", [128, 1024], BF16)
        Cp = pst("psC", [128, 512], F32)

        sems = {}
        for s in ("LC", "LW0", "LW1", "LW2", "LSEQ", "LX0", "LX1", "LWS0", "LWS1",
                  "LWD", "LG0", "LG1", "LU0", "LU1", "LXG", "LRS", "LRS0", "LRS1",
                  "LO0", "LO1", "LO2", "LO3", "LO4", "LO5", "LO6", "LO7",
                  "GD", "TD", "VD", "SD", "PD", "CC",
                  "SX0", "SX1", "SSG", "SG0", "SG1", "SG2", "SG3",
                  "STB0", "STB1", "SY0", "SY1"):
            sems[s] = ctx.enter_context(nc.semaphore(s))

        p1 = contextlib.ExitStack()
        s1 = lambda name, shape, dt: p1.enter_context(nc.sbuf_tensor(name, shape, dt))
        wqkvT = s1("wqkvT", [128, 24576], BF16)
        woutT = s1("woutT", [128, 8192], BF16)
        pmkf = s1("pmkfS", [128, 256], BF16)
        pmvst = s1("pmvstS", [16, 1040], BF16)
        pstage = contextlib.ExitStack()
        wldf = pstage.enter_context(nc.sbuf_tensor("wldf", [128, 3072], F32))
        wcst = pstage.enter_context(nc.sbuf_tensor("wcst", [128, 2048], BF16))
        pstage.close()
        seqsb = s1("seqsb", [128, 8192], F32)
        zbf = s1("zbf", [128, 4096], BF16)
        zfm = s1("zfm", [128, 4096], BF16)
        qksb = s1("qksb", [128, 8192], BF16)
        qfm = s1("qfm", [128, 4096], BF16)
        kfm = s1("kfm", [128, 4096], BF16)
        vst = s1("vst", [128, 4160], BF16)
        expt = s1("expt", [128, 3584], BF16)
        aoun = s1("aoun", [128, 4096], BF16)
        rsum = s1("rsum", [16, 512], BF16)
        sumsb = s1("sumsb", [1, 1024], BF16)
        rxp = s1("rxp", [128, 4096], BF16)
        x1t = s1("x1t", [128, 1024], F32)
        x2t = s1("x2t", [128, 1024], F32)
        rta = s1("rta", [128, 256], BF16)
        qke = s1("qke", [128, 2048], BF16)
        rtb = s1("rtb", [128, 256], BF16)
        p1.close()

        p2 = contextlib.ExitStack()
        s2 = lambda name, shape, dt: p2.enter_context(nc.sbuf_tensor(name, shape, dt))
        xqfm = s2("xqfm", [128, 65536], BF16)
        wstr = s2("wstr", [128, 2048], F32)
        mxb32 = s2("mxb32", [128, 32], F32)
        rmxb = s2("rmxb", [128, 32], F32)
        sinvb = s2("sinvb", [128, 32], F32)
        sblk = s2("sblkS", [128, 32], F32)
        wq1 = s2("wq1", [128, 1024], F32)
        wr2 = s2("wr2", [128, 1024], F32)
        wdqb = s2("wdqb", [128, 1024], BF16)
        wtr = s2("wtrS", [128, 2048], BF16)
        stg = s2("stgS", [128, 2048], BF16)
        p2a = contextlib.ExitStack()
        xin = p2a.enter_context(nc.sbuf_tensor("xinS", [128, 2048], F32))
        xt1 = p2a.enter_context(nc.sbuf_tensor("xt1S", [128, 1024], F32))
        xr2 = p2a.enter_context(nc.sbuf_tensor("xr2S", [128, 1024], F32))
        xqtm = p2a.enter_context(nc.sbuf_tensor("xqtm", [128, 1024], BF16))
        p2a.close()
        p2b = contextlib.ExitStack()
        wdnin = p2b.enter_context(nc.sbuf_tensor("wdnin", [128, 1536], F32))
        wdn1 = p2b.enter_context(nc.sbuf_tensor("wdn1", [128, 1536], F32))
        wdnr = p2b.enter_context(nc.sbuf_tensor("wdnr", [128, 1536], F32))
        wdnb = p2b.enter_context(nc.sbuf_tensor("wdnb", [128, 1536], BF16))
        mxb48 = p2b.enter_context(nc.sbuf_tensor("mxb48", [128, 48], F32))
        rmx48 = p2b.enter_context(nc.sbuf_tensor("rmx48", [128, 48], F32))
        siv48 = p2b.enter_context(nc.sbuf_tensor("siv48", [128, 48], F32))
        sbk48 = p2b.enter_context(nc.sbuf_tensor("sbk48", [128, 48], F32))
        p2b.close()
        p2.close()

        p2c = contextlib.ExitStack()
        s2c = lambda name, shape, dt: p2c.enter_context(nc.sbuf_tensor(name, shape, dt))
        wdT = s2c("wdT", [128, 24576], BF16)
        gin = s2c("ginS", [128, 4096], BF16)
        uin = s2c("uinS", [128, 4096], BF16)
        p34 = contextlib.ExitStack()
        s34 = lambda name, shape, dt: p34.enter_context(nc.sbuf_tensor(name, shape, dt))
        tb1 = s34("tb1S", [128, 4096], F32)
        rb3 = s34("rb3S", [128, 4096], F32)
        sigb = s34("sigbS", [128, 4096], F32)
        a3b = s34("a3bS", [128, 4096], F32)
        c2b = s34("c2bS", [128, 4096], F32)
        tob = s34("tobS", [128, 4096], BF16)
        p34.close()
        p4b = contextlib.ExitStack()
        s4b = lambda name, shape, dt: p4b.enter_context(nc.sbuf_tensor(name, shape, dt))
        oqb = s4b("oqbS", [128, 24576], BF16)
        x2g = s4b("x2gS", [128, 8192], F32)
        lnb = s4b("lnbS", [128, 1024], F32)
        x1b = s4b("x1bS", [128, 1024], F32)
        outb = s4b("outbS", [128, 2048], F32)
        ot1 = s4b("ot1S", [128, 512], F32)
        ot2 = s4b("ot2S", [128, 1024], F32)
        sg4b = s4b("sg4bS", [128, 16], F32)
        p4b.close()
        p2c.close()

        blk = ctx.enter_context(nc.Block())

        def dma(eng, sem, dst, src):
            return OP(eng, lambda e, d=dst, s=src: e.dma_start(out=d, in_=s), sem, 16)

        ld_const = 0
        for dst, srcap in ((ident[:], ident_d), (identf[:], identf_d),
                           (mask[:], mask_d), (indsel[:], ind_d),
                           (gsb[:], g_d),
                           (costab[:], cos_d), (sintab[:], sin_d)):
            ld_const = dma("sync", "LC", dst, srcap)
        W("tensor", "LC", ld_const)
        W("vector", "LC", ld_const)
        W("scalar", "LC", ld_const)
        gld_pm = dma("gpsimd", "GD", pmkf[:], pmkf_d)
        gld_pm = dma("gpsimd", "GD", pmvst[:], pmvst_d)
        OP("vector", lambda e: e.memset(onesrow[:], 1.0))
        OP("vector", lambda e: e.memset(mcol8[:], 0.0))
        OP("vector", lambda e: e.memset(cmag[:, 0:1], MAGIC))
        OP("vector", lambda e: e.memset(cmag[:, 1:2], -MAGIC))
        vst_ones = OP("vector", lambda e: e.memset(
            vst[:].rearrange("p (t h c) -> p t h c", h=16, c=65)[:, :, :, 64:65], 1.0), "VD")

        vcast_done = {}
        ttr_done = {}
        evac_done = {}
        prep_jobs = [("qkv", m) for m in range(24)] + [("out", m) for m in range(8)]
        for gi, (kind, m) in enumerate(prep_jobs):
            lslot = (gi % 3) * 1024
            cslot = (gi % 2) * 1024
            if gi >= 3:
                W("sync", "VD", vcast_done[gi - 3])
            src_rows = (wqkv_d if kind == "qkv" else wout_d)[m * 128:(m + 1) * 128, :]
            lt = dma("sync", "LW%d" % (gi % 3), wldf[:, lslot:lslot + 1024], src_rows)
            W("vector", "LW%d" % (gi % 3), lt)
            if gi >= 2:
                W("vector", "TD", ttr_done[gi - 2])
            vcast_done[gi] = OP("vector", lambda e, ls=lslot, cs=cslot: e.tensor_copy(
                wcst[:, cs:cs + 1024], wldf[:, ls:ls + 1024]), "VD")
            W("tensor", "VD", vcast_done[gi])
            if gi >= 1:
                W("tensor", "SD", evac_done[gi - 1])
            for k in range(DC):
                ttr_done[gi] = OP("tensor", lambda e, cs=cslot, kk=k: e.transpose(
                    Bp[:, kk * 128:kk * 128 + 128],
                    wcst[:, cs + kk * 128:cs + kk * 128 + 128], ident[:]),
                    "TD" if k == DC - 1 else None)
            W("scalar", "TD", ttr_done[gi])
            if kind == "qkv":
                for k in range(DC):
                    evac_done[gi] = OP("scalar", lambda e, mm=m, kk=k: e.activation(
                        wqkvT[:, kk * 3072 + mm * 128: kk * 3072 + mm * 128 + 128],
                        Bp[:, kk * 128:kk * 128 + 128], AF.Identity,
                        scale=gsb[:, kk:kk + 1]), "SD" if k == DC - 1 else None)
            else:
                evac_done[gi] = OP("scalar", lambda e, mm=m: e.copy(
                    woutT[:].rearrange("p (k x) -> p k x", x=1024)
                    [:, :, mm * 128:mm * 128 + 128],
                    Bp[:].rearrange("p (k x) -> p k x", x=128)), "SD")
        bp_free_sd = evac_done[len(prep_jobs) - 1]


        if max_phase >= 1:
            rope_done = {}
            rope_bank = {}
            qke_sd = {}
            vevac_done = {}
            zfm_done = {}
            qkv_done = {}
            exp_done = {}
            mask_done = {}
            av_done = {}
            x2_store_st = {}
            sg_store_st = 0
            woutln_vd = 0
            norm_vd = 0
            rec_done = 0
            ao_done = 0
            prev_norm_vd = 0
            last_rmm_td = 0
            rs_row_st = {}

            z_done_map = {}
            ld_seq_of = {}
            seq_par_read = {0: 0, 1: 0}

            def emit_AB(un):
                    bb, ss = un // 4, un % 4
                    sh = (un % 2) * 4096
                    if un == 0:
                        W("sync", "VD", CV("VD"))
                        W("sync", "TD", CV("TD"))
                    W("sync", "VD", seq_par_read[un % 2])
                    ld_seq = dma("sync", "LSEQ",
                                 seqsb[:, sh:sh + 4096].rearrange("p (t d) -> p t d", d=1024),
                                 seq_d[bb, ss * 512:(ss + 1) * 512, :]
                                 .rearrange("(t p) d -> p t d", p=128))
                    ld_seq_of[un] = ld_seq
                    if un:
                        W("vector", "SD", z_done_map[(un - 1, NT - 1)])
                        DR("vector")
                        DR("scalar")
                    bv1 = None
                    for t in range(NT):
                        c0 = 32 + t * 8
                        sq = seqsb[:, sh + t * 1024: sh + (t + 1) * 1024]
                        W("vector", "LSEQ", ld_seq)
                        if t:
                            DR("vector")
                        OP("vector", lambda e, sq=sq: e.bn_stats(st6[:, 0:6], sq[:, 0:512]))
                        OP("vector", lambda e, sq=sq: e.bn_stats(st6[:, 6:12], sq[:, 512:1024]))
                        DR("vector")
                        OP("vector", lambda e, t=t: e.bn_aggr(mvt4[:, 2 * t:2 * t + 2], st6[:]))
                        DR("vector")
                        OP("vector", lambda e, t=t, c0=c0: e.tensor_scalar(
                            cw[:, c0:c0 + 1], mvt4[:, 2 * t + 1:2 * t + 2],
                            EPS_LN, None, ALU.add))
                        DR("vector")
                        OP("vector", lambda e, c0=c0: e.reciprocal(cw[:, c0 + 1:c0 + 2],
                                                                   cw[:, c0:c0 + 1]))
                        DR("vector")
                        OP("vector", lambda e, t=t, c0=c0: e.tensor_tensor(
                            cw[:, c0 + 4:c0 + 5], mvt4[:, 2 * t + 1:2 * t + 2],
                            cw[:, c0 + 1:c0 + 2], ALU.mult))
                        DR("vector")
                        OP("vector", lambda e, c0=c0: e.tensor_scalar(
                            cw[:, c0 + 5:c0 + 6], cw[:, c0 + 4:c0 + 5],
                            EPS_RMS, None, ALU.add))
                        DR("vector")
                        bv1 = OP("vector", lambda e, c0=c0: e.reciprocal(
                            cw[:, c0 + 6:c0 + 7], cw[:, c0 + 5:c0 + 6]),
                            "VD" if t == NT - 1 else None)
                    W("scalar", "VD", bv1)
                    for t in range(NT):
                        c0 = 32 + t * 8
                        bs1 = OP("scalar", lambda e, c0=c0: e.activation(
                            cw[:, c0 + 2:c0 + 3], cw[:, c0 + 1:c0 + 2], AF.Sqrt))
                        bs1 = OP("scalar", lambda e, c0=c0: e.activation(
                            cw[:, c0 + 7:c0 + 8], cw[:, c0 + 6:c0 + 7], AF.Sqrt),
                            "SD" if t == NT - 1 else None)
                    W("vector", "SD", bs1)
                    bv2 = None
                    for t in range(NT):
                        c0 = 32 + t * 8
                        OP("vector", lambda e, c0=c0: e.tensor_tensor(
                            cw[:, c0 + 3:c0 + 4], cw[:, c0 + 2:c0 + 3],
                            cw[:, c0 + 7:c0 + 8], ALU.mult))
                        DR("vector")
                        bv2 = OP("vector", lambda e, t=t, c0=c0: e.tensor_scalar(
                            cw[:, c0 + 4:c0 + 5], mvt4[:, 2 * t:2 * t + 1],
                            cw[:, c0 + 3:c0 + 4], -1.0, ALU.mult, ALU.mult),
                            "VD" if t == NT - 1 else None)
                        if t != NT - 1:
                            DR("vector")
                    W("scalar", "VD", bv2)
                    for t in range(NT):
                        c0 = 32 + t * 8
                        sq = seqsb[:, sh + t * 1024: sh + (t + 1) * 1024]
                        if un and t == 0:
                            W("scalar", "TD", qkv_done[(un - 1, NT - 1)])
                        z_done_map[(un, t)] = OP("scalar", lambda e, t=t, sq=sq, c0=c0: e.activation(
                            zbf[:, t * 1024:(t + 1) * 1024], sq, AF.Identity,
                            scale=cw[:, c0 + 3:c0 + 4], bias=cw[:, c0 + 4:c0 + 5]), "SD")

            emit_AB(0)
            for u in range(16):
                b, s = u // 4, u % 4
                ld_seq = ld_seq_of[u]
                for t in range(NT):
                    gt = s * 4 + t
                    W("tensor", "SD", z_done_map[(u, t)])
                    W("tensor", "SD", bp_free_sd)
                    if u and t == 0:
                        W("tensor", "TD", qkv_done[(u - 1, NT - 1)])
                    for k in range(DC):
                        ztr = OP("tensor", lambda e, t=t, kk=k: e.transpose(
                            Bp[:, kk * 128:kk * 128 + 128],
                            zbf[:, t * 1024 + kk * 128: t * 1024 + kk * 128 + 128],
                            ident[:]), "TD" if k == DC - 1 else None)
                    W("scalar", "TD", ztr)
                    zfm_done[t] = OP("scalar", lambda e, t=t: e.copy(
                        zfm[:].rearrange("p (k x) -> p k x", x=512)[:, :, t * 128:t * 128 + 128],
                        Bp[:].rearrange("p (k x) -> p k x", x=128)), "SD")
                    bp_free_sd = zfm_done[t]
                    W("tensor", "SD", zfm_done[t])
                    pt = (u, t - 1) if t else ((u - 1, NT - 1) if u else None)
                    if u and t == 0:
                        W("tensor", "VD", woutln_vd)
                    for k in range(DC):
                        for jj in range(6):
                            if k == 0 and pt is not None:
                                if jj < 4:
                                    W("tensor", "SD", qke_sd[(pt[0], pt[1], jj)])
                                elif jj == 4:
                                    W("tensor", "SD", vevac_done[pt])
                            qkv_done[(u, t)] = OP("tensor", lambda e, t=t, kk=k, j=jj: e.matmul(
                                A[j][:],
                                zfm[:, kk * 512 + t * 128: kk * 512 + t * 128 + 128],
                                wqkvT[:, kk * 3072 + j * 512: kk * 3072 + (j + 1) * 512],
                                start=(kk == 0), stop=(kk == DC - 1)),
                                "TD" if (k == DC - 1 and jj == 5) else None)
                    W("scalar", "TD", qkv_done[(u, t)])
                    for bank in range(4):
                        if (u, t) != (0, 0):
                            W("scalar", "VD", rope_bank.get(
                                (u, t - 1, bank) if t else (u - 1, NT - 1, bank), 0))
                        qke_sd[(u, t, bank)] = OP("scalar", lambda e, b=bank: e.copy(
                            qke[:, b * 512:(b + 1) * 512], A[b][:]), "SD")
                    if u or t:
                        DR("vector")
                    for bank in range(4):
                        base = t * 2048 + bank * 512
                        W("vector", "SD", qke_sd[(u, t, bank)])
                        Eap = qke[:, bank * 512:(bank + 1) * 512].rearrange(
                            "p (h x) -> p h x", x=64)[:, :, 0:32]
                        Oap = qke[:, bank * 512:(bank + 1) * 512].rearrange(
                            "p (h x) -> p h x", x=64)[:, :, 32:64]
                        cosap = costab[:, gt * 32:(gt + 1) * 32] \
                            .rearrange("p (o x) -> p o x", o=1).broadcast_to([128, 8, 32])
                        sinap = sintab[:, gt * 32:(gt + 1) * 32] \
                            .rearrange("p (o x) -> p o x", o=1).broadcast_to([128, 8, 32])
                        r1v = rta[:].rearrange("p (h x) -> p h x", x=32)
                        r2v = rtb[:].rearrange("p (h x) -> p h x", x=32)
                        outE = qksb[:, base:base + 512].rearrange(
                            "p (h x) -> p h x", x=64)[:, :, 0:32]
                        outO = qksb[:, base:base + 512].rearrange(
                            "p (h x) -> p h x", x=64)[:, :, 32:64]
                        OP("vector", lambda e, a=Eap, c=cosap, o=r1v: e.tensor_tensor(o, a, c, ALU.mult))
                        OP("vector", lambda e, a=Oap, c=sinap, o=r2v: e.tensor_tensor(o, a, c, ALU.mult))
                        DR("vector")
                        OP("vector", lambda e, o=outE, x=r1v, y=r2v: e.tensor_tensor(o, x, y, ALU.subtract))
                        DR("vector")
                        OP("vector", lambda e, a=Oap, c=cosap, o=r1v: e.tensor_tensor(o, a, c, ALU.mult))
                        OP("vector", lambda e, a=Eap, c=sinap, o=r2v: e.tensor_tensor(o, a, c, ALU.mult))
                        DR("vector")
                        rope_done[(u, t)] = OP("vector", lambda e, o=outO, x=r1v, y=r2v:
                                               e.tensor_tensor(o, x, y, ALU.add), "VD")
                        rope_bank[(u, t, bank)] = rope_done[(u, t)]
                        if bank != 3:
                            DR("vector")
                    W("scalar", "TD", qkv_done[(u, t)])
                    for half in range(2):
                        dstv = vst[:, t * 1040 + half * 8 * 65: t * 1040 + (half * 8 + 8) * 65] \
                            .rearrange("p (h c) -> p h c", c=65)[:, :, 0:64]
                        vevac_done[(u, t)] = OP("scalar", lambda e, hh=half, dv=dstv: e.copy(
                            dv, A[4 + hh][:].rearrange("p (h x) -> p h x", x=64)),
                            "SD" if half == 1 else None)
                qkfm_sd = None
                for t in range(NT):
                    for part, dst in ((0, qfm), (1, kfm)):
                        W("tensor", "VD", rope_done[(u, t)])
                        W("tensor", "SD", bp_free_sd)
                        for c in range(DC):
                            ftr = OP("tensor", lambda e, t=t, pp=part, cc=c: e.transpose(
                                Bp[:, cc * 128:cc * 128 + 128],
                                qksb[:, t * 2048 + pp * 1024 + cc * 128:
                                     t * 2048 + pp * 1024 + cc * 128 + 128],
                                ident[:]), "TD" if c == DC - 1 else None)
                        W("scalar", "TD", ftr)
                        qkfm_sd = OP("scalar", lambda e, t=t, dd=dst: e.copy(
                            dd[:].rearrange("p (c x) -> p c x", x=512)[:, :, t * 128:t * 128 + 128],
                            Bp[:].rearrange("p (c x) -> p c x", x=128)), "SD")
                        bp_free_sd = qkfm_sd
                sc_all = {}
                for h in range(H + 1):
                    if h < H:
                        par, ch, pb = h % 2, h // 2, (h % 2) * 64
                        qc = ch * 512
                        _off = [0, 512, 896, 1152, 1280]
                        ebs = [par * 1792 + _off[jc] for jc in range(5)]
                        sc_done = {}
                        if h >= 1:
                            for jc in range(5):
                                W("tensor", "SD", exp_done[(u, h - 1, jc)])
                        else:
                            W("tensor", "SD", qkfm_sd)
                            if u == 0:
                                W("tensor", "GD", gld_pm)
                        sc_done[4] = OP("tensor", lambda e, pb=pb, h=h, qc=qc: e.matmul(
                            A[4][0:16, :], pmkf[pb:pb + 64, h * 16:(h + 1) * 16],
                            qfm[pb:pb + 64, qc:qc + 512], start=True, stop=True), "TD")
                        for jc in range(4):
                            sc_done[jc] = OP("tensor", lambda e, pb=pb, qc=qc, jc=jc: e.matmul(
                                A[jc][:, jc * 128:512],
                                kfm[pb:pb + 64, qc + jc * 128: qc + jc * 128 + 128],
                                qfm[pb:pb + 64, qc + jc * 128: qc + 512],
                                start=True, stop=True), "TD")
                        sc_all[h] = sc_done
                        for jc in (4, 0, 1, 2, 3):
                            W("scalar", "TD", sc_done[jc])
                            if h >= 2:
                                W("scalar", "TD", av_done[(u, h - 2)])
                                if jc < 4:
                                    W("scalar", "VD", mask_done[(u, h - 2, jc)])
                            elif u and h < 2:
                                W("scalar", "TD", av_done[(u - 1, h + 14)])
                                if jc < 4:
                                    W("scalar", "VD", mask_done[(u - 1, h + 14, jc)])
                            if jc == 4:
                                exp_done[(u, h, 4)] = OP("scalar", lambda e, b4=ebs[4]: e.activation(
                                    expt[0:16, b4:b4 + 512], A[4][0:16, :],
                                    AF.Exp, scale=0.125), "SD")
                            else:
                                wdt = 512 - jc * 128
                                exp_done[(u, h, jc)] = OP("scalar",
                                    lambda e, jc=jc, bj=ebs[jc], w=wdt: e.activation(
                                        expt[:, bj: bj + w],
                                        A[jc][:, jc * 128:512], AF.Exp, scale=0.125), "SD")
                        for jc in range(4):
                            W("vector", "SD", exp_done[(u, h, jc)])
                            dg = expt[:, ebs[jc]: ebs[jc] + 128]
                            mask_done[(u, h, jc)] = OP("vector", lambda e, d=dg: e.tensor_tensor(
                                d, d, mask[:], ALU.mult), "VD")
                    if h >= 1:
                        hh2 = h - 1
                        par2 = hh2 % 2
                        _off = [0, 512, 896, 1152, 1280]
                        eb2 = [par2 * 1792 + _off[jc] for jc in range(5)]
                        W("tensor", "SD", exp_done[(u, hh2, 4)])
                        for jc in range(4):
                            W("tensor", "SD", exp_done[(u, hh2, jc)])
                            W("tensor", "VD", mask_done[(u, hh2, jc)])
                        if hh2 >= 1 or u:
                            W("tensor", "VD", rec_done)
                            W("tensor", "SD", ao_done)
                        OP("tensor", lambda e, h=hh2, b4=eb2[4]: e.matmul(
                            A[5][0:65, :], pmvst[0:16, h * 65:(h + 1) * 65],
                            expt[0:16, b4:b4 + 512], start=True, stop=False))
                        for jc in range(4):
                            wdt = 512 - jc * 128
                            av_done[(u, hh2)] = OP("tensor",
                                lambda e, h=hh2, jc=jc, bj=eb2[jc], w=wdt: e.matmul(
                                    A[5][0:65, jc * 128:512],
                                    vst[:, jc * 1040 + h * 65: jc * 1040 + h * 65 + 65],
                                    expt[:, bj: bj + w],
                                    start=False, stop=(jc == 3)), "TD" if jc == 3 else None)
                        W("vector", "TD", av_done[(u, hh2)])
                        if hh2 >= 2:
                            W("vector", "LRS%d" % (hh2 % 2), rs_row_st[(u, hh2 - 2)])
                        elif u and hh2 < 2:
                            W("vector", "LRS%d" % (hh2 % 2), rs_row_st[(u - 1, hh2 + 14)])

                        def _recbf(e, h=hh2):
                            with nc.allow_low_precision(reason="softmax normalizer as bf16 matmul operand"):
                                return e.reciprocal(sumsb[0:1, (h % 2) * 512:(h % 2) * 512 + 512],
                                                    A[5][64:65, :])

                        rec_done = OP("vector", _recbf, "VD")
                        W("sync", "VD", rec_done)
                        rs_row_st[(u, hh2)] = dma("sync", "LRS%d" % (hh2 % 2),
                                                  rs_d[hh2:hh2 + 1, :],
                                                  sumsb[0:1, (hh2 % 2) * 512:(hh2 % 2) * 512 + 512])
                        W("scalar", "TD", av_done[(u, hh2)])
                        W("scalar", "VD", rec_done)
                        if hh2 == 0:
                            W("scalar", "VD", prev_norm_vd)
                        ao_done = OP("scalar", lambda e, h=hh2, pb2=(hh2 % 2) * 64, ch2=hh2 // 2: e.copy(
                            aoun[pb2:pb2 + 64, ch2 * 512:(ch2 + 1) * 512], A[5][0:64, :]), "SD")
                rxp_vd = {}
                if u >= 1:
                    W("sync", "TD", last_rmm_td)
                W("sync", "LRS0", CV("LRS0"))
                W("sync", "LRS1", CV("LRS1"))
                ld_rs = dma("sync", "LRS", rsum[:], rs_d)
                for p in range(DC):
                    W("tensor", "LRS", ld_rs)
                    if p >= 1:
                        W("tensor", "VD", rxp_vd[p - 1])
                    elif u:
                        W("tensor", "VD", woutln_vd)
                    rtd = OP("tensor", lambda e, p=p: e.matmul(
                        Cp[:], indsel[0:16, p * 128:(p + 1) * 128], rsum[0:16, :],
                        start=True, stop=True), "TD")
                    last_rmm_td = rtd
                    W("vector", "TD", rtd)
                    rxp_vd[p] = OP("vector", lambda e, p=p: e.tensor_copy(
                        rxp[:, p * 512:(p + 1) * 512], Cp[:]), "VD")
                DR("vector")
                W("vector", "SD", ao_done)
                for p in range(DC):
                    norm_vd = OP("vector", lambda e, p=p: e.tensor_tensor(
                        aoun[:, p * 512:(p + 1) * 512], aoun[:, p * 512:(p + 1) * 512],
                        rxp[:, p * 512:(p + 1) * 512], ALU.mult), "VD")
                prev_norm_vd = norm_vd
                if u + 1 < 16:
                    emit_AB(u + 1)
                wout_vd_par = {}
                for t in range(NT):
                    bw = (t % 2) * 2
                    W("tensor", "VD", norm_vd)
                    if t >= 2:
                        W("tensor", "VD", wout_vd_par[t - 2])
                    for k in range(DC):
                        for jj in range(2):
                            wmm = OP("tensor", lambda e, t=t, kk=k, j=jj, bw=bw: e.matmul(
                                A[bw + j][:],
                                aoun[:, kk * 512 + t * 128: kk * 512 + t * 128 + 128],
                                woutT[:, kk * 1024 + j * 512: kk * 1024 + (j + 1) * 512],
                                start=(kk == 0), stop=(kk == DC - 1)),
                                "TD" if (k == DC - 1 and jj == 1) else None)
                    W("vector", "TD", wmm)
                    DR("vector")
                    DR("scalar")
                    OP("vector", lambda e, bw=bw: e.bn_stats(st6[:, 0:6], A[bw][:]))
                    OP("vector", lambda e, bw=bw: e.bn_stats(st6[:, 6:12], A[bw + 1][:]))
                    DR("vector")
                    OP("vector", lambda e: e.bn_aggr(mvt[:], st6[:]))
                    DR("vector")
                    OP("vector", lambda e: e.tensor_scalar(cw[:, 10:11], mvt[:, 1:2],
                                                           EPS_LN, None, ALU.add))
                    DR("vector")
                    vv = OP("vector", lambda e: e.reciprocal(cw[:, 11:12], cw[:, 10:11]), "VD")
                    W("scalar", "VD", vv)
                    sv = OP("scalar", lambda e: e.activation(cw[:, 12:13], cw[:, 11:12], AF.Sqrt), "SD")
                    W("vector", "SD", sv)
                    for half in range(2):
                        OP("vector", lambda e, hh=half, bw=bw: e.tensor_scalar(
                            x1t[:, hh * 512:(hh + 1) * 512], A[bw + hh][:],
                            mvt[:, 0:1], cw[:, 12:13], ALU.subtract, ALU.mult))
                    DR("vector")
                    OP("vector", lambda e, t=t, shu=(u % 2) * 4096: e.tensor_tensor(
                        x1t[:], x1t[:],
                        seqsb[:, shu + t * 1024: shu + (t + 1) * 1024], ALU.add))
                    DR("vector")
                    OP("vector", lambda e: e.bn_stats(st6[:, 0:6], x1t[:, 0:512]))
                    OP("vector", lambda e: e.bn_stats(st6[:, 6:12], x1t[:, 512:1024]))
                    DR("vector")
                    OP("vector", lambda e: e.bn_aggr(mvt[:], st6[:]))
                    DR("vector")
                    OP("vector", lambda e: e.tensor_scalar(cw[:, 14:15], mvt[:, 1:2],
                                                           EPS_LN, None, ALU.add))
                    DR("vector")
                    vv = OP("vector", lambda e: e.reciprocal(cw[:, 15:16], cw[:, 14:15]), "VD")
                    seq_par_read[u % 2] = vv
                    W("scalar", "VD", vv)
                    sv = OP("scalar", lambda e: e.activation(cw[:, 16:17], cw[:, 15:16], AF.Sqrt))
                    sv = OP("scalar", lambda e: e.activation(cw[:, 17:18], cw[:, 14:15], AF.Sqrt), "SD", 1)
                    W("vector", "SD", sv)
                    slot = 0
                    if (u * 4 + t - 1) >= 0:
                        W("vector", "SX%d" % ((t - 1) % 2), x2_store_st.get(u * 4 + t - 1, 0))
                    OP("vector", lambda e, sl=slot: e.tensor_scalar(
                        x2t[:, sl:sl + 1024], x1t[:], mvt[:, 0:1], cw[:, 16:17],
                        ALU.subtract, ALU.mult))
                    DR("vector")
                    OP("vector", lambda e, u=u, t=t, sl=slot: e.tensor_reduce(
                        mx[:, MX_X + u * 4 + t: MX_X + u * 4 + t + 1],
                        x2t[:, sl:sl + 1024], axis=AX.X, op=ALU.max,
                        apply_absolute_value=True))
                    OP("vector", lambda e, t=t: e.tensor_copy(sgst[:, t * 2:t * 2 + 1], cw[:, 17:18]))
                    woutln_vd = OP("vector", lambda e, t=t: e.tensor_copy(
                        sgst[:, t * 2 + 1:t * 2 + 2], mvt[:, 0:1]), "VD")
                    wout_vd_par[t] = woutln_vd
                    W("scalar", "VD", woutln_vd)
                    x2_store_st[u * 4 + t] = dma("scalar", "SX%d" % (t % 2),
                                                 x2_d[u * 4 + t], x2t[:, slot:slot + 1024])
                W("scalar", "VD", woutln_vd)
                if u >= 1:
                    pass
                sg_store_st = dma("scalar", "SSG", sig_d[u], sgst[:])
                if u < 15:
                    W("vector", "SSG", sg_store_st)

        if max_phase >= 2:
            DR("vector")
            OP("vector", lambda e: e.tensor_reduce(cw[:, 18:19], mx[:, MX_X:MX_X + 64],
                                                   axis=AX.X, op=ALU.max))
            DR("vector")
            mxv = OP("vector", lambda e: e.tensor_copy(cw[:, 18:19], cw[:, 18:19]), "VD")

            cc_state = {"btd": 0}

            def collective(idx, col_ap, after_vd, out_col_ap):
                W("tensor", "VD", after_vd)
                ttd = OP("tensor", lambda e, c=col_ap: e.transpose(
                    Cp[0:1, 0:128], c, identf[:]), "TD")
                W("vector", "TD", ttd)
                DR("vector")
                mv_ = OP("vector", lambda e: e.reduce_max(
                    mcol8[0:1, 0:1], Cp[0:1, 0:128], axis=AX.X), "VD")
                W("gpsimd", "VD", mv_)
                W("gpsimd", "TD", cc_state["btd"])
                gd = dma("gpsimd", "GD", cci[idx], mcol8[:])
                W("gpsimd", "GD", gd)
                ccv = OP("gpsimd", lambda e, i=idx: e.collective_compute(
                    "AllReduce", ALU.max, replica_groups=[list(range(n_cores))],
                    ins=[cci[i].opt()], outs=[cco[i].opt()]), "CC")
                W("gpsimd", "CC", ccv)
                gd = dma("gpsimd", "GD", mg8[:], cco[idx])
                W("tensor", "GD", gd)
                btd = OP("tensor", lambda e: e.matmul(
                    Cp[0:128, 0:1], onesrow[0:1, 0:128], mg8[0:1, 0:1],
                    start=True, stop=True), "TD")
                cc_state["btd"] = btd
                W("vector", "TD", btd)
                vv = OP("vector", lambda e, o=out_col_ap: e.tensor_copy(
                    o, Cp[0:128, 0:1]), "VD")
                return vv

            collective(0, cw[:, 18:19], mxv, cw[:, 19:20])
            DR("vector")
            OP("vector", lambda e: e.reciprocal(cw[:, 20:21], cw[:, 19:20]))
            DR("vector")
            OP("vector", lambda e: e.tensor_scalar(scl[:, SXI:SXI + 1], cw[:, 20:21],
                                                   32767.0, CLIP, ALU.mult, ALU.min))
            DR("vector")
            sc_x_vd = OP("vector", lambda e: e.reciprocal(scl[:, SX:SX + 1],
                                                          scl[:, SXI:SXI + 1]), "VD")

            def barrier():
                vals = dict(cnt)
                for e in ENGINES:
                    for k, v in vals.items():
                        W(e, k, v)
            barrier()

        if max_phase >= 2.2:
            xq_sd = {}
            xq_td = {}
            xq_vd_hist = {}
            NTT_RUN = NTT if max_phase >= 2.5 else 8
            for i in range(NTT_RUN):
                slot = (i % 2) * 1024
                if i >= 2:
                    W("sync", "VD", xq_vd_hist[i - 2])
                ldt = dma("sync", "LX%d" % (i % 2), xin[:, slot:slot + 1024],
                          x2_d[i][:, :])
                W("vector", "LX%d" % (i % 2), ldt)
                W("vector", "VD", sc_x_vd)
                if i >= 1:
                    W("vector", "TD", xq_td[i - 1])
                    DR("vector")
                OP("vector", lambda e, sl=slot: e.tensor_scalar(
                    xt1[:], xin[:, sl:sl + 1024], scl[:, SXI:SXI + 1], None, ALU.mult))
                DR("vector")
                OP("vector", lambda e: e.tensor_scalar(xr2[:], xt1[:], MAGIC, MAGIC,
                                                       ALU.add, ALU.subtract))
                DR("vector")
                xq_vd = OP("vector", lambda e: e.tensor_scalar(
                    xqtm[:], xr2[:], scl[:, SX:SX + 1], None, ALU.mult), "VD")
                xq_vd_hist[i] = xq_vd
                W("tensor", "VD", xq_vd)
                W("tensor", "SD", bp_free_sd)
                for k in range(DC):
                    xq_td[i] = OP("tensor", lambda e, kk=k: e.transpose(
                        Bp[:, kk * 128:kk * 128 + 128],
                        xqtm[:, kk * 128:kk * 128 + 128], ident[:]),
                        "TD" if k == DC - 1 else None)
                W("scalar", "TD", xq_td[i])
                xq_sd[i] = OP("scalar", lambda e, i=i: e.copy(
                    xqfm[:].rearrange("p (k x) -> p k x", x=8192)[:, :, i * 128:i * 128 + 128],
                    Bp[:].rearrange("p (k x) -> p k x", x=128)), "SD")
                bp_free_sd = xq_sd[i]
            xq_all_sd = xq_sd[NTT_RUN - 1]

            bank_vd = [0, 0, 0, 0]
            bank_sd = [0, 0, 0, 0]
            stg_st = [0, 0, 0, 0]
            wq1_done = {}
            wtr_sd_of = {}
            last_mm_td = 0
            mm_last_of = {}
            CC2_RUN = 48 if max_phase >= 2.75 else (4 if max_phase >= 2.6 else 0)

            def srcw(cc2):
                m, wg = cc2 // 2, cc2 % 2
                return (wup_d if wg == 0 else wgate_d)[m * 128:(m + 1) * 128, :]

            def emit_dequant(cc2):
                slot = (cc2 % 2) * 1024
                W("vector", "LWS%d" % (cc2 % 2), ld_of[cc2])
                if cc2 >= 1:
                    W("vector", "TD", wtr_td_of[cc2 - 1])
                DR("vector")
                OP("vector", lambda e, sl=slot: e.tensor_reduce(
                    mxb32[:], wstr[:, sl:sl + 1024].rearrange("p (a b) -> p a b", b=32),
                    axis=AX.X, op=ALU.max, apply_absolute_value=True))
                DR("vector")
                OP("vector", lambda e: e.reciprocal(rmxb[:], mxb32[:]))
                DR("vector")
                OP("vector", lambda e: e.tensor_scalar(sinvb[:], rmxb[:], 7.0,
                                                       CLIP, ALU.mult, ALU.min))
                DR("vector")
                OP("vector", lambda e: e.reciprocal(sblk[:], sinvb[:]))
                DR("vector")
                wq1_done[cc2] = OP("vector", lambda e, sl=slot: e.tensor_tensor(
                    wq1[:].rearrange("p (a b) -> p a b", b=32),
                    wstr[:, sl:sl + 1024].rearrange("p (a b) -> p a b", b=32),
                    sinvb[:].rearrange("p (a b) -> p a b", b=1).broadcast_to([128, 32, 32]),
                    ALU.mult), "VD")
                DR("vector")
                OP("vector", lambda e: e.tensor_scalar(wr2[:], wq1[:], MAGIC, MAGIC,
                                                       ALU.add, ALU.subtract))
                DR("vector")
                wdq_vd = OP("vector", lambda e: e.tensor_tensor(
                    wdqb[:].rearrange("p (a b) -> p a b", b=32),
                    wr2[:].rearrange("p (a b) -> p a b", b=32),
                    sblk[:].rearrange("p (a b) -> p a b", b=1).broadcast_to([128, 32, 32]),
                    ALU.mult), "VD")
                W("tensor", "VD", wdq_vd)
                W("tensor", "SD", bp_state[0])
                for k in range(DC):
                    wtr_td = OP("tensor", lambda e, kk=k: e.transpose(
                        Bp[:, kk * 128:kk * 128 + 128],
                        wdqb[:, kk * 128:kk * 128 + 128], ident[:]),
                        "TD" if k == DC - 1 else None)
                wtr_td_of[cc2] = wtr_td
                W("scalar", "TD", wtr_td)
                if cc2 >= 2:
                    W("scalar", "TD", mm_last_of[cc2 - 2])
                wslot = (cc2 % 2) * 1024
                wtr_sd_of[cc2] = OP("scalar", lambda e, ws=wslot: e.copy(
                    wtr[:, ws:ws + 1024], Bp[:]), "SD")
                bp_state[0] = wtr_sd_of[cc2]

            bp_state = [bp_free_sd]
            wtr_td_of = {}
            ld_of = {}
            if CC2_RUN:
                ld_of[0] = dma("sync", "LWS0", wstr[:, 0:1024], srcw(0))
                emit_dequant(0)
            for cc2 in range(CC2_RUN):
                m, wg = cc2 // 2, cc2 % 2
                wdst = u_d if wg == 0 else gg_d
                mxbase = MX_U if wg == 0 else MX_G
                wslot = (cc2 % 2) * 1024
                for gq in range(NG):
                    bk = gq % 4
                    if gq == 3 and cc2 + 1 < CC2_RUN:
                        W("sync", "VD", wq1_done.get(cc2 - 1, 0))
                        ld_of[cc2 + 1] = dma("sync", "LWS%d" % ((cc2 + 1) % 2),
                                             wstr[:, ((cc2 + 1) % 2) * 1024:
                                                  ((cc2 + 1) % 2) * 1024 + 1024],
                                             srcw(cc2 + 1))
                        emit_dequant(cc2 + 1)
                    W("tensor", "SD", wtr_sd_of[cc2])
                    W("tensor", "VD", bank_vd[bk])
                    W("tensor", "SD", bank_sd[bk])
                    for k in range(DC):
                        mmtd = OP("tensor", lambda e, kk=k, gq=gq, bk=bk, ws=wslot: e.matmul(
                            A[bk][:], wtr[:, ws + kk * 128: ws + kk * 128 + 128],
                            xqfm[:, kk * 8192 + gq * 512: kk * 8192 + (gq + 1) * 512],
                            start=(kk == 0), stop=(kk == DC - 1)),
                            "TD" if k == DC - 1 else None)
                    last_mm_td = mmtd
                    mm_last_of[cc2] = mmtd
                    W("vector", "TD", mmtd)
                    bank_vd[bk] = OP("vector", lambda e, bk=bk, col=mxbase + m * 16 + gq:
                                     e.tensor_reduce(mx[:, col:col + 1], A[bk][:],
                                                     axis=AX.X, op=ALU.max,
                                                     apply_absolute_value=True), "VD")
                    W("scalar", "TD", mmtd)
                    W("scalar", "VD", bank_vd[bk])
                    W("scalar", "SG%d" % bk, stg_st[bk])
                    bank_sd[bk] = OP("scalar", lambda e, bk=bk: e.copy(
                        stg[:, bk * 512:(bk + 1) * 512], A[bk][:]), "SD")
                    W("scalar", "SD", bank_sd[bk])
                    stg_st[bk] = dma("scalar", "SG%d" % bk, wdst[m, gq],
                                     stg[:, bk * 512:(bk + 1) * 512])
            bp_free_sd = bp_state[0]

            W("sync", "VD", CV("VD"))
            W("sync", "TD", CV("TD"))
            wdn_vd = 0
            wdn_td = 0
            WDN_RUN = 16 if max_phase >= 2.9 else (1 if max_phase >= 2.8 else 0)
            for r2 in range(WDN_RUN):
                r, hh = r2 // 2, r2 % 2
                if r2 >= 1:
                    W("sync", "VD", wdn_vd)
                ldw = dma("sync", "LWD", wdnin[:],
                          wdown_d[r * 128:(r + 1) * 128, hh * 1536:(hh + 1) * 1536])
                W("vector", "LWD", ldw)
                if r2 >= 1:
                    W("vector", "TD", wdn_td)
                DR("vector")
                OP("vector", lambda e: e.tensor_reduce(
                    mxb48[:], wdnin[:].rearrange("p (a b) -> p a b", b=32),
                    axis=AX.X, op=ALU.max, apply_absolute_value=True))
                DR("vector")
                OP("vector", lambda e: e.reciprocal(rmx48[:], mxb48[:]))
                DR("vector")
                OP("vector", lambda e: e.tensor_scalar(siv48[:], rmx48[:], 7.0, CLIP,
                                                       ALU.mult, ALU.min))
                DR("vector")
                OP("vector", lambda e: e.reciprocal(sbk48[:], siv48[:]))
                DR("vector")
                OP("vector", lambda e: e.tensor_tensor(
                    wdn1[:].rearrange("p (a b) -> p a b", b=32),
                    wdnin[:].rearrange("p (a b) -> p a b", b=32),
                    siv48[:].rearrange("p (a b) -> p a b", b=1).broadcast_to([128, 48, 32]),
                    ALU.mult))
                DR("vector")
                OP("vector", lambda e: e.tensor_scalar(wdnr[:], wdn1[:], MAGIC, MAGIC,
                                                       ALU.add, ALU.subtract))
                DR("vector")
                wdn_vd = OP("vector", lambda e: e.tensor_tensor(
                    wdnb[:].rearrange("p (a b) -> p a b", b=32),
                    wdnr[:].rearrange("p (a b) -> p a b", b=32),
                    sbk48[:].rearrange("p (a b) -> p a b", b=1).broadcast_to([128, 48, 32]),
                    ALU.mult), "VD")
                W("tensor", "VD", wdn_vd)
                for bb in range(2):
                    nblk = 8 if bb == 0 else 4
                    W("tensor", "SD", bp_free_sd)
                    for k in range(nblk):
                        mblk = bb * 8 + k
                        wdn_td = OP("tensor", lambda e, kk=k, mb=mblk: e.transpose(
                            Bp[:, kk * 128:kk * 128 + 128],
                            wdnb[:, mb * 128:mb * 128 + 128], ident[:]),
                            "TD" if k == nblk - 1 else None)
                    W("scalar", "TD", wdn_td)
                    mg0 = hh * 12 + bb * 8
                    bp_free_sd = OP("scalar", lambda e, mg0=mg0, r=r, nb=nblk: e.copy(
                        wdT[:].rearrange("p (mm x) -> p mm x", x=1024)
                        [:, mg0:mg0 + nb, r * 128:r * 128 + 128],
                        Bp[:, 0:nb * 128].rearrange("p (k x) -> p k x", x=128)), "SD")

        if max_phase >= 3.2:
            DR("vector")
            OP("vector", lambda e: e.tensor_reduce(cw[:, 18:19], mx[:, MX_U:MX_U + 384],
                                                   axis=AX.X, op=ALU.max))
            mug = OP("vector", lambda e: e.tensor_reduce(cw[:, 19:20], mx[:, MX_G:MX_G + 384],
                                                         axis=AX.X, op=ALU.max), "VD")
            DR("vector")
            if max_phase >= 3.4:
                collective(1, cw[:, 18:19], mug, cw[:, 20:21])
            if max_phase >= 3.6:
                collective(2, cw[:, 19:20], CV("VD"), cw[:, 21:22])
            if max_phase >= 3.8:
                DR("vector")
                OP("vector", lambda e: e.reciprocal(cw[:, 22:23], cw[:, 20:21]))
                OP("vector", lambda e: e.reciprocal(cw[:, 23:24], cw[:, 21:22]))
                DR("vector")
                OP("vector", lambda e: e.tensor_scalar(scl[:, SUI:SUI + 1], cw[:, 22:23],
                                                       32767.0, CLIP, ALU.mult, ALU.min))
                OP("vector", lambda e: e.tensor_scalar(scl[:, SGI:SGI + 1], cw[:, 23:24],
                                                       32767.0, CLIP, ALU.mult, ALU.min))
                DR("vector")
                OP("vector", lambda e: e.reciprocal(scl[:, SU:SU + 1], scl[:, SUI:SUI + 1]))
                OP("vector", lambda e: e.reciprocal(scl[:, SG:SG + 1], scl[:, SGI:SGI + 1]))
                DR("vector")
                sc_ug_vd = OP("vector", lambda e: e.tensor_scalar(
                    scl[:, SGQ:SGQ + 1], scl[:, SG:SG + 1], float(2.0 ** -16), None,
                    ALU.mult), "VD")

        if max_phase >= 5:
            barrier()

            p3_st = {}
            p3v1 = {}
            p3v2 = {}
            a3sd3 = {}
            for ci in range(97):
                if ci < 96:
                    m, q4 = ci // 4, ci % 4
                    hf = (ci % 2) * 2048
                    if ci >= 2:
                        W("sync", "VD", p3v1[ci - 2])
                    ldg = dma("sync", "LG%d" % (ci % 2), gin[:, hf:hf + 2048]
                              .rearrange("p (a b) -> p a b", b=512),
                              gg_d[m, q4 * 4:(q4 + 1) * 4].rearrange("a p b -> p a b"))
                    W("vector", "LG%d" % (ci % 2), ldg)
                    W("vector", "VD", sc_ug_vd)
                    if ci >= 2:
                        W("vector", "SD", a3sd3[ci - 2])
                    DR("vector")
                    OP("vector", lambda e, h=hf: e.tensor_scalar(
                        tb1[:, h:h + 2048], gin[:, h:h + 2048],
                        scl[:, SGI:SGI + 1], None, ALU.mult))
                    DR("vector")
                    p3v1[ci] = OP("vector", lambda e, h=hf: e.tensor_scalar(
                        rb3[:, h:h + 2048], tb1[:, h:h + 2048], MAGIC, MAGIC,
                        ALU.add, ALU.subtract), "VD")
                    W("scalar", "VD", p3v1[ci])
                    DR("scalar")
                    OP("scalar", lambda e, h=hf: e.activation(
                        sigb[:, h:h + 2048], rb3[:, h:h + 2048], AF.Sigmoid,
                        scale=scl[:, SG:SG + 1]))
                    DR("scalar")
                    OP("scalar", lambda e, h=hf: e.activation(
                        tb1[:, h:h + 2048], sigb[:, h:h + 2048], AF.Identity,
                        scale=65536.0, bias=cmag[:, 0:1]))
                    DR("scalar")
                    a3sd3[ci] = OP("scalar", lambda e, h=hf: e.activation(
                        a3b[:, h:h + 2048], tb1[:, h:h + 2048], AF.Identity,
                        bias=cmag[:, 1:2]), "SD")
                if ci >= 1:
                    j = ci - 1
                    mj, q4j = j // 4, j % 4
                    hj = (j % 2) * 2048
                    W("vector", "SD", a3sd3[j])
                    OP("vector", lambda e, h=hj: e.tensor_scalar(
                        c2b[:, h:h + 2048], a3b[:, h:h + 2048], 65535.0,
                        scl[:, SGQ:SGQ + 1], ALU.min, ALU.mult))
                    DR("vector")
                    W("vector", "STB%d" % (j % 2), p3_st.get(j - 2, 0))
                    OP("vector", lambda e, h=hj: e.tensor_tensor(
                        tob[:, h:h + 2048], rb3[:, h:h + 2048],
                        c2b[:, h:h + 2048], ALU.mult))
                    DR("vector")
                    p3v2[j] = OP("vector", lambda e, j=j, h=hj: e.tensor_reduce(
                        mx[:, MX_T + j:MX_T + j + 1], tob[:, h:h + 2048],
                        axis=AX.X, op=ALU.max, apply_absolute_value=True), "VD")
                    W("scalar", "VD", p3v2[j])
                    p3_st[j] = dma("scalar", "STB%d" % (j % 2),
                                   t_d[mj, q4j * 4:(q4j + 1) * 4].rearrange("a p b -> p a b"),
                                   tob[:, hj:hj + 2048].rearrange("p (a b) -> p a b", b=512))

            DR("vector")
            mt = OP("vector", lambda e: e.tensor_reduce(cw[:, 18:19], mx[:, MX_T:MX_T + 96],
                                                        axis=AX.X, op=ALU.max), "VD")
            DR("vector")
            collective(3, cw[:, 18:19], mt, cw[:, 19:20])
            DR("vector")
            OP("vector", lambda e: e.reciprocal(cw[:, 20:21], cw[:, 19:20]))
            DR("vector")
            OP("vector", lambda e: e.tensor_scalar(scl[:, STI:STI + 1], cw[:, 20:21],
                                                   32767.0, CLIP, ALU.mult, ALU.min))
            DR("vector")
            OP("vector", lambda e: e.reciprocal(cw[:, 21:22], scl[:, STI:STI + 1]))
            DR("vector")
            sc_t_vd = OP("vector", lambda e: e.tensor_tensor(
                scl[:, STU:STU + 1], cw[:, 21:22], scl[:, SU:SU + 1], ALU.mult), "VD")

        if max_phase >= 6:
            barrier()

            p4_st = {}
            p4v1 = {}
            p4v2 = {}
            rusd4 = {}
            for ci in range(97):
                if ci < 96:
                    m, q4 = ci // 4, ci % 4
                    hf = (ci % 2) * 2048
                    if ci >= 2:
                        W("sync", "VD", p4v1[ci - 2])
                        W("sync", "SD", rusd4[ci - 2])
                    ldt = dma("sync", "LG%d" % (ci % 2), gin[:, hf:hf + 2048]
                              .rearrange("p (a b) -> p a b", b=512),
                              t_d[m, q4 * 4:(q4 + 1) * 4].rearrange("a p b -> p a b"))
                    ldu = dma("sync", "LU%d" % (ci % 2), uin[:, hf:hf + 2048]
                              .rearrange("p (a b) -> p a b", b=512),
                              u_d[m, q4 * 4:(q4 + 1) * 4].rearrange("a p b -> p a b"))
                    W("scalar", "LG%d" % (ci % 2), ldt)
                    W("scalar", "VD", sc_t_vd)
                    if ci >= 2:
                        W("scalar", "VD", p4v1[ci - 2])
                    DR("scalar")
                    t1sd = OP("scalar", lambda e, h=hf: e.activation(
                        tb1[:, h:h + 2048], gin[:, h:h + 2048], AF.Identity,
                        scale=scl[:, STI:STI + 1], bias=cmag[:, 0:1]), "SD")
                    W("vector", "SD", t1sd)
                    DR("vector")
                    OP("vector", lambda e, h=hf: e.tensor_scalar(
                        rb3[:, h:h + 2048], tb1[:, h:h + 2048], -MAGIC, None,
                        ALU.add))
                    DR("vector")
                    p4v1[ci] = OP("vector", lambda e, h=hf: e.tensor_scalar(
                        c2b[:, h:h + 2048], rb3[:, h:h + 2048],
                        scl[:, STU:STU + 1], None, ALU.mult), "VD")
                    W("scalar", "LU%d" % (ci % 2), ldu)
                    W("scalar", "VD", p4v1[ci])
                    DR("scalar")
                    OP("scalar", lambda e, h=hf: e.activation(
                        tb1[:, h:h + 2048], uin[:, h:h + 2048], AF.Identity,
                        scale=scl[:, SUI:SUI + 1], bias=cmag[:, 0:1]))
                    DR("scalar")
                    rusd4[ci] = OP("scalar", lambda e, h=hf: e.activation(
                        a3b[:, h:h + 2048], tb1[:, h:h + 2048], AF.Identity,
                        bias=cmag[:, 1:2]), "SD")
                if ci >= 1:
                    j = ci - 1
                    mj, q4j = j // 4, j % 4
                    hj = (j % 2) * 2048
                    W("vector", "SD", rusd4[j])
                    W("vector", "STB%d" % (j % 2), p4_st.get(j - 2, 0))
                    OP("vector", lambda e, h=hj: e.tensor_tensor(
                        tob[:, h:h + 2048], c2b[:, h:h + 2048],
                        a3b[:, h:h + 2048], ALU.mult))
                    DR("vector")
                    p4v2[j] = OP("vector", lambda e, j=j, h=hj: e.tensor_reduce(
                        mx[:, MX_O + j:MX_O + j + 1], tob[:, h:h + 2048],
                        axis=AX.X, op=ALU.max, apply_absolute_value=True), "VD")
                    W("scalar", "VD", p4v2[j])
                    p4_st[j] = dma("scalar", "STB%d" % (j % 2),
                                   o_d[mj, q4j * 4:(q4j + 1) * 4].rearrange("a p b -> p a b"),
                                   tob[:, hj:hj + 2048].rearrange("p (a b) -> p a b", b=512))

            DR("vector")
            mo = OP("vector", lambda e: e.tensor_reduce(cw[:, 18:19], mx[:, MX_O:MX_O + 96],
                                                        axis=AX.X, op=ALU.max), "VD")
            DR("vector")
            collective(4, cw[:, 18:19], mo, cw[:, 19:20])
            DR("vector")
            OP("vector", lambda e: e.reciprocal(cw[:, 20:21], cw[:, 19:20]))
            DR("vector")
            OP("vector", lambda e: e.tensor_scalar(scl[:, SOI:SOI + 1], cw[:, 20:21],
                                                   32767.0, CLIP, ALU.mult, ALU.min))
            DR("vector")
            sc_o_vd = OP("vector", lambda e: e.reciprocal(scl[:, SO:SO + 1],
                                                          scl[:, SOI:SOI + 1]), "VD")

        if max_phase >= 7:
            barrier()

            dn_td = {}
            out_st = {}
            ln_vd = 0
            ln_vd_par = {}
            x2g_vd_par = {}
            x2g_vd = 0
            oq_vd_hist = {}
            oq_sd_hist = {}
            oq_last_vd = {}
            ld_oq = {}

            def emit_oq_prep(g, mlist):
                half = (g % 2) * 12288
                for m in mlist:
                    slot = (m % 8) * 512
                    if (g, m - 8) in oq_sd_hist:
                        W("sync", "SD", oq_sd_hist[(g, m - 8)])
                    elif (g - 1, 16 + m) in oq_sd_hist:
                        W("sync", "SD", oq_sd_hist[(g - 1, 16 + m)])
                    ldo = dma("sync", "LO%d" % (m % 8), gin[:, slot:slot + 512],
                              o_d[m, g])
                    W("scalar", "LO%d" % (m % 8), ldo)
                    W("scalar", "VD", sc_o_vd)
                    DR("scalar")
                    OP("scalar", lambda e, sl=slot: e.activation(
                        ot1[:], gin[:, sl:sl + 512], AF.Identity,
                        scale=scl[:, SOI:SOI + 1], bias=cmag[:, 0:1]))
                    DR("scalar")
                    if (g, m - 2) in oq_vd_hist:
                        W("scalar", "VD", oq_vd_hist[(g, m - 2)])
                    elif (g - 1, 22 + m) in oq_vd_hist:
                        W("scalar", "VD", oq_vd_hist[(g - 1, 22 + m)])
                    o2s = (m % 2) * 512
                    osd = OP("scalar", lambda e, o2s=o2s: e.activation(
                        ot2[:, o2s:o2s + 512], ot1[:], AF.Identity, bias=cmag[:, 1:2]), "SD")
                    W("vector", "SD", osd)
                    if g >= 2 and m == 0:
                        W("vector", "TD", dn_td[(g - 2, NT - 1)])
                    DR("vector")
                    oq_sd_hist[(g, m)] = osd
                    oq_vd_hist[(g, m)] = OP("vector", lambda e, m=m, hh=half, o2s=o2s: e.tensor_scalar(
                        oqb[:, hh + m * 512: hh + (m + 1) * 512], ot2[:, o2s:o2s + 512],
                        scl[:, SO:SO + 1], None, ALU.mult), "VD")
                oq_last_vd[g] = oq_vd_hist[(g, mlist[-1])] if mlist else oq_last_vd.get(g, 0)

            emit_oq_prep(0, list(range(FC)))
            for gq in range(NG):
                half = (gq % 2) * 12288
                xgh = (gq % 2) * 4096
                if gq >= 2:
                    W("sync", "VD", x2g_vd_par[gq - 2])
                ldx = dma("sync", "LXG",
                          x2g[:, xgh:xgh + 4096].rearrange("p (t d) -> p t d", d=1024),
                          x2_d[gq * 4:(gq + 1) * 4].rearrange("t p d -> p t d"))
                lds = dma("sync", "LXG", sg4b[:, (gq % 2) * 8:(gq % 2) * 8 + 8], sig_d[gq])
                for tt in range(NT):
                    bw = (tt % 2) * 2
                    W("tensor", "VD", oq_last_vd[gq])
                    kprev = (gq, tt - 2) if tt >= 2 else (gq - 1, tt + 2)
                    if kprev in ln_vd_par:
                        W("tensor", "VD", ln_vd_par[kprev])
                    for m in range(FC):
                        for jj in range(2):
                            dn_td[(gq, tt)] = OP("tensor", lambda e, m=m, tt=tt, j=jj, hh=half, bw=bw:
                                                 e.matmul(
                                A[bw + j][:], oqb[:, hh + m * 512 + tt * 128:
                                             hh + m * 512 + tt * 128 + 128],
                                wdT[:, m * 1024 + j * 512: m * 1024 + (j + 1) * 512],
                                start=(m == 0), stop=(m == FC - 1)),
                                "TD" if (m == FC - 1 and jj == 1) else None)
                    W("vector", "TD", dn_td[(gq, tt)])
                    W("vector", "LXG", lds)
                    DR("vector")
                    DR("scalar")
                    OP("vector", lambda e, bw=bw: e.bn_stats(st6[:, 0:6], A[bw][:]))
                    OP("vector", lambda e, bw=bw: e.bn_stats(st6[:, 6:12], A[bw + 1][:]))
                    DR("vector")
                    OP("vector", lambda e: e.bn_aggr(mvt[:], st6[:]))
                    DR("vector")
                    OP("vector", lambda e: e.tensor_scalar(cw[:, 10:11], mvt[:, 1:2],
                                                           EPS_LN, None, ALU.add))
                    DR("vector")
                    vv = OP("vector", lambda e: e.reciprocal(cw[:, 11:12], cw[:, 10:11]), "VD")
                    W("scalar", "VD", vv)
                    sv = OP("scalar", lambda e: e.activation(cw[:, 12:13], cw[:, 11:12],
                                                             AF.Sqrt), "SD")
                    W("vector", "SD", sv)
                    for half2 in range(2):
                        OP("vector", lambda e, hh=half2, bw=bw: e.tensor_scalar(
                            lnb[:, hh * 512:(hh + 1) * 512], A[bw + hh][:],
                            mvt[:, 0:1], cw[:, 12:13], ALU.subtract, ALU.mult))
                    OP("vector", lambda e, tt=tt, xgh=xgh, sb=(gq % 2) * 8: e.tensor_scalar(
                        x1b[:], x2g[:, xgh + tt * 1024: xgh + (tt + 1) * 1024],
                        sg4b[:, sb + tt * 2:sb + tt * 2 + 1],
                        sg4b[:, sb + tt * 2 + 1:sb + tt * 2 + 2],
                        ALU.mult, ALU.add))
                    DR("vector")
                    oslot = (tt % 2) * 1024
                    prev_o = (gq, tt - 2) if tt >= 2 else (gq - 1, tt + 2)
                    W("vector", "SY%d" % (tt % 2), out_st.get(prev_o, 0))
                    x2g_vd = ln_vd = OP("vector", lambda e, osl=oslot: e.tensor_tensor(
                        outb[:, osl:osl + 1024], lnb[:], x1b[:], ALU.add), "VD")
                    ln_vd_par[(gq, tt)] = ln_vd
                    x2g_vd_par[gq] = ln_vd
                    W("scalar", "VD", ln_vd)
                    tok0 = gq * 512 + tt * 128
                    out_st[(gq, tt)] = dma("scalar", "SY%d" % (tt % 2),
                                           y_d[tok0 // 2048,
                                               tok0 % 2048: tok0 % 2048 + 128, :],
                                           outb[:, oslot:oslot + 1024])
                    if gq + 1 < NG and tt < 2:
                        emit_oq_prep(gq + 1, list(range(tt * 12, tt * 12 + 12)))

        for k, v in dict(cnt).items():
            W("sync", k, v)

        waited = {}
        for e in ENGINES:
            for item in prog[e]:
                if item[0] == "w":
                    waited.setdefault(item[1], set()).add(item[2])
        dma_sems = {s for s in cnt if s not in ("TD", "VD", "SD", "PD", "CC")}
        run_cnt = {}
        for e in ENGINES:
            newops = []
            pend = {}
            for item in prog[e]:
                if item[0] == "o":
                    _, fn, inc, n = item
                    if inc is not None and inc not in dma_sems:
                        v = run_cnt.get(inc, 0) + n
                        run_cnt[inc] = v
                        if v in waited.get(inc, ()):
                            newops.append(("o", fn, inc, n + pend.pop(inc, 0)))
                        else:
                            pend[inc] = pend.get(inc, 0) + n
                            newops.append(("o", fn, None, 0))
                        continue
                newops.append(item)
            assert not pend or all(v == 0 for v in pend.values()) or True
            prog[e] = newops

        def runner(eng_name):
            def run(eng):
                lastw = {}
                for item in prog[eng_name]:
                    if item[0] == "w":
                        if lastw.get(item[1], -1) >= item[2]:
                            continue
                        lastw[item[1]] = item[2]
                        eng.wait_ge(sems[item[1]], item[2])
                    elif item[0] == "d":
                        eng.drain()
                    else:
                        _, fn, inc, n = item
                        instr = fn(eng)
                        if inc is not None:
                            instr.then_inc(sems[inc], n)
            return run

        blk.sync(runner("sync"))
        blk.tensor(runner("tensor"))
        blk.vector(runner("vector"))
        blk.scalar(runner("scalar"))
        blk.gpsimd(runner("gpsimd"))

    return nc


def make_in_maps(inputs, n_cores=8):
    consts = host_constants()
    perm = qk_row_perm()
    wqkv = np.ascontiguousarray(inputs["L0_attn_w_qkv"][perm])
    g8 = np.ascontiguousarray(inputs["L0_attn_g"].reshape(DC, 128).T)
    pm = inputs["L0_attn_pm"]
    pmkf = np.empty((128, 256), dtype=np.float32)
    for h in range(H):
        blkv = pm[0, h].T
        pmkf[0:64, h * 16:(h + 1) * 16] = blkv
        pmkf[64:128, h * 16:(h + 1) * 16] = blkv
    per_head = np.concatenate([np.arange(0, DH, 2), np.arange(1, DH, 2)])
    pmkf_p = np.empty_like(pmkf)
    pmkf_p[0:64] = pmkf[0:64][per_head]
    pmkf_p[64:128] = pmkf[64:128][per_head]
    pmvst = np.zeros((16, 1040), dtype=np.float32)
    for h in range(H):
        pmvst[:, h * 65: h * 65 + 64] = pm[1, h]
        pmvst[:, h * 65 + 64] = 1.0
    seq = inputs["L0_attn_seq"]
    maps = []
    for c in range(n_cores):
        maps.append({
            "seq": np.ascontiguousarray(seq[c * NB:(c + 1) * NB]),
            "wqkv": wqkv,
            "wout": np.ascontiguousarray(inputs["L0_attn_w_out"]),
            "wup": np.ascontiguousarray(inputs["L0_ffn_w_up"]),
            "wgate": np.ascontiguousarray(inputs["L0_ffn_w_gate"]),
            "wdown": np.ascontiguousarray(inputs["L0_ffn_w_down"]),
            "gvec": g8,
            "pmkf": pmkf_p,
            "pmvst": pmvst,
            "identc": consts["ident"],
            "identf": np.eye(128, dtype=np.float32),
            "maskmul": consts["maskmul"],
            "costab": consts["costab"],
            "sintab": consts["sintab"],
            "indsel": consts["indsel"],
        })
    return maps


def kernel(**inputs):
    n_cores = 8
    maps = make_in_maps(inputs, n_cores)
    nc = build_nc(n_cores)
    res = run_bass_kernel_spmd(nc, maps, core_ids=list(range(n_cores)))
    out = np.concatenate([res.results[c]["y"] for c in range(n_cores)], axis=0)
    return out.astype(np.float32)


# version 14
# speedup vs baseline: 1.0883x; 1.0883x over previous
import contextlib
import numpy as np
import ml_dtypes
import concourse.bass as bass
import concourse.mybir as mybir
import bass_rust
from concourse import library_config
from concourse.bass_utils import run_bass_kernel_spmd

F32 = mybir.dt.float32
BF16 = mybir.dt.bfloat16
AF = mybir.ActivationFunctionType
ALU = mybir.AluOpType
AX = mybir.AxisListType
RED = bass_rust.ReduceOp

NB, NSEG, NT = 4, 4, 4
D, DC, H, DH = 1024, 8, 16, 64
FF, FC = 3072, 24
NTT, NG = 64, 16
MAGIC = float(np.float32(1.5 * 2 ** 23))
EPS_LN = 1e-5
EPS_RMS = 1.1920929e-07
CLIP = 1e12

ENGINES = ("sync", "tensor", "vector", "scalar", "gpsimd")


def host_constants():
    c = {}
    c["ident"] = np.eye(128, dtype=ml_dtypes.bfloat16)
    j = np.arange(128)[:, None]
    i = np.arange(128)[None, :]
    c["maskmul"] = (j <= i).astype(ml_dtypes.bfloat16)
    pos = np.arange(2048, dtype=np.float64)
    inv = 1.0 / 10000.0 ** (np.arange(0, DH, 2, dtype=np.float64) / DH)
    ang = pos[:, None] * inv[None, :]
    c["costab"] = np.cos(ang).astype(ml_dtypes.bfloat16).reshape(16, 128, 32) \
        .transpose(1, 0, 2).reshape(128, 512).copy()
    c["sintab"] = np.sin(ang).astype(ml_dtypes.bfloat16).reshape(16, 128, 32) \
        .transpose(1, 0, 2).reshape(128, 512).copy()
    ind = np.zeros((16, 1024), dtype=ml_dtypes.bfloat16)
    for h in range(16):
        ind[h, (h // 2) * 128 + (h % 2) * 64: (h // 2) * 128 + (h % 2) * 64 + 64] = 1.0
    c["indsel"] = ind
    return c


def qk_row_perm():
    per_head = np.concatenate([np.arange(0, DH, 2), np.arange(1, DH, 2)])
    one = np.concatenate([h * DH + per_head for h in range(H)])
    return np.concatenate([one, 1024 + one, 2048 + np.arange(1024)])


def build_nc(n_cores=8, max_phase=7, stub_cc=False, p2opts='dmae'):
    nc = bass.Bass("TRN2", target_bir_lowering=False)

    seq_d = nc.dram_tensor("seq", [NB, 2048, D], F32, kind="ExternalInput").ap()
    wqkv_d = nc.dram_tensor("wqkv", [3 * D, D], F32, kind="ExternalInput").ap()
    wout_d = nc.dram_tensor("wout", [D, D], F32, kind="ExternalInput").ap()
    wup_d = nc.dram_tensor("wup", [FF, D], F32, kind="ExternalInput").ap()
    wgate_d = nc.dram_tensor("wgate", [FF, D], F32, kind="ExternalInput").ap()
    wdown_d = nc.dram_tensor("wdown", [D, FF], F32, kind="ExternalInput").ap()
    g_d = nc.dram_tensor("gvec", [128, DC], F32, kind="ExternalInput").ap()
    pmkf_d = nc.dram_tensor("pmkf", [128, 256], F32, kind="ExternalInput").ap()
    pmvst_d = nc.dram_tensor("pmvst", [16, 1040], F32, kind="ExternalInput").ap()
    ident_d = nc.dram_tensor("identc", [128, 128], BF16, kind="ExternalInput").ap()
    identf_d = nc.dram_tensor("identf", [128, 128], F32, kind="ExternalInput").ap()
    mask_d = nc.dram_tensor("maskmul", [128, 128], BF16, kind="ExternalInput").ap()
    cos_d = nc.dram_tensor("costab", [128, 512], BF16, kind="ExternalInput").ap()
    sin_d = nc.dram_tensor("sintab", [128, 512], BF16, kind="ExternalInput").ap()
    ind_d = nc.dram_tensor("indsel", [16, 1024], BF16, kind="ExternalInput").ap()
    y_d = nc.dram_tensor("y", [NB, 2048, D], F32, kind="ExternalOutput").ap()

    x2_d = nc.dram_tensor("X2s", [NTT, 128, D], F32).ap()
    sig_d = nc.dram_tensor("SIGs", [NG, 128, 8], F32).ap()
    u_d = nc.dram_tensor("Usc", [FC, NG, 128, 512], BF16).ap()
    gg_d = nc.dram_tensor("Gsc", [FC, NG, 128, 512], BF16).ap()
    t_d = nc.dram_tensor("Tsc", [FC, NG, 128, 512], BF16).ap()
    o_d = nc.dram_tensor("Osc", [FC, NG, 128, 512], BF16).ap()
    cci = [nc.dram_tensor(f"cci{k}", [1, 8], F32).ap() for k in range(5)]
    cco = [nc.dram_tensor(f"cco{k}", [1, 8], F32).ap() for k in range(5)]
    rs_d = nc.dram_tensor("RSb", [16, 512], F32).ap()

    prog = {e: [] for e in ENGINES}
    cnt = {}

    def W(eng, semname, val):
        if val and val > 0:
            prog[eng].append(("w", semname, val))

    def OP(eng, fn, inc=None, n=1):
        tot = None
        if inc is not None:
            cnt[inc] = cnt.get(inc, 0) + n
            tot = cnt[inc]
        prog[eng].append(("o", fn, inc, n))
        return tot

    def DR(eng):
        prog[eng].append(("d",))

    def CV(semname):
        return cnt.get(semname, 0)

    ctx = contextlib.ExitStack()
    sbt = lambda name, shape, dt: ctx.enter_context(nc.sbuf_tensor(name, shape, dt))
    pst = lambda name, shape, dt: ctx.enter_context(nc.psum_tensor(name, shape, dt))

    with ctx:
        ident = sbt("identS", [128, 128], BF16)
        identf = sbt("identfS", [128, 128], F32)
        onesrow = sbt("onesrowS", [1, 128], F32)
        mcol8 = sbt("mcol8S", [1, 8], F32)
        mg8 = sbt("mg8S", [1, 8], F32)
        mask = sbt("maskS", [128, 128], BF16)
        indsel = sbt("indselS", [16, 1024], BF16)
        gsb = sbt("gsbS", [128, DC], F32)
        costab = sbt("costabS", [128, 512], BF16)
        sintab = sbt("sintabS", [128, 512], BF16)
        mx = sbt("mxS", [128, 1024], F32)
        MX_X, MX_U, MX_G, MX_T, MX_O = 0, 64, 448, 832, 928
        scl = sbt("sclS", [128, 16], F32)
        SXI, SX, SUI, SU, SGI, SG, SGQ, STI, STU, SOI, SO = range(11)
        cw = sbt("cwS", [128, 64], F32)
        st6 = sbt("st6S", [128, 12], F32)
        mvt = sbt("mvtS", [128, 2], F32)
        mvt4 = sbt("mvt4S", [128, 8], F32)
        sgst = sbt("sgstS", [128, 8], F32)
        cmag = sbt("cmagS", [128, 2], F32)

        A = [pst(f"psA{i}", [128, 512], F32) for i in range(6)]
        Bp = pst("psB", [128, 1024], BF16)
        Cp = pst("psC", [128, 512], F32)

        sems = {}
        for s in ("LC", "LW0", "LW1", "LW2", "LSEQ", "LX0", "LX1", "LWS0", "LWS1",
                  "LWD", "LG0", "LG1", "LU0", "LU1", "LXG", "LRS", "LRS0", "LRS1",
                  "LO0", "LO1", "LO2", "LO3", "LO4", "LO5", "LO6", "LO7", "GR0", "GR1",
                  "GD", "TD", "VD", "SD", "PD", "CC",
                  "SX0", "SX1", "SSG", "SG0", "SG1", "SG2", "SG3",
                  "STB0", "STB1", "SY0", "SY1"):
            sems[s] = ctx.enter_context(nc.semaphore(s))

        p1 = contextlib.ExitStack()
        s1 = lambda name, shape, dt: p1.enter_context(nc.sbuf_tensor(name, shape, dt))
        wqkvT = s1("wqkvT", [128, 24576], BF16)
        woutT = s1("woutT", [128, 8192], BF16)
        pmkf = s1("pmkfS", [128, 256], BF16)
        pmvst = s1("pmvstS", [16, 1040], BF16)
        pstage = contextlib.ExitStack()
        wldf = pstage.enter_context(nc.sbuf_tensor("wldf", [128, 3072], F32))
        wcst = pstage.enter_context(nc.sbuf_tensor("wcst", [128, 2048], BF16))
        pstage.close()
        seqsb = s1("seqsb", [128, 8192], F32)
        zbf = s1("zbf", [128, 4096], BF16)
        zfm = s1("zfm", [128, 4096], BF16)
        qksb = s1("qksb", [128, 8192], BF16)
        qfm = s1("qfm", [128, 4096], BF16)
        kfm = s1("kfm", [128, 4096], BF16)
        vst = s1("vst", [128, 4160], BF16)
        expt = s1("expt", [128, 3584], BF16)
        aoun = s1("aoun", [128, 4096], BF16)
        rsum = s1("rsum", [16, 512], BF16)
        rsum32 = s1("rsum32", [16, 512], F32)
        sumsb = s1("sumsb", [1, 1024], F32)
        rxp = s1("rxp", [128, 4096], BF16)
        x1t = s1("x1t", [128, 1024], F32)
        x2t = s1("x2t", [128, 1024], F32)
        rta = s1("rta", [128, 256], BF16)
        qke = s1("qke", [128, 2048], BF16)
        rtb = s1("rtb", [128, 256], BF16)
        p1.close()

        p2 = contextlib.ExitStack()
        s2 = lambda name, shape, dt: p2.enter_context(nc.sbuf_tensor(name, shape, dt))
        xqfm = s2("xqfm", [128, 65536], BF16)
        wstr = s2("wstr", [128, 2048], F32)
        mxb32 = s2("mxb32", [128, 32], F32)
        rmxb = s2("rmxb", [128, 32], F32)
        sinvb = s2("sinvb", [128, 32], F32)
        sblk = s2("sblkS", [128, 32], F32)
        wq1 = s2("wq1", [128, 1024], F32)
        wr2 = s2("wr2", [128, 1024], F32)
        wdqb = s2("wdqb", [128, 1024], BF16)
        wtr = s2("wtrS", [128, 2048], BF16)
        stg = s2("stgS", [128, 2048], BF16)
        p2a = contextlib.ExitStack()
        xin = p2a.enter_context(nc.sbuf_tensor("xinS", [128, 2048], F32))
        xt1 = p2a.enter_context(nc.sbuf_tensor("xt1S", [128, 1024], F32))
        xr2 = p2a.enter_context(nc.sbuf_tensor("xr2S", [128, 1024], F32))
        xqtm = p2a.enter_context(nc.sbuf_tensor("xqtm", [128, 1024], BF16))
        p2a.close()
        p2b = contextlib.ExitStack()
        wdnin = p2b.enter_context(nc.sbuf_tensor("wdnin", [128, 1536], F32))
        wdn1 = p2b.enter_context(nc.sbuf_tensor("wdn1", [128, 1536], F32))
        wdnr = p2b.enter_context(nc.sbuf_tensor("wdnr", [128, 1536], F32))
        wdnb = p2b.enter_context(nc.sbuf_tensor("wdnb", [128, 1536], BF16))
        mxb48 = p2b.enter_context(nc.sbuf_tensor("mxb48", [128, 48], F32))
        rmx48 = p2b.enter_context(nc.sbuf_tensor("rmx48", [128, 48], F32))
        siv48 = p2b.enter_context(nc.sbuf_tensor("siv48", [128, 48], F32))
        sbk48 = p2b.enter_context(nc.sbuf_tensor("sbk48", [128, 48], F32))
        p2b.close()
        p2.close()

        p2c = contextlib.ExitStack()
        s2c = lambda name, shape, dt: p2c.enter_context(nc.sbuf_tensor(name, shape, dt))
        wdT = s2c("wdT", [128, 24576], BF16)
        gin = s2c("ginS", [128, 4096], BF16)
        uin = s2c("uinS", [128, 4096], BF16)
        p34 = contextlib.ExitStack()
        s34 = lambda name, shape, dt: p34.enter_context(nc.sbuf_tensor(name, shape, dt))
        tb1 = s34("tb1S", [128, 4096], F32)
        rb3 = s34("rb3S", [128, 4096], F32)
        sigb = s34("sigbS", [128, 4096], F32)
        a3b = s34("a3bS", [128, 4096], F32)
        c2b = s34("c2bS", [128, 4096], F32)
        tob = s34("tobS", [128, 4096], BF16)
        p34.close()
        p4b = contextlib.ExitStack()
        s4b = lambda name, shape, dt: p4b.enter_context(nc.sbuf_tensor(name, shape, dt))
        oqb = s4b("oqbS", [128, 24576], BF16)
        x2g = s4b("x2gS", [128, 8192], F32)
        lnb = s4b("lnbS", [128, 1024], F32)
        x1b = s4b("x1bS", [128, 1024], F32)
        outb = s4b("outbS", [128, 2048], F32)
        ot1 = s4b("ot1S", [128, 512], F32)
        ot2 = s4b("ot2S", [128, 1024], F32)
        sg4b = s4b("sg4bS", [128, 16], F32)
        p4b.close()
        p2c.close()

        blk = ctx.enter_context(nc.Block())

        def dma(eng, sem, dst, src):
            return OP(eng, lambda e, d=dst, s=src: e.dma_start(out=d, in_=s), sem, 16)

        ld_const = 0
        for dst, srcap in ((ident[:], ident_d), (identf[:], identf_d),
                           (mask[:], mask_d), (indsel[:], ind_d),
                           (gsb[:], g_d),
                           (costab[:], cos_d), (sintab[:], sin_d)):
            ld_const = dma("sync", "LC", dst, srcap)
        W("tensor", "LC", ld_const)
        W("vector", "LC", ld_const)
        W("scalar", "LC", ld_const)
        gld_pm = dma("gpsimd", "GD", pmkf[:], pmkf_d)
        gld_pm = dma("gpsimd", "GD", pmvst[:], pmvst_d)
        OP("vector", lambda e: e.memset(onesrow[:], 1.0))
        OP("vector", lambda e: e.memset(mcol8[:], 0.0))
        OP("vector", lambda e: e.memset(cmag[:, 0:1], MAGIC))
        OP("vector", lambda e: e.memset(cmag[:, 1:2], -MAGIC))
        vst_ones = OP("vector", lambda e: e.memset(
            vst[:].rearrange("p (t h c) -> p t h c", h=16, c=65)[:, :, :, 64:65], 1.0), "VD")

        vcast_done = {}
        ttr_done = {}
        evac_done = {}
        prep_jobs = [("qkv", m) for m in range(24)] + [("out", m) for m in range(8)]
        for gi, (kind, m) in enumerate(prep_jobs):
            lslot = (gi % 3) * 1024
            cslot = (gi % 2) * 1024
            if gi >= 3:
                W("sync", "VD", vcast_done[gi - 3])
            src_rows = (wqkv_d if kind == "qkv" else wout_d)[m * 128:(m + 1) * 128, :]
            lt = dma("sync", "LW%d" % (gi % 3), wldf[:, lslot:lslot + 1024], src_rows)
            W("vector", "LW%d" % (gi % 3), lt)
            if gi >= 2:
                W("vector", "TD", ttr_done[gi - 2])
            vcast_done[gi] = OP("vector", lambda e, ls=lslot, cs=cslot: e.tensor_copy(
                wcst[:, cs:cs + 1024], wldf[:, ls:ls + 1024]), "VD")
            W("tensor", "VD", vcast_done[gi])
            if gi >= 1:
                W("tensor", "SD", evac_done[gi - 1])
            for k in range(DC):
                ttr_done[gi] = OP("tensor", lambda e, cs=cslot, kk=k: e.transpose(
                    Bp[:, kk * 128:kk * 128 + 128],
                    wcst[:, cs + kk * 128:cs + kk * 128 + 128], ident[:]),
                    "TD" if k == DC - 1 else None)
            W("scalar", "TD", ttr_done[gi])
            if kind == "qkv":
                for k in range(DC):
                    evac_done[gi] = OP("scalar", lambda e, mm=m, kk=k: e.activation(
                        wqkvT[:, kk * 3072 + mm * 128: kk * 3072 + mm * 128 + 128],
                        Bp[:, kk * 128:kk * 128 + 128], AF.Identity,
                        scale=gsb[:, kk:kk + 1]), "SD" if k == DC - 1 else None)
            else:
                evac_done[gi] = OP("scalar", lambda e, mm=m: e.copy(
                    woutT[:].rearrange("p (k x) -> p k x", x=1024)
                    [:, :, mm * 128:mm * 128 + 128],
                    Bp[:].rearrange("p (k x) -> p k x", x=128)), "SD")
        bp_free_sd = evac_done[len(prep_jobs) - 1]


        if max_phase >= 1:
            rope_done = {}
            rope_bank = {}
            qke_sd = {}
            vevac_done = {}
            zfm_done = {}
            qkv_done = {}
            exp_done = {}
            mask_done = {}
            av_done = {}
            x2_store_st = {}
            sg_store_st = 0
            woutln_vd = 0
            norm_vd = 0
            rec_done = 0
            ao_done = 0
            prev_norm_vd = 0
            last_rmm_td = 0
            rs_row_st = {}
            rsum32_vd = 0

            z_done_map = {}
            ld_seq_of = {}
            seq_par_read = {0: 0, 1: 0}

            def emit_AB(un):
                    bb, ss = un // 4, un % 4
                    sh = (un % 2) * 4096
                    if un == 0:
                        W("sync", "VD", CV("VD"))
                        W("sync", "TD", CV("TD"))
                    W("sync", "VD", seq_par_read[un % 2])
                    ld_seq = dma("sync", "LSEQ",
                                 seqsb[:, sh:sh + 4096].rearrange("p (t d) -> p t d", d=1024),
                                 seq_d[bb, ss * 512:(ss + 1) * 512, :]
                                 .rearrange("(t p) d -> p t d", p=128))
                    ld_seq_of[un] = ld_seq
                    if un:
                        W("vector", "SD", z_done_map[(un - 1, NT - 1)])
                        DR("vector")
                        DR("scalar")
                    bv1 = None
                    for t in range(NT):
                        c0 = 32 + t * 8
                        sq = seqsb[:, sh + t * 1024: sh + (t + 1) * 1024]
                        W("vector", "LSEQ", ld_seq)
                        if t:
                            DR("vector")
                        OP("vector", lambda e, sq=sq: e.bn_stats(st6[:, 0:6], sq[:, 0:512]))
                        OP("vector", lambda e, sq=sq: e.bn_stats(st6[:, 6:12], sq[:, 512:1024]))
                        DR("vector")
                        OP("vector", lambda e, t=t: e.bn_aggr(mvt4[:, 2 * t:2 * t + 2], st6[:]))
                        DR("vector")
                        OP("vector", lambda e, t=t, c0=c0: e.tensor_scalar(
                            cw[:, c0:c0 + 1], mvt4[:, 2 * t + 1:2 * t + 2],
                            EPS_LN, None, ALU.add))
                        DR("vector")
                        OP("vector", lambda e, c0=c0: e.reciprocal(cw[:, c0 + 1:c0 + 2],
                                                                   cw[:, c0:c0 + 1]))
                        DR("vector")
                        OP("vector", lambda e, t=t, c0=c0: e.tensor_tensor(
                            cw[:, c0 + 4:c0 + 5], mvt4[:, 2 * t + 1:2 * t + 2],
                            cw[:, c0 + 1:c0 + 2], ALU.mult))
                        DR("vector")
                        OP("vector", lambda e, c0=c0: e.tensor_scalar(
                            cw[:, c0 + 5:c0 + 6], cw[:, c0 + 4:c0 + 5],
                            EPS_RMS, None, ALU.add))
                        DR("vector")
                        bv1 = OP("vector", lambda e, c0=c0: e.reciprocal(
                            cw[:, c0 + 6:c0 + 7], cw[:, c0 + 5:c0 + 6]),
                            "VD" if t == NT - 1 else None)
                    W("scalar", "VD", bv1)
                    for t in range(NT):
                        c0 = 32 + t * 8
                        bs1 = OP("scalar", lambda e, c0=c0: e.activation(
                            cw[:, c0 + 2:c0 + 3], cw[:, c0 + 1:c0 + 2], AF.Sqrt))
                        bs1 = OP("scalar", lambda e, c0=c0: e.activation(
                            cw[:, c0 + 7:c0 + 8], cw[:, c0 + 6:c0 + 7], AF.Sqrt),
                            "SD" if t == NT - 1 else None)
                    W("vector", "SD", bs1)
                    bv2 = None
                    for t in range(NT):
                        c0 = 32 + t * 8
                        OP("vector", lambda e, c0=c0: e.tensor_tensor(
                            cw[:, c0 + 3:c0 + 4], cw[:, c0 + 2:c0 + 3],
                            cw[:, c0 + 7:c0 + 8], ALU.mult))
                        DR("vector")
                        bv2 = OP("vector", lambda e, t=t, c0=c0: e.tensor_scalar(
                            cw[:, c0 + 4:c0 + 5], mvt4[:, 2 * t:2 * t + 1],
                            cw[:, c0 + 3:c0 + 4], -1.0, ALU.mult, ALU.mult),
                            "VD" if t == NT - 1 else None)
                        if t != NT - 1:
                            DR("vector")
                    W("scalar", "VD", bv2)
                    for t in range(NT):
                        c0 = 32 + t * 8
                        sq = seqsb[:, sh + t * 1024: sh + (t + 1) * 1024]
                        if un and t == 0:
                            W("scalar", "TD", qkv_done[(un - 1, NT - 1)])
                        z_done_map[(un, t)] = OP("scalar", lambda e, t=t, sq=sq, c0=c0: e.activation(
                            zbf[:, t * 1024:(t + 1) * 1024], sq, AF.Identity,
                            scale=cw[:, c0 + 3:c0 + 4], bias=cw[:, c0 + 4:c0 + 5]), "SD")

            emit_AB(0)
            for u in range(16):
                b, s = u // 4, u % 4
                ld_seq = ld_seq_of[u]
                for t in range(NT):
                    gt = s * 4 + t
                    W("tensor", "SD", z_done_map[(u, t)])
                    W("tensor", "SD", bp_free_sd)
                    if u and t == 0:
                        W("tensor", "TD", qkv_done[(u - 1, NT - 1)])
                    for k in range(DC):
                        ztr = OP("tensor", lambda e, t=t, kk=k: e.transpose(
                            Bp[:, kk * 128:kk * 128 + 128],
                            zbf[:, t * 1024 + kk * 128: t * 1024 + kk * 128 + 128],
                            ident[:]), "TD" if k == DC - 1 else None)
                    W("scalar", "TD", ztr)
                    zfm_done[t] = OP("scalar", lambda e, t=t: e.copy(
                        zfm[:].rearrange("p (k x) -> p k x", x=512)[:, :, t * 128:t * 128 + 128],
                        Bp[:].rearrange("p (k x) -> p k x", x=128)), "SD")
                    bp_free_sd = zfm_done[t]
                    W("tensor", "SD", zfm_done[t])
                    pt = (u, t - 1) if t else ((u - 1, NT - 1) if u else None)
                    if u and t == 0:
                        W("tensor", "VD", woutln_vd)
                    for k in range(DC):
                        for jj in range(6):
                            if k == 0 and pt is not None:
                                if jj < 4:
                                    W("tensor", "SD", qke_sd[(pt[0], pt[1], jj)])
                                elif jj == 4:
                                    W("tensor", "SD", vevac_done[pt])
                            qkv_done[(u, t)] = OP("tensor", lambda e, t=t, kk=k, j=jj: e.matmul(
                                A[j][:],
                                zfm[:, kk * 512 + t * 128: kk * 512 + t * 128 + 128],
                                wqkvT[:, kk * 3072 + j * 512: kk * 3072 + (j + 1) * 512],
                                start=(kk == 0), stop=(kk == DC - 1)),
                                "TD" if (k == DC - 1 and jj == 5) else None)
                    W("scalar", "TD", qkv_done[(u, t)])
                    for bank in range(4):
                        if (u, t) != (0, 0):
                            W("scalar", "VD", rope_bank.get(
                                (u, t - 1, bank) if t else (u - 1, NT - 1, bank), 0))
                        qke_sd[(u, t, bank)] = OP("scalar", lambda e, b=bank: e.copy(
                            qke[:, b * 512:(b + 1) * 512], A[b][:]), "SD")
                    if u or t:
                        DR("vector")
                    for bank in range(4):
                        base = t * 2048 + bank * 512
                        W("vector", "SD", qke_sd[(u, t, bank)])
                        Eap = qke[:, bank * 512:(bank + 1) * 512].rearrange(
                            "p (h x) -> p h x", x=64)[:, :, 0:32]
                        Oap = qke[:, bank * 512:(bank + 1) * 512].rearrange(
                            "p (h x) -> p h x", x=64)[:, :, 32:64]
                        cosap = costab[:, gt * 32:(gt + 1) * 32] \
                            .rearrange("p (o x) -> p o x", o=1).broadcast_to([128, 8, 32])
                        sinap = sintab[:, gt * 32:(gt + 1) * 32] \
                            .rearrange("p (o x) -> p o x", o=1).broadcast_to([128, 8, 32])
                        r1v = rta[:].rearrange("p (h x) -> p h x", x=32)
                        r2v = rtb[:].rearrange("p (h x) -> p h x", x=32)
                        outE = qksb[:, base:base + 512].rearrange(
                            "p (h x) -> p h x", x=64)[:, :, 0:32]
                        outO = qksb[:, base:base + 512].rearrange(
                            "p (h x) -> p h x", x=64)[:, :, 32:64]
                        OP("vector", lambda e, a=Eap, c=cosap, o=r1v: e.tensor_tensor(o, a, c, ALU.mult))
                        OP("vector", lambda e, a=Oap, c=sinap, o=r2v: e.tensor_tensor(o, a, c, ALU.mult))
                        DR("vector")
                        OP("vector", lambda e, o=outE, x=r1v, y=r2v: e.tensor_tensor(o, x, y, ALU.subtract))
                        DR("vector")
                        OP("vector", lambda e, a=Oap, c=cosap, o=r1v: e.tensor_tensor(o, a, c, ALU.mult))
                        OP("vector", lambda e, a=Eap, c=sinap, o=r2v: e.tensor_tensor(o, a, c, ALU.mult))
                        DR("vector")
                        rope_done[(u, t)] = OP("vector", lambda e, o=outO, x=r1v, y=r2v:
                                               e.tensor_tensor(o, x, y, ALU.add), "VD")
                        rope_bank[(u, t, bank)] = rope_done[(u, t)]
                        if bank != 3:
                            DR("vector")
                    W("scalar", "TD", qkv_done[(u, t)])
                    for half in range(2):
                        dstv = vst[:, t * 1040 + half * 8 * 65: t * 1040 + (half * 8 + 8) * 65] \
                            .rearrange("p (h c) -> p h c", c=65)[:, :, 0:64]
                        vevac_done[(u, t)] = OP("scalar", lambda e, hh=half, dv=dstv: e.copy(
                            dv, A[4 + hh][:].rearrange("p (h x) -> p h x", x=64)),
                            "SD" if half == 1 else None)
                qkfm_sd = None
                for t in range(NT):
                    for part, dst in ((0, qfm), (1, kfm)):
                        W("tensor", "VD", rope_done[(u, t)])
                        W("tensor", "SD", bp_free_sd)
                        for c in range(DC):
                            ftr = OP("tensor", lambda e, t=t, pp=part, cc=c: e.transpose(
                                Bp[:, cc * 128:cc * 128 + 128],
                                qksb[:, t * 2048 + pp * 1024 + cc * 128:
                                     t * 2048 + pp * 1024 + cc * 128 + 128],
                                ident[:]), "TD" if c == DC - 1 else None)
                        W("scalar", "TD", ftr)
                        qkfm_sd = OP("scalar", lambda e, t=t, dd=dst: e.copy(
                            dd[:].rearrange("p (c x) -> p c x", x=512)[:, :, t * 128:t * 128 + 128],
                            Bp[:].rearrange("p (c x) -> p c x", x=128)), "SD")
                        bp_free_sd = qkfm_sd
                sc_all = {}
                for h in range(H + 1):
                    if h < H:
                        par, ch, pb = h % 2, h // 2, (h % 2) * 64
                        qc = ch * 512
                        _off = [0, 512, 896, 1152, 1280]
                        ebs = [par * 1792 + _off[jc] for jc in range(5)]
                        sc_done = {}
                        if h >= 1:
                            for jc in range(5):
                                W("tensor", "SD", exp_done[(u, h - 1, jc)])
                        else:
                            W("tensor", "SD", qkfm_sd)
                            if u == 0:
                                W("tensor", "GD", gld_pm)
                        sc_done[4] = OP("tensor", lambda e, pb=pb, h=h, qc=qc: e.matmul(
                            A[4][0:16, :], pmkf[pb:pb + 64, h * 16:(h + 1) * 16],
                            qfm[pb:pb + 64, qc:qc + 512], start=True, stop=True), "TD")
                        for jc in range(4):
                            sc_done[jc] = OP("tensor", lambda e, pb=pb, qc=qc, jc=jc: e.matmul(
                                A[jc][:, jc * 128:512],
                                kfm[pb:pb + 64, qc + jc * 128: qc + jc * 128 + 128],
                                qfm[pb:pb + 64, qc + jc * 128: qc + 512],
                                start=True, stop=True), "TD")
                        sc_all[h] = sc_done
                        for jc in (4, 0, 1, 2, 3):
                            W("scalar", "TD", sc_done[jc])
                            if h >= 2:
                                W("scalar", "TD", av_done[(u, h - 2)])
                                if jc < 4:
                                    W("scalar", "VD", mask_done[(u, h - 2, jc)])
                            elif u and h < 2:
                                W("scalar", "TD", av_done[(u - 1, h + 14)])
                                if jc < 4:
                                    W("scalar", "VD", mask_done[(u - 1, h + 14, jc)])
                            if jc == 4:
                                exp_done[(u, h, 4)] = OP("scalar", lambda e, b4=ebs[4]: e.activation(
                                    expt[0:16, b4:b4 + 512], A[4][0:16, :],
                                    AF.Exp, scale=0.125), "SD")
                            else:
                                wdt = 512 - jc * 128
                                exp_done[(u, h, jc)] = OP("scalar",
                                    lambda e, jc=jc, bj=ebs[jc], w=wdt: e.activation(
                                        expt[:, bj: bj + w],
                                        A[jc][:, jc * 128:512], AF.Exp, scale=0.125), "SD")
                        for jc in range(4):
                            W("vector", "SD", exp_done[(u, h, jc)])
                            dg = expt[:, ebs[jc]: ebs[jc] + 128]
                            mask_done[(u, h, jc)] = OP("vector", lambda e, d=dg: e.tensor_tensor(
                                d, d, mask[:], ALU.mult), "VD")
                    if h >= 1:
                        hh2 = h - 1
                        par2 = hh2 % 2
                        _off = [0, 512, 896, 1152, 1280]
                        eb2 = [par2 * 1792 + _off[jc] for jc in range(5)]
                        W("tensor", "SD", exp_done[(u, hh2, 4)])
                        for jc in range(4):
                            W("tensor", "SD", exp_done[(u, hh2, jc)])
                            W("tensor", "VD", mask_done[(u, hh2, jc)])
                        if hh2 >= 1 or u:
                            W("tensor", "VD", rec_done)
                            W("tensor", "SD", ao_done)
                        OP("tensor", lambda e, h=hh2, b4=eb2[4]: e.matmul(
                            A[5][0:65, :], pmvst[0:16, h * 65:(h + 1) * 65],
                            expt[0:16, b4:b4 + 512], start=True, stop=False))
                        for jc in range(4):
                            wdt = 512 - jc * 128
                            av_done[(u, hh2)] = OP("tensor",
                                lambda e, h=hh2, jc=jc, bj=eb2[jc], w=wdt: e.matmul(
                                    A[5][0:65, jc * 128:512],
                                    vst[:, jc * 1040 + h * 65: jc * 1040 + h * 65 + 65],
                                    expt[:, bj: bj + w],
                                    start=False, stop=(jc == 3)), "TD" if jc == 3 else None)
                        W("vector", "TD", av_done[(u, hh2)])
                        if hh2 >= 2:
                            W("vector", "GR%d" % (hh2 % 2), rs_row_st[(u, hh2 - 2)])
                        elif u and hh2 < 2:
                            W("vector", "GR%d" % (hh2 % 2), rs_row_st[(u - 1, hh2 + 14)])
                        rec_done = OP("vector", lambda e, h=hh2: e.tensor_copy(
                            sumsb[0:1, (h % 2) * 512:(h % 2) * 512 + 512],
                            A[5][64:65, :]), "VD")
                        W("gpsimd", "VD", rec_done)
                        rs_row_st[(u, hh2)] = dma("gpsimd", "GR%d" % (hh2 % 2),
                                                  rs_d[hh2:hh2 + 1, :],
                                                  sumsb[0:1, (hh2 % 2) * 512:(hh2 % 2) * 512 + 512])
                        W("scalar", "TD", av_done[(u, hh2)])
                        W("scalar", "VD", rec_done)
                        if hh2 == 0:
                            W("scalar", "VD", prev_norm_vd)
                        ao_done = OP("scalar", lambda e, h=hh2, pb2=(hh2 % 2) * 64, ch2=hh2 // 2: e.copy(
                            aoun[pb2:pb2 + 64, ch2 * 512:(ch2 + 1) * 512], A[5][0:64, :]), "SD")
                rxp_vd = {}
                if u >= 1:
                    W("sync", "TD", last_rmm_td)
                W("sync", "GR0", CV("GR0"))
                W("sync", "GR1", CV("GR1"))
                if u >= 1:
                    W("sync", "VD", rsum32_vd)
                ld_rs = dma("sync", "LRS", rsum32[:], rs_d)
                W("vector", "LRS", ld_rs)
                DR("vector")

                if u >= 1:
                    W("vector", "TD", last_rmm_td)

                def _recb(e):
                    with nc.allow_low_precision(reason="softmax normalizer as bf16 matmul operand"):
                        return e.reciprocal(rsum[:], rsum32[:])

                rsum32_vd = OP("vector", _recb, "VD")
                for p in range(DC):
                    W("tensor", "VD", rsum32_vd)
                    if p >= 1:
                        W("tensor", "VD", rxp_vd[p - 1])
                    elif u:
                        W("tensor", "VD", woutln_vd)
                    rtd = OP("tensor", lambda e, p=p: e.matmul(
                        Cp[:], indsel[0:16, p * 128:(p + 1) * 128], rsum[0:16, :],
                        start=True, stop=True), "TD")
                    last_rmm_td = rtd
                    W("vector", "TD", rtd)
                    rxp_vd[p] = OP("vector", lambda e, p=p: e.tensor_copy(
                        rxp[:, p * 512:(p + 1) * 512], Cp[:]), "VD")
                DR("vector")
                W("vector", "SD", ao_done)
                for p in range(DC):
                    norm_vd = OP("vector", lambda e, p=p: e.tensor_tensor(
                        aoun[:, p * 512:(p + 1) * 512], aoun[:, p * 512:(p + 1) * 512],
                        rxp[:, p * 512:(p + 1) * 512], ALU.mult), "VD")
                prev_norm_vd = norm_vd
                if u + 1 < 16:
                    emit_AB(u + 1)
                wout_vd_par = {}
                for t in range(NT):
                    bw = (t % 2) * 2
                    W("tensor", "VD", norm_vd)
                    if t >= 2:
                        W("tensor", "VD", wout_vd_par[t - 2])
                    for k in range(DC):
                        for jj in range(2):
                            wmm = OP("tensor", lambda e, t=t, kk=k, j=jj, bw=bw: e.matmul(
                                A[bw + j][:],
                                aoun[:, kk * 512 + t * 128: kk * 512 + t * 128 + 128],
                                woutT[:, kk * 1024 + j * 512: kk * 1024 + (j + 1) * 512],
                                start=(kk == 0), stop=(kk == DC - 1)),
                                "TD" if (k == DC - 1 and jj == 1) else None)
                    W("vector", "TD", wmm)
                    DR("vector")
                    DR("scalar")
                    OP("vector", lambda e, bw=bw: e.bn_stats(st6[:, 0:6], A[bw][:]))
                    OP("vector", lambda e, bw=bw: e.bn_stats(st6[:, 6:12], A[bw + 1][:]))
                    DR("vector")
                    OP("vector", lambda e: e.bn_aggr(mvt[:], st6[:]))
                    DR("vector")
                    OP("vector", lambda e: e.tensor_scalar(cw[:, 10:11], mvt[:, 1:2],
                                                           EPS_LN, None, ALU.add))
                    DR("vector")
                    vv = OP("vector", lambda e: e.reciprocal(cw[:, 11:12], cw[:, 10:11]), "VD")
                    W("scalar", "VD", vv)
                    sv = OP("scalar", lambda e: e.activation(cw[:, 12:13], cw[:, 11:12], AF.Sqrt), "SD")
                    W("vector", "SD", sv)
                    for half in range(2):
                        OP("vector", lambda e, hh=half, bw=bw: e.tensor_scalar(
                            x1t[:, hh * 512:(hh + 1) * 512], A[bw + hh][:],
                            mvt[:, 0:1], cw[:, 12:13], ALU.subtract, ALU.mult))
                    DR("vector")
                    OP("vector", lambda e, t=t, shu=(u % 2) * 4096: e.tensor_tensor(
                        x1t[:], x1t[:],
                        seqsb[:, shu + t * 1024: shu + (t + 1) * 1024], ALU.add))
                    DR("vector")
                    OP("vector", lambda e: e.bn_stats(st6[:, 0:6], x1t[:, 0:512]))
                    OP("vector", lambda e: e.bn_stats(st6[:, 6:12], x1t[:, 512:1024]))
                    DR("vector")
                    OP("vector", lambda e: e.bn_aggr(mvt[:], st6[:]))
                    DR("vector")
                    OP("vector", lambda e: e.tensor_scalar(cw[:, 14:15], mvt[:, 1:2],
                                                           EPS_LN, None, ALU.add))
                    DR("vector")
                    vv = OP("vector", lambda e: e.reciprocal(cw[:, 15:16], cw[:, 14:15]), "VD")
                    seq_par_read[u % 2] = vv
                    W("scalar", "VD", vv)
                    sv = OP("scalar", lambda e: e.activation(cw[:, 16:17], cw[:, 15:16], AF.Sqrt))
                    sv = OP("scalar", lambda e: e.activation(cw[:, 17:18], cw[:, 14:15], AF.Sqrt), "SD", 1)
                    W("vector", "SD", sv)
                    slot = 0
                    if (u * 4 + t - 1) >= 0:
                        W("vector", "SX%d" % ((t - 1) % 2), x2_store_st.get(u * 4 + t - 1, 0))
                    OP("vector", lambda e, sl=slot: e.tensor_scalar(
                        x2t[:, sl:sl + 1024], x1t[:], mvt[:, 0:1], cw[:, 16:17],
                        ALU.subtract, ALU.mult))
                    DR("vector")
                    OP("vector", lambda e, u=u, t=t, sl=slot: e.tensor_reduce(
                        mx[:, MX_X + u * 4 + t: MX_X + u * 4 + t + 1],
                        x2t[:, sl:sl + 1024], axis=AX.X, op=ALU.max,
                        apply_absolute_value=True))
                    OP("vector", lambda e, t=t: e.tensor_copy(sgst[:, t * 2:t * 2 + 1], cw[:, 17:18]))
                    woutln_vd = OP("vector", lambda e, t=t: e.tensor_copy(
                        sgst[:, t * 2 + 1:t * 2 + 2], mvt[:, 0:1]), "VD")
                    wout_vd_par[t] = woutln_vd
                    W("scalar", "VD", woutln_vd)
                    x2_store_st[u * 4 + t] = dma("scalar", "SX%d" % (t % 2),
                                                 x2_d[u * 4 + t], x2t[:, slot:slot + 1024])
                W("scalar", "VD", woutln_vd)
                if u >= 1:
                    pass
                sg_store_st = dma("scalar", "SSG", sig_d[u], sgst[:])
                if u < 15:
                    W("vector", "SSG", sg_store_st)

        if max_phase >= 2:
            DR("vector")
            OP("vector", lambda e: e.tensor_reduce(cw[:, 18:19], mx[:, MX_X:MX_X + 64],
                                                   axis=AX.X, op=ALU.max))
            DR("vector")
            mxv = OP("vector", lambda e: e.tensor_copy(cw[:, 18:19], cw[:, 18:19]), "VD")

            cc_state = {"btd": 0}

            def collective(idx, col_ap, after_vd, out_col_ap):
                W("tensor", "VD", after_vd)
                ttd = OP("tensor", lambda e, c=col_ap: e.transpose(
                    Cp[0:1, 0:128], c, identf[:]), "TD")
                W("vector", "TD", ttd)
                DR("vector")
                mv_ = OP("vector", lambda e: e.reduce_max(
                    mcol8[0:1, 0:1], Cp[0:1, 0:128], axis=AX.X), "VD")
                W("gpsimd", "VD", mv_)
                W("gpsimd", "TD", cc_state["btd"])
                gd = dma("gpsimd", "GD", cci[idx], mcol8[:])
                W("gpsimd", "GD", gd)
                ccv = OP("gpsimd", lambda e, i=idx: e.collective_compute(
                    "AllReduce", ALU.max, replica_groups=[list(range(n_cores))],
                    ins=[cci[i].opt()], outs=[cco[i].opt()]), "CC")
                W("gpsimd", "CC", ccv)
                gd = dma("gpsimd", "GD", mg8[:], cco[idx])
                W("tensor", "GD", gd)
                btd = OP("tensor", lambda e: e.matmul(
                    Cp[0:128, 0:1], onesrow[0:1, 0:128], mg8[0:1, 0:1],
                    start=True, stop=True), "TD")
                cc_state["btd"] = btd
                W("vector", "TD", btd)
                vv = OP("vector", lambda e, o=out_col_ap: e.tensor_copy(
                    o, Cp[0:128, 0:1]), "VD")
                return vv

            collective(0, cw[:, 18:19], mxv, cw[:, 19:20])
            DR("vector")
            OP("vector", lambda e: e.reciprocal(cw[:, 20:21], cw[:, 19:20]))
            DR("vector")
            OP("vector", lambda e: e.tensor_scalar(scl[:, SXI:SXI + 1], cw[:, 20:21],
                                                   32767.0, CLIP, ALU.mult, ALU.min))
            DR("vector")
            sc_x_vd = OP("vector", lambda e: e.reciprocal(scl[:, SX:SX + 1],
                                                          scl[:, SXI:SXI + 1]), "VD")

            def barrier():
                vals = dict(cnt)
                for e in ENGINES:
                    for k, v in vals.items():
                        W(e, k, v)
            barrier()

        if max_phase >= 2.2:
            xq_sd = {}
            xq_td = {}
            xq_vd_hist = {}
            NTT_RUN = NTT if max_phase >= 2.5 else 8
            for i in range(NTT_RUN):
                slot = (i % 2) * 1024
                if i >= 2:
                    W("sync", "VD", xq_vd_hist[i - 2])
                ldt = dma("sync", "LX%d" % (i % 2), xin[:, slot:slot + 1024],
                          x2_d[i][:, :])
                W("vector", "LX%d" % (i % 2), ldt)
                W("vector", "VD", sc_x_vd)
                if i >= 1:
                    W("vector", "TD", xq_td[i - 1])
                    DR("vector")
                OP("vector", lambda e, sl=slot: e.tensor_scalar(
                    xt1[:], xin[:, sl:sl + 1024], scl[:, SXI:SXI + 1], None, ALU.mult))
                DR("vector")
                OP("vector", lambda e: e.tensor_scalar(xr2[:], xt1[:], MAGIC, MAGIC,
                                                       ALU.add, ALU.subtract))
                DR("vector")
                xq_vd = OP("vector", lambda e: e.tensor_scalar(
                    xqtm[:], xr2[:], scl[:, SX:SX + 1], None, ALU.mult), "VD")
                xq_vd_hist[i] = xq_vd
                W("tensor", "VD", xq_vd)
                W("tensor", "SD", bp_free_sd)
                for k in range(DC):
                    xq_td[i] = OP("tensor", lambda e, kk=k: e.transpose(
                        Bp[:, kk * 128:kk * 128 + 128],
                        xqtm[:, kk * 128:kk * 128 + 128], ident[:]),
                        "TD" if k == DC - 1 else None)
                W("scalar", "TD", xq_td[i])
                xq_sd[i] = OP("scalar", lambda e, i=i: e.copy(
                    xqfm[:].rearrange("p (k x) -> p k x", x=8192)[:, :, i * 128:i * 128 + 128],
                    Bp[:].rearrange("p (k x) -> p k x", x=128)), "SD")
                bp_free_sd = xq_sd[i]
            xq_all_sd = xq_sd[NTT_RUN - 1]

            bank_vd = [0, 0, 0, 0]
            bank_sd = [0, 0, 0, 0]
            stg_st = [0, 0, 0, 0]
            wq1_done = {}
            wtr_sd_of = {}
            last_mm_td = 0
            mm_last_of = {}
            CC2_RUN = 48 if max_phase >= 2.75 else (4 if max_phase >= 2.6 else 0)

            def srcw(cc2):
                m, wg = cc2 // 2, cc2 % 2
                return (wup_d if wg == 0 else wgate_d)[m * 128:(m + 1) * 128, :]

            def emit_dequant(cc2):
                slot = (cc2 % 2) * 1024
                W("vector", "LWS%d" % (cc2 % 2), ld_of[cc2])
                if cc2 >= 1:
                    W("vector", "TD", wtr_td_of[cc2 - 1])
                DR("vector")
                OP("vector", lambda e, sl=slot: e.tensor_reduce(
                    mxb32[:], wstr[:, sl:sl + 1024].rearrange("p (a b) -> p a b", b=32),
                    axis=AX.X, op=ALU.max, apply_absolute_value=True))
                DR("vector")
                OP("vector", lambda e: e.reciprocal(rmxb[:], mxb32[:]))
                DR("vector")
                OP("vector", lambda e: e.tensor_scalar(sinvb[:], rmxb[:], 7.0,
                                                       CLIP, ALU.mult, ALU.min))
                DR("vector")
                OP("vector", lambda e: e.reciprocal(sblk[:], sinvb[:]))
                DR("vector")
                wq1_done[cc2] = OP("vector", lambda e, sl=slot: e.tensor_tensor(
                    wq1[:].rearrange("p (a b) -> p a b", b=32),
                    wstr[:, sl:sl + 1024].rearrange("p (a b) -> p a b", b=32),
                    sinvb[:].rearrange("p (a b) -> p a b", b=1).broadcast_to([128, 32, 32]),
                    ALU.mult), "VD")
                DR("vector")
                OP("vector", lambda e: e.tensor_scalar(wr2[:], wq1[:], MAGIC, MAGIC,
                                                       ALU.add, ALU.subtract))
                DR("vector")
                wdq_vd = OP("vector", lambda e: e.tensor_tensor(
                    wdqb[:].rearrange("p (a b) -> p a b", b=32),
                    wr2[:].rearrange("p (a b) -> p a b", b=32),
                    sblk[:].rearrange("p (a b) -> p a b", b=1).broadcast_to([128, 32, 32]),
                    ALU.mult), "VD")
                W("tensor", "VD", wdq_vd)
                W("tensor", "SD", bp_state[0])
                for k in range(DC):
                    wtr_td = OP("tensor", lambda e, kk=k: e.transpose(
                        Bp[:, kk * 128:kk * 128 + 128],
                        wdqb[:, kk * 128:kk * 128 + 128], ident[:]),
                        "TD" if k == DC - 1 else None)
                wtr_td_of[cc2] = wtr_td
                W("scalar", "TD", wtr_td)
                if cc2 >= 2:
                    W("scalar", "TD", mm_last_of[cc2 - 2])
                wslot = (cc2 % 2) * 1024
                wtr_sd_of[cc2] = OP("scalar", lambda e, ws=wslot: e.copy(
                    wtr[:, ws:ws + 1024], Bp[:]), "SD")
                bp_state[0] = wtr_sd_of[cc2]

            bp_state = [bp_free_sd]
            wtr_td_of = {}
            ld_of = {}
            if CC2_RUN:
                ld_of[0] = dma("sync", "LWS0", wstr[:, 0:1024], srcw(0))
                emit_dequant(0)
            for cc2 in range(CC2_RUN):
                m, wg = cc2 // 2, cc2 % 2
                wdst = u_d if wg == 0 else gg_d
                mxbase = MX_U if wg == 0 else MX_G
                wslot = (cc2 % 2) * 1024
                for gq in range(NG):
                    bk = gq % 4
                    if gq == 3 and cc2 + 1 < CC2_RUN:
                        W("sync", "VD", wq1_done.get(cc2 - 1, 0))
                        ld_of[cc2 + 1] = dma("sync", "LWS%d" % ((cc2 + 1) % 2),
                                             wstr[:, ((cc2 + 1) % 2) * 1024:
                                                  ((cc2 + 1) % 2) * 1024 + 1024],
                                             srcw(cc2 + 1))
                        emit_dequant(cc2 + 1)
                    W("tensor", "SD", wtr_sd_of[cc2])
                    W("tensor", "VD", bank_vd[bk])
                    W("tensor", "SD", bank_sd[bk])
                    for k in range(DC):
                        mmtd = OP("tensor", lambda e, kk=k, gq=gq, bk=bk, ws=wslot: e.matmul(
                            A[bk][:], wtr[:, ws + kk * 128: ws + kk * 128 + 128],
                            xqfm[:, kk * 8192 + gq * 512: kk * 8192 + (gq + 1) * 512],
                            start=(kk == 0), stop=(kk == DC - 1)),
                            "TD" if k == DC - 1 else None)
                    last_mm_td = mmtd
                    mm_last_of[cc2] = mmtd
                    W("vector", "TD", mmtd)
                    bank_vd[bk] = OP("vector", lambda e, bk=bk, col=mxbase + m * 16 + gq:
                                     e.tensor_reduce(mx[:, col:col + 1], A[bk][:],
                                                     axis=AX.X, op=ALU.max,
                                                     apply_absolute_value=True), "VD")
                    W("scalar", "TD", mmtd)
                    W("scalar", "VD", bank_vd[bk])
                    W("scalar", "SG%d" % bk, stg_st[bk])
                    bank_sd[bk] = OP("scalar", lambda e, bk=bk: e.copy(
                        stg[:, bk * 512:(bk + 1) * 512], A[bk][:]), "SD")
                    W("scalar", "SD", bank_sd[bk])
                    stg_st[bk] = dma("scalar", "SG%d" % bk, wdst[m, gq],
                                     stg[:, bk * 512:(bk + 1) * 512])
            bp_free_sd = bp_state[0]

            W("sync", "VD", CV("VD"))
            W("sync", "TD", CV("TD"))
            wdn_vd = 0
            wdn_td = 0
            WDN_RUN = 16 if max_phase >= 2.9 else (1 if max_phase >= 2.8 else 0)
            for r2 in range(WDN_RUN):
                r, hh = r2 // 2, r2 % 2
                if r2 >= 1:
                    W("sync", "VD", wdn_vd)
                ldw = dma("sync", "LWD", wdnin[:],
                          wdown_d[r * 128:(r + 1) * 128, hh * 1536:(hh + 1) * 1536])
                W("vector", "LWD", ldw)
                if r2 >= 1:
                    W("vector", "TD", wdn_td)
                DR("vector")
                OP("vector", lambda e: e.tensor_reduce(
                    mxb48[:], wdnin[:].rearrange("p (a b) -> p a b", b=32),
                    axis=AX.X, op=ALU.max, apply_absolute_value=True))
                DR("vector")
                OP("vector", lambda e: e.reciprocal(rmx48[:], mxb48[:]))
                DR("vector")
                OP("vector", lambda e: e.tensor_scalar(siv48[:], rmx48[:], 7.0, CLIP,
                                                       ALU.mult, ALU.min))
                DR("vector")
                OP("vector", lambda e: e.reciprocal(sbk48[:], siv48[:]))
                DR("vector")
                OP("vector", lambda e: e.tensor_tensor(
                    wdn1[:].rearrange("p (a b) -> p a b", b=32),
                    wdnin[:].rearrange("p (a b) -> p a b", b=32),
                    siv48[:].rearrange("p (a b) -> p a b", b=1).broadcast_to([128, 48, 32]),
                    ALU.mult))
                DR("vector")
                OP("vector", lambda e: e.tensor_scalar(wdnr[:], wdn1[:], MAGIC, MAGIC,
                                                       ALU.add, ALU.subtract))
                DR("vector")
                wdn_vd = OP("vector", lambda e: e.tensor_tensor(
                    wdnb[:].rearrange("p (a b) -> p a b", b=32),
                    wdnr[:].rearrange("p (a b) -> p a b", b=32),
                    sbk48[:].rearrange("p (a b) -> p a b", b=1).broadcast_to([128, 48, 32]),
                    ALU.mult), "VD")
                W("tensor", "VD", wdn_vd)
                for bb in range(2):
                    nblk = 8 if bb == 0 else 4
                    W("tensor", "SD", bp_free_sd)
                    for k in range(nblk):
                        mblk = bb * 8 + k
                        wdn_td = OP("tensor", lambda e, kk=k, mb=mblk: e.transpose(
                            Bp[:, kk * 128:kk * 128 + 128],
                            wdnb[:, mb * 128:mb * 128 + 128], ident[:]),
                            "TD" if k == nblk - 1 else None)
                    W("scalar", "TD", wdn_td)
                    mg0 = hh * 12 + bb * 8
                    bp_free_sd = OP("scalar", lambda e, mg0=mg0, r=r, nb=nblk: e.copy(
                        wdT[:].rearrange("p (mm x) -> p mm x", x=1024)
                        [:, mg0:mg0 + nb, r * 128:r * 128 + 128],
                        Bp[:, 0:nb * 128].rearrange("p (k x) -> p k x", x=128)), "SD")

        if max_phase >= 3.2:
            DR("vector")
            OP("vector", lambda e: e.tensor_reduce(cw[:, 18:19], mx[:, MX_U:MX_U + 384],
                                                   axis=AX.X, op=ALU.max))
            mug = OP("vector", lambda e: e.tensor_reduce(cw[:, 19:20], mx[:, MX_G:MX_G + 384],
                                                         axis=AX.X, op=ALU.max), "VD")
            DR("vector")
            if max_phase >= 3.4:
                collective(1, cw[:, 18:19], mug, cw[:, 20:21])
            if max_phase >= 3.6:
                collective(2, cw[:, 19:20], CV("VD"), cw[:, 21:22])
            if max_phase >= 3.8:
                DR("vector")
                OP("vector", lambda e: e.reciprocal(cw[:, 22:23], cw[:, 20:21]))
                OP("vector", lambda e: e.reciprocal(cw[:, 23:24], cw[:, 21:22]))
                DR("vector")
                OP("vector", lambda e: e.tensor_scalar(scl[:, SUI:SUI + 1], cw[:, 22:23],
                                                       32767.0, CLIP, ALU.mult, ALU.min))
                OP("vector", lambda e: e.tensor_scalar(scl[:, SGI:SGI + 1], cw[:, 23:24],
                                                       32767.0, CLIP, ALU.mult, ALU.min))
                DR("vector")
                OP("vector", lambda e: e.reciprocal(scl[:, SU:SU + 1], scl[:, SUI:SUI + 1]))
                OP("vector", lambda e: e.reciprocal(scl[:, SG:SG + 1], scl[:, SGI:SGI + 1]))
                DR("vector")
                sc_ug_vd = OP("vector", lambda e: e.tensor_scalar(
                    scl[:, SGQ:SGQ + 1], scl[:, SG:SG + 1], float(2.0 ** -16), None,
                    ALU.mult), "VD")

        if max_phase >= 5:
            barrier()

            p3_st = {}
            p3v1 = {}
            p3v2 = {}
            a3sd3 = {}
            for ci in range(97):
                if ci < 96:
                    m, q4 = ci // 4, ci % 4
                    hf = (ci % 2) * 2048
                    if ci >= 2:
                        W("sync", "VD", p3v1[ci - 2])
                    ldg = dma("sync", "LG%d" % (ci % 2), gin[:, hf:hf + 2048]
                              .rearrange("p (a b) -> p a b", b=512),
                              gg_d[m, q4 * 4:(q4 + 1) * 4].rearrange("a p b -> p a b"))
                    W("vector", "LG%d" % (ci % 2), ldg)
                    W("vector", "VD", sc_ug_vd)
                    if ci >= 2:
                        W("vector", "SD", a3sd3[ci - 2])
                    DR("vector")
                    OP("vector", lambda e, h=hf: e.tensor_scalar(
                        tb1[:, h:h + 2048], gin[:, h:h + 2048],
                        scl[:, SGI:SGI + 1], None, ALU.mult))
                    DR("vector")
                    p3v1[ci] = OP("vector", lambda e, h=hf: e.tensor_scalar(
                        rb3[:, h:h + 2048], tb1[:, h:h + 2048], MAGIC, MAGIC,
                        ALU.add, ALU.subtract), "VD")
                    W("scalar", "VD", p3v1[ci])
                    DR("scalar")
                    OP("scalar", lambda e, h=hf: e.activation(
                        sigb[:, h:h + 2048], rb3[:, h:h + 2048], AF.Sigmoid,
                        scale=scl[:, SG:SG + 1]))
                    DR("scalar")
                    OP("scalar", lambda e, h=hf: e.activation(
                        tb1[:, h:h + 2048], sigb[:, h:h + 2048], AF.Identity,
                        scale=65536.0, bias=cmag[:, 0:1]))
                    DR("scalar")
                    a3sd3[ci] = OP("scalar", lambda e, h=hf: e.activation(
                        a3b[:, h:h + 2048], tb1[:, h:h + 2048], AF.Identity,
                        bias=cmag[:, 1:2]), "SD")
                if ci >= 1:
                    j = ci - 1
                    mj, q4j = j // 4, j % 4
                    hj = (j % 2) * 2048
                    W("vector", "SD", a3sd3[j])
                    OP("vector", lambda e, h=hj: e.tensor_scalar(
                        c2b[:, h:h + 2048], a3b[:, h:h + 2048], 65535.0,
                        scl[:, SGQ:SGQ + 1], ALU.min, ALU.mult))
                    DR("vector")
                    W("vector", "STB%d" % (j % 2), p3_st.get(j - 2, 0))
                    OP("vector", lambda e, h=hj: e.tensor_tensor(
                        tob[:, h:h + 2048], rb3[:, h:h + 2048],
                        c2b[:, h:h + 2048], ALU.mult))
                    DR("vector")
                    p3v2[j] = OP("vector", lambda e, j=j, h=hj: e.tensor_reduce(
                        mx[:, MX_T + j:MX_T + j + 1], tob[:, h:h + 2048],
                        axis=AX.X, op=ALU.max, apply_absolute_value=True), "VD")
                    W("scalar", "VD", p3v2[j])
                    p3_st[j] = dma("scalar", "STB%d" % (j % 2),
                                   t_d[mj, q4j * 4:(q4j + 1) * 4].rearrange("a p b -> p a b"),
                                   tob[:, hj:hj + 2048].rearrange("p (a b) -> p a b", b=512))

            DR("vector")
            mt = OP("vector", lambda e: e.tensor_reduce(cw[:, 18:19], mx[:, MX_T:MX_T + 96],
                                                        axis=AX.X, op=ALU.max), "VD")
            DR("vector")
            collective(3, cw[:, 18:19], mt, cw[:, 19:20])
            DR("vector")
            OP("vector", lambda e: e.reciprocal(cw[:, 20:21], cw[:, 19:20]))
            DR("vector")
            OP("vector", lambda e: e.tensor_scalar(scl[:, STI:STI + 1], cw[:, 20:21],
                                                   32767.0, CLIP, ALU.mult, ALU.min))
            DR("vector")
            OP("vector", lambda e: e.reciprocal(cw[:, 21:22], scl[:, STI:STI + 1]))
            DR("vector")
            sc_t_vd = OP("vector", lambda e: e.tensor_tensor(
                scl[:, STU:STU + 1], cw[:, 21:22], scl[:, SU:SU + 1], ALU.mult), "VD")

        if max_phase >= 6:
            barrier()

            p4_st = {}
            p4v1 = {}
            p4v2 = {}
            rusd4 = {}
            for ci in range(97):
                if ci < 96:
                    m, q4 = ci // 4, ci % 4
                    hf = (ci % 2) * 2048
                    if ci >= 2:
                        W("sync", "VD", p4v1[ci - 2])
                        W("sync", "SD", rusd4[ci - 2])
                    ldt = dma("sync", "LG%d" % (ci % 2), gin[:, hf:hf + 2048]
                              .rearrange("p (a b) -> p a b", b=512),
                              t_d[m, q4 * 4:(q4 + 1) * 4].rearrange("a p b -> p a b"))
                    ldu = dma("sync", "LU%d" % (ci % 2), uin[:, hf:hf + 2048]
                              .rearrange("p (a b) -> p a b", b=512),
                              u_d[m, q4 * 4:(q4 + 1) * 4].rearrange("a p b -> p a b"))
                    W("vector", "LG%d" % (ci % 2), ldt)
                    W("vector", "VD", sc_t_vd)
                    if ci >= 2:
                        W("vector", "SD", rusd4[ci - 2])
                    DR("vector")
                    OP("vector", lambda e, h=hf: e.tensor_scalar(
                        tb1[:, h:h + 2048], gin[:, h:h + 2048],
                        scl[:, STI:STI + 1], None, ALU.mult))
                    DR("vector")
                    OP("vector", lambda e, h=hf: e.tensor_scalar(
                        rb3[:, h:h + 2048], tb1[:, h:h + 2048], MAGIC, MAGIC,
                        ALU.add, ALU.subtract))
                    DR("vector")
                    p4v1[ci] = OP("vector", lambda e, h=hf: e.tensor_scalar(
                        c2b[:, h:h + 2048], rb3[:, h:h + 2048],
                        scl[:, STU:STU + 1], None, ALU.mult), "VD")
                    W("scalar", "LU%d" % (ci % 2), ldu)
                    W("scalar", "VD", p4v1[ci])
                    DR("scalar")
                    OP("scalar", lambda e, h=hf: e.activation(
                        tb1[:, h:h + 2048], uin[:, h:h + 2048], AF.Identity,
                        scale=scl[:, SUI:SUI + 1], bias=cmag[:, 0:1]))
                    DR("scalar")
                    rusd4[ci] = OP("scalar", lambda e, h=hf: e.activation(
                        a3b[:, h:h + 2048], tb1[:, h:h + 2048], AF.Identity,
                        bias=cmag[:, 1:2]), "SD")
                if ci >= 1:
                    j = ci - 1
                    mj, q4j = j // 4, j % 4
                    hj = (j % 2) * 2048
                    W("vector", "SD", rusd4[j])
                    W("vector", "STB%d" % (j % 2), p4_st.get(j - 2, 0))
                    OP("vector", lambda e, h=hj: e.tensor_tensor(
                        tob[:, h:h + 2048], c2b[:, h:h + 2048],
                        a3b[:, h:h + 2048], ALU.mult))
                    DR("vector")
                    p4v2[j] = OP("vector", lambda e, j=j, h=hj: e.tensor_reduce(
                        mx[:, MX_O + j:MX_O + j + 1], tob[:, h:h + 2048],
                        axis=AX.X, op=ALU.max, apply_absolute_value=True), "VD")
                    W("scalar", "VD", p4v2[j])
                    p4_st[j] = dma("scalar", "STB%d" % (j % 2),
                                   o_d[mj, q4j * 4:(q4j + 1) * 4].rearrange("a p b -> p a b"),
                                   tob[:, hj:hj + 2048].rearrange("p (a b) -> p a b", b=512))

            DR("vector")
            mo = OP("vector", lambda e: e.tensor_reduce(cw[:, 18:19], mx[:, MX_O:MX_O + 96],
                                                        axis=AX.X, op=ALU.max), "VD")
            DR("vector")
            collective(4, cw[:, 18:19], mo, cw[:, 19:20])
            DR("vector")
            OP("vector", lambda e: e.reciprocal(cw[:, 20:21], cw[:, 19:20]))
            DR("vector")
            OP("vector", lambda e: e.tensor_scalar(scl[:, SOI:SOI + 1], cw[:, 20:21],
                                                   32767.0, CLIP, ALU.mult, ALU.min))
            DR("vector")
            sc_o_vd = OP("vector", lambda e: e.reciprocal(scl[:, SO:SO + 1],
                                                          scl[:, SOI:SOI + 1]), "VD")

        if max_phase >= 7:
            barrier()

            dn_td = {}
            out_st = {}
            ln_vd = 0
            ln_vd_par = {}
            x2g_vd_par = {}
            x2g_vd = 0
            oq_vd_hist = {}
            oq_sd_hist = {}
            oq_last_vd = {}
            ld_oq = {}

            def emit_oq_prep(g, mlist):
                half = (g % 2) * 12288
                for m in mlist:
                    slot = (m % 8) * 512
                    if (g, m - 8) in oq_sd_hist:
                        W("sync", "SD", oq_sd_hist[(g, m - 8)])
                    elif (g - 1, 16 + m) in oq_sd_hist:
                        W("sync", "SD", oq_sd_hist[(g - 1, 16 + m)])
                    ldo = dma("sync", "LO%d" % (m % 8), gin[:, slot:slot + 512],
                              o_d[m, g])
                    W("scalar", "LO%d" % (m % 8), ldo)
                    W("scalar", "VD", sc_o_vd)
                    DR("scalar")
                    OP("scalar", lambda e, sl=slot: e.activation(
                        ot1[:], gin[:, sl:sl + 512], AF.Identity,
                        scale=scl[:, SOI:SOI + 1], bias=cmag[:, 0:1]))
                    DR("scalar")
                    if (g, m - 2) in oq_vd_hist:
                        W("scalar", "VD", oq_vd_hist[(g, m - 2)])
                    elif (g - 1, 22 + m) in oq_vd_hist:
                        W("scalar", "VD", oq_vd_hist[(g - 1, 22 + m)])
                    o2s = (m % 2) * 512
                    osd = OP("scalar", lambda e, o2s=o2s: e.activation(
                        ot2[:, o2s:o2s + 512], ot1[:], AF.Identity, bias=cmag[:, 1:2]), "SD")
                    W("vector", "SD", osd)
                    if g >= 2 and m == 0:
                        W("vector", "TD", dn_td[(g - 2, NT - 1)])
                    DR("vector")
                    oq_sd_hist[(g, m)] = osd
                    oq_vd_hist[(g, m)] = OP("vector", lambda e, m=m, hh=half, o2s=o2s: e.tensor_scalar(
                        oqb[:, hh + m * 512: hh + (m + 1) * 512], ot2[:, o2s:o2s + 512],
                        scl[:, SO:SO + 1], None, ALU.mult), "VD")
                oq_last_vd[g] = oq_vd_hist[(g, mlist[-1])] if mlist else oq_last_vd.get(g, 0)

            emit_oq_prep(0, list(range(FC)))
            for gq in range(NG):
                half = (gq % 2) * 12288
                xgh = (gq % 2) * 4096
                if gq >= 2:
                    W("sync", "VD", x2g_vd_par[gq - 2])
                ldx = dma("sync", "LXG",
                          x2g[:, xgh:xgh + 4096].rearrange("p (t d) -> p t d", d=1024),
                          x2_d[gq * 4:(gq + 1) * 4].rearrange("t p d -> p t d"))
                lds = dma("sync", "LXG", sg4b[:, (gq % 2) * 8:(gq % 2) * 8 + 8], sig_d[gq])
                for tt in range(NT):
                    bw = (tt % 2) * 2
                    W("tensor", "VD", oq_last_vd[gq])
                    kprev = (gq, tt - 2) if tt >= 2 else (gq - 1, tt + 2)
                    if kprev in ln_vd_par:
                        W("tensor", "VD", ln_vd_par[kprev])
                    for m in range(FC):
                        for jj in range(2):
                            dn_td[(gq, tt)] = OP("tensor", lambda e, m=m, tt=tt, j=jj, hh=half, bw=bw:
                                                 e.matmul(
                                A[bw + j][:], oqb[:, hh + m * 512 + tt * 128:
                                             hh + m * 512 + tt * 128 + 128],
                                wdT[:, m * 1024 + j * 512: m * 1024 + (j + 1) * 512],
                                start=(m == 0), stop=(m == FC - 1)),
                                "TD" if (m == FC - 1 and jj == 1) else None)
                    W("vector", "TD", dn_td[(gq, tt)])
                    W("vector", "LXG", lds)
                    DR("vector")
                    DR("scalar")
                    OP("vector", lambda e, bw=bw: e.bn_stats(st6[:, 0:6], A[bw][:]))
                    OP("vector", lambda e, bw=bw: e.bn_stats(st6[:, 6:12], A[bw + 1][:]))
                    DR("vector")
                    OP("vector", lambda e: e.bn_aggr(mvt[:], st6[:]))
                    DR("vector")
                    OP("vector", lambda e: e.tensor_scalar(cw[:, 10:11], mvt[:, 1:2],
                                                           EPS_LN, None, ALU.add))
                    DR("vector")
                    vv = OP("vector", lambda e: e.reciprocal(cw[:, 11:12], cw[:, 10:11]), "VD")
                    W("scalar", "VD", vv)
                    sv = OP("scalar", lambda e: e.activation(cw[:, 12:13], cw[:, 11:12],
                                                             AF.Sqrt), "SD")
                    W("vector", "SD", sv)
                    for half2 in range(2):
                        OP("vector", lambda e, hh=half2, bw=bw: e.tensor_scalar(
                            lnb[:, hh * 512:(hh + 1) * 512], A[bw + hh][:],
                            mvt[:, 0:1], cw[:, 12:13], ALU.subtract, ALU.mult))
                    OP("vector", lambda e, tt=tt, xgh=xgh, sb=(gq % 2) * 8: e.tensor_scalar(
                        x1b[:], x2g[:, xgh + tt * 1024: xgh + (tt + 1) * 1024],
                        sg4b[:, sb + tt * 2:sb + tt * 2 + 1],
                        sg4b[:, sb + tt * 2 + 1:sb + tt * 2 + 2],
                        ALU.mult, ALU.add))
                    DR("vector")
                    oslot = (tt % 2) * 1024
                    prev_o = (gq, tt - 2) if tt >= 2 else (gq - 1, tt + 2)
                    W("vector", "SY%d" % (tt % 2), out_st.get(prev_o, 0))
                    x2g_vd = ln_vd = OP("vector", lambda e, osl=oslot: e.tensor_tensor(
                        outb[:, osl:osl + 1024], lnb[:], x1b[:], ALU.add), "VD")
                    ln_vd_par[(gq, tt)] = ln_vd
                    x2g_vd_par[gq] = ln_vd
                    W("scalar", "VD", ln_vd)
                    tok0 = gq * 512 + tt * 128
                    out_st[(gq, tt)] = dma("scalar", "SY%d" % (tt % 2),
                                           y_d[tok0 // 2048,
                                               tok0 % 2048: tok0 % 2048 + 128, :],
                                           outb[:, oslot:oslot + 1024])
                    if gq + 1 < NG and tt < 2:
                        emit_oq_prep(gq + 1, list(range(tt * 12, tt * 12 + 12)))

        for k, v in dict(cnt).items():
            W("sync", k, v)

        waited = {}
        for e in ENGINES:
            for item in prog[e]:
                if item[0] == "w":
                    waited.setdefault(item[1], set()).add(item[2])
        dma_sems = {s for s in cnt if s not in ("TD", "VD", "SD", "PD", "CC")}
        run_cnt = {}
        for e in ENGINES:
            newops = []
            pend = {}
            for item in prog[e]:
                if item[0] == "o":
                    _, fn, inc, n = item
                    if inc is not None and inc not in dma_sems:
                        v = run_cnt.get(inc, 0) + n
                        run_cnt[inc] = v
                        if v in waited.get(inc, ()):
                            newops.append(("o", fn, inc, n + pend.pop(inc, 0)))
                        else:
                            pend[inc] = pend.get(inc, 0) + n
                            newops.append(("o", fn, None, 0))
                        continue
                newops.append(item)
            assert not pend or all(v == 0 for v in pend.values()) or True
            prog[e] = newops

        def runner(eng_name):
            def run(eng):
                lastw = {}
                for item in prog[eng_name]:
                    if item[0] == "w":
                        if lastw.get(item[1], -1) >= item[2]:
                            continue
                        lastw[item[1]] = item[2]
                        eng.wait_ge(sems[item[1]], item[2])
                    elif item[0] == "d":
                        eng.drain()
                    else:
                        _, fn, inc, n = item
                        instr = fn(eng)
                        if inc is not None:
                            instr.then_inc(sems[inc], n)
            return run

        blk.sync(runner("sync"))
        blk.tensor(runner("tensor"))
        blk.vector(runner("vector"))
        blk.scalar(runner("scalar"))
        blk.gpsimd(runner("gpsimd"))

    return nc


def make_in_maps(inputs, n_cores=8):
    consts = host_constants()
    perm = qk_row_perm()
    wqkv = np.ascontiguousarray(inputs["L0_attn_w_qkv"][perm])
    g8 = np.ascontiguousarray(inputs["L0_attn_g"].reshape(DC, 128).T)
    pm = inputs["L0_attn_pm"]
    pmkf = np.empty((128, 256), dtype=np.float32)
    for h in range(H):
        blkv = pm[0, h].T
        pmkf[0:64, h * 16:(h + 1) * 16] = blkv
        pmkf[64:128, h * 16:(h + 1) * 16] = blkv
    per_head = np.concatenate([np.arange(0, DH, 2), np.arange(1, DH, 2)])
    pmkf_p = np.empty_like(pmkf)
    pmkf_p[0:64] = pmkf[0:64][per_head]
    pmkf_p[64:128] = pmkf[64:128][per_head]
    pmvst = np.zeros((16, 1040), dtype=np.float32)
    for h in range(H):
        pmvst[:, h * 65: h * 65 + 64] = pm[1, h]
        pmvst[:, h * 65 + 64] = 1.0
    seq = inputs["L0_attn_seq"]
    maps = []
    for c in range(n_cores):
        maps.append({
            "seq": np.ascontiguousarray(seq[c * NB:(c + 1) * NB]),
            "wqkv": wqkv,
            "wout": np.ascontiguousarray(inputs["L0_attn_w_out"]),
            "wup": np.ascontiguousarray(inputs["L0_ffn_w_up"]),
            "wgate": np.ascontiguousarray(inputs["L0_ffn_w_gate"]),
            "wdown": np.ascontiguousarray(inputs["L0_ffn_w_down"]),
            "gvec": g8,
            "pmkf": pmkf_p,
            "pmvst": pmvst,
            "identc": consts["ident"],
            "identf": np.eye(128, dtype=np.float32),
            "maskmul": consts["maskmul"],
            "costab": consts["costab"],
            "sintab": consts["sintab"],
            "indsel": consts["indsel"],
        })
    return maps


def kernel(**inputs):
    n_cores = 8
    maps = make_in_maps(inputs, n_cores)
    nc = build_nc(n_cores)
    res = run_bass_kernel_spmd(nc, maps, core_ids=list(range(n_cores)))
    out = np.concatenate([res.results[c]["y"] for c in range(n_cores)], axis=0)
    return out.astype(np.float32)


# version 15
# speedup vs baseline: 1.0954x; 1.0065x over previous
import contextlib
import numpy as np
import ml_dtypes
import concourse.bass as bass
import concourse.mybir as mybir
import bass_rust
from concourse import library_config
from concourse.bass_utils import run_bass_kernel_spmd

F32 = mybir.dt.float32
BF16 = mybir.dt.bfloat16
AF = mybir.ActivationFunctionType
ALU = mybir.AluOpType
AX = mybir.AxisListType
RED = bass_rust.ReduceOp

NB, NSEG, NT = 4, 4, 4
D, DC, H, DH = 1024, 8, 16, 64
FF, FC = 3072, 24
NTT, NG = 64, 16
MAGIC = float(np.float32(1.5 * 2 ** 23))
EPS_LN = 1e-5
EPS_RMS = 1.1920929e-07
CLIP = 1e12

ENGINES = ("sync", "tensor", "vector", "scalar", "gpsimd")


def host_constants():
    c = {}
    c["ident"] = np.eye(128, dtype=ml_dtypes.bfloat16)
    j = np.arange(128)[:, None]
    i = np.arange(128)[None, :]
    c["maskmul"] = (j <= i).astype(ml_dtypes.bfloat16)
    pos = np.arange(2048, dtype=np.float64)
    inv = 1.0 / 10000.0 ** (np.arange(0, DH, 2, dtype=np.float64) / DH)
    ang = pos[:, None] * inv[None, :]
    c["costab"] = np.cos(ang).astype(ml_dtypes.bfloat16).reshape(16, 128, 32) \
        .transpose(1, 0, 2).reshape(128, 512).copy()
    c["sintab"] = np.sin(ang).astype(ml_dtypes.bfloat16).reshape(16, 128, 32) \
        .transpose(1, 0, 2).reshape(128, 512).copy()
    ind = np.zeros((16, 1024), dtype=ml_dtypes.bfloat16)
    for h in range(16):
        ind[h, (h // 2) * 128 + (h % 2) * 64: (h // 2) * 128 + (h % 2) * 64 + 64] = 1.0
    c["indsel"] = ind
    return c


def qk_row_perm():
    per_head = np.concatenate([np.arange(0, DH, 2), np.arange(1, DH, 2)])
    one = np.concatenate([h * DH + per_head for h in range(H)])
    return np.concatenate([one, 1024 + one, 2048 + np.arange(1024)])


def build_nc(n_cores=8, max_phase=7, stub_cc=False, p2opts='dmae'):
    nc = bass.Bass("TRN2", target_bir_lowering=False)

    seq_d = nc.dram_tensor("seq", [NB, 2048, D], F32, kind="ExternalInput").ap()
    wqkv_d = nc.dram_tensor("wqkv", [3 * D, D], F32, kind="ExternalInput").ap()
    wout_d = nc.dram_tensor("wout", [D, D], F32, kind="ExternalInput").ap()
    wup_d = nc.dram_tensor("wup", [FF, D], F32, kind="ExternalInput").ap()
    wgate_d = nc.dram_tensor("wgate", [FF, D], F32, kind="ExternalInput").ap()
    wdown_d = nc.dram_tensor("wdown", [D, FF], F32, kind="ExternalInput").ap()
    g_d = nc.dram_tensor("gvec", [128, DC], F32, kind="ExternalInput").ap()
    pmkf_d = nc.dram_tensor("pmkf", [128, 256], F32, kind="ExternalInput").ap()
    pmvst_d = nc.dram_tensor("pmvst", [16, 1040], F32, kind="ExternalInput").ap()
    ident_d = nc.dram_tensor("identc", [128, 128], BF16, kind="ExternalInput").ap()
    identf_d = nc.dram_tensor("identf", [128, 128], F32, kind="ExternalInput").ap()
    mask_d = nc.dram_tensor("maskmul", [128, 128], BF16, kind="ExternalInput").ap()
    cos_d = nc.dram_tensor("costab", [128, 512], BF16, kind="ExternalInput").ap()
    sin_d = nc.dram_tensor("sintab", [128, 512], BF16, kind="ExternalInput").ap()
    ind_d = nc.dram_tensor("indsel", [16, 1024], BF16, kind="ExternalInput").ap()
    y_d = nc.dram_tensor("y", [NB, 2048, D], F32, kind="ExternalOutput").ap()

    x2_d = nc.dram_tensor("X2s", [NTT, 128, D], F32).ap()
    sig_d = nc.dram_tensor("SIGs", [NG, 128, 8], F32).ap()
    u_d = nc.dram_tensor("Usc", [FC, NG, 128, 512], BF16).ap()
    gg_d = nc.dram_tensor("Gsc", [FC, NG, 128, 512], BF16).ap()
    t_d = nc.dram_tensor("Tsc", [FC, NG, 128, 512], BF16).ap()
    o_d = nc.dram_tensor("Osc", [FC, NG, 128, 512], BF16).ap()
    cci = [nc.dram_tensor(f"cci{k}", [1, 8], F32).ap() for k in range(5)]
    cco = [nc.dram_tensor(f"cco{k}", [1, 8], F32).ap() for k in range(5)]
    rs_d = nc.dram_tensor("RSb", [16, 512], F32).ap()

    prog = {e: [] for e in ENGINES}
    cnt = {}

    def W(eng, semname, val):
        if val and val > 0:
            prog[eng].append(("w", semname, val))

    def OP(eng, fn, inc=None, n=1):
        tot = None
        if inc is not None:
            cnt[inc] = cnt.get(inc, 0) + n
            tot = cnt[inc]
        prog[eng].append(("o", fn, inc, n))
        return tot

    def DR(eng):
        prog[eng].append(("d",))

    def CV(semname):
        return cnt.get(semname, 0)

    ctx = contextlib.ExitStack()
    sbt = lambda name, shape, dt: ctx.enter_context(nc.sbuf_tensor(name, shape, dt))
    pst = lambda name, shape, dt: ctx.enter_context(nc.psum_tensor(name, shape, dt))

    with ctx:
        ident = sbt("identS", [128, 128], BF16)
        identf = sbt("identfS", [128, 128], F32)
        onesrow = sbt("onesrowS", [1, 128], F32)
        mcol8 = sbt("mcol8S", [1, 8], F32)
        mg8 = sbt("mg8S", [1, 8], F32)
        mask = sbt("maskS", [128, 128], BF16)
        indsel = sbt("indselS", [16, 1024], BF16)
        gsb = sbt("gsbS", [128, DC], F32)
        costab = sbt("costabS", [128, 512], BF16)
        sintab = sbt("sintabS", [128, 512], BF16)
        mx = sbt("mxS", [128, 1024], F32)
        MX_X, MX_U, MX_G, MX_T, MX_O = 0, 64, 448, 832, 928
        scl = sbt("sclS", [128, 16], F32)
        SXI, SX, SUI, SU, SGI, SG, SGQ, STI, STU, SOI, SO = range(11)
        cw = sbt("cwS", [128, 64], F32)
        st6 = sbt("st6S", [128, 12], F32)
        mvt = sbt("mvtS", [128, 2], F32)
        mvt4 = sbt("mvt4S", [128, 8], F32)
        sgst = sbt("sgstS", [128, 8], F32)
        cmag = sbt("cmagS", [128, 2], F32)

        A = [pst(f"psA{i}", [128, 512], F32) for i in range(6)]
        Bp = pst("psB", [128, 1024], BF16)
        Cp = pst("psC", [128, 512], F32)

        sems = {}
        for s in ("LC", "LW0", "LW1", "LW2", "LSEQ", "LX0", "LX1", "LWS0", "LWS1",
                  "LWD", "LG0", "LG1", "LU0", "LU1", "LXG", "LRS", "LRS0", "LRS1",
                  "LO0", "LO1", "LO2", "LO3", "LO4", "LO5", "LO6", "LO7", "GR0", "GR1",
                  "GD", "TD", "VD", "SD", "PD", "CC",
                  "SX0", "SX1", "SSG", "SG0", "SG1", "SG2", "SG3",
                  "STB0", "STB1", "SY0", "SY1"):
            sems[s] = ctx.enter_context(nc.semaphore(s))

        p1 = contextlib.ExitStack()
        s1 = lambda name, shape, dt: p1.enter_context(nc.sbuf_tensor(name, shape, dt))
        wqkvT = s1("wqkvT", [128, 24576], BF16)
        woutT = s1("woutT", [128, 8192], BF16)
        pmkf = s1("pmkfS", [128, 256], BF16)
        pmvst = s1("pmvstS", [16, 1040], BF16)
        pstage = contextlib.ExitStack()
        wldf = pstage.enter_context(nc.sbuf_tensor("wldf", [128, 3072], F32))
        wcst = pstage.enter_context(nc.sbuf_tensor("wcst", [128, 2048], BF16))
        pstage.close()
        seqsb = s1("seqsb", [128, 8192], F32)
        zbf = s1("zbf", [128, 4096], BF16)
        zfm = s1("zfm", [128, 4096], BF16)
        qksb = s1("qksb", [128, 8192], BF16)
        qfm = s1("qfm", [128, 4096], BF16)
        kfm = s1("kfm", [128, 4096], BF16)
        vst = s1("vst", [128, 4160], BF16)
        expt = s1("expt", [128, 3584], BF16)
        aoun = s1("aoun", [128, 4096], BF16)
        rsum = s1("rsum", [16, 512], BF16)
        rsum32 = s1("rsum32", [16, 512], F32)
        sumsb = s1("sumsb", [1, 1024], F32)
        rxp = s1("rxp", [128, 4096], BF16)
        x1t = s1("x1t", [128, 1024], F32)
        x2t = s1("x2t", [128, 1024], F32)
        rta = s1("rta", [128, 256], BF16)
        qke = s1("qke", [128, 2048], BF16)
        rtb = s1("rtb", [128, 256], BF16)
        p1.close()

        p2 = contextlib.ExitStack()
        s2 = lambda name, shape, dt: p2.enter_context(nc.sbuf_tensor(name, shape, dt))
        xqfm = s2("xqfm", [128, 65536], BF16)
        wstr = s2("wstr", [128, 2048], F32)
        mxb32 = s2("mxb32", [128, 32], F32)
        rmxb = s2("rmxb", [128, 32], F32)
        sinvb = s2("sinvb", [128, 32], F32)
        sblk = s2("sblkS", [128, 32], F32)
        wq1 = s2("wq1", [128, 1024], F32)
        wr2 = s2("wr2", [128, 1024], F32)
        wdqb = s2("wdqb", [128, 1024], BF16)
        wtr = s2("wtrS", [128, 2048], BF16)
        stg = s2("stgS", [128, 2048], BF16)
        p2a = contextlib.ExitStack()
        xin = p2a.enter_context(nc.sbuf_tensor("xinS", [128, 2048], F32))
        xt1 = p2a.enter_context(nc.sbuf_tensor("xt1S", [128, 1024], F32))
        xr2 = p2a.enter_context(nc.sbuf_tensor("xr2S", [128, 1024], F32))
        xqtm = p2a.enter_context(nc.sbuf_tensor("xqtm", [128, 1024], BF16))
        p2a.close()
        p2b = contextlib.ExitStack()
        wdnin = p2b.enter_context(nc.sbuf_tensor("wdnin", [128, 1536], F32))
        wdn1 = p2b.enter_context(nc.sbuf_tensor("wdn1", [128, 1536], F32))
        wdnr = p2b.enter_context(nc.sbuf_tensor("wdnr", [128, 1536], F32))
        wdnb = p2b.enter_context(nc.sbuf_tensor("wdnb", [128, 1536], BF16))
        mxb48 = p2b.enter_context(nc.sbuf_tensor("mxb48", [128, 48], F32))
        rmx48 = p2b.enter_context(nc.sbuf_tensor("rmx48", [128, 48], F32))
        siv48 = p2b.enter_context(nc.sbuf_tensor("siv48", [128, 48], F32))
        sbk48 = p2b.enter_context(nc.sbuf_tensor("sbk48", [128, 48], F32))
        p2b.close()
        p2.close()

        p2c = contextlib.ExitStack()
        s2c = lambda name, shape, dt: p2c.enter_context(nc.sbuf_tensor(name, shape, dt))
        wdT = s2c("wdT", [128, 24576], BF16)
        gin = s2c("ginS", [128, 4096], BF16)
        uin = s2c("uinS", [128, 4096], BF16)
        p34 = contextlib.ExitStack()
        s34 = lambda name, shape, dt: p34.enter_context(nc.sbuf_tensor(name, shape, dt))
        tb1 = s34("tb1S", [128, 4096], F32)
        rb3 = s34("rb3S", [128, 4096], F32)
        sigb = s34("sigbS", [128, 4096], F32)
        a3b = s34("a3bS", [128, 4096], F32)
        c2b = s34("c2bS", [128, 4096], F32)
        tob = s34("tobS", [128, 4096], BF16)
        p34.close()
        p4b = contextlib.ExitStack()
        s4b = lambda name, shape, dt: p4b.enter_context(nc.sbuf_tensor(name, shape, dt))
        oqb = s4b("oqbS", [128, 24576], BF16)
        x2g = s4b("x2gS", [128, 8192], F32)
        lnb = s4b("lnbS", [128, 1024], F32)
        x1b = s4b("x1bS", [128, 1024], F32)
        outb = s4b("outbS", [128, 2048], F32)
        ot1 = s4b("ot1S", [128, 512], F32)
        ot2 = s4b("ot2S", [128, 1024], F32)
        sg4b = s4b("sg4bS", [128, 16], F32)
        p4b.close()
        p2c.close()

        blk = ctx.enter_context(nc.Block())

        def dma(eng, sem, dst, src):
            return OP(eng, lambda e, d=dst, s=src: e.dma_start(out=d, in_=s), sem, 16)

        ld_const = 0
        for dst, srcap in ((ident[:], ident_d), (identf[:], identf_d),
                           (mask[:], mask_d), (indsel[:], ind_d),
                           (gsb[:], g_d),
                           (costab[:], cos_d), (sintab[:], sin_d)):
            ld_const = dma("sync", "LC", dst, srcap)
        W("tensor", "LC", ld_const)
        W("vector", "LC", ld_const)
        W("scalar", "LC", ld_const)
        gld_pm = dma("gpsimd", "GD", pmkf[:], pmkf_d)
        gld_pm = dma("gpsimd", "GD", pmvst[:], pmvst_d)
        OP("vector", lambda e: e.memset(onesrow[:], 1.0))
        OP("vector", lambda e: e.memset(mcol8[:], 0.0))
        OP("vector", lambda e: e.memset(cmag[:, 0:1], MAGIC))
        OP("vector", lambda e: e.memset(cmag[:, 1:2], -MAGIC))
        vst_ones = OP("vector", lambda e: e.memset(
            vst[:].rearrange("p (t h c) -> p t h c", h=16, c=65)[:, :, :, 64:65], 1.0), "VD")

        vcast_done = {}
        ttr_done = {}
        evac_done = {}
        prep_jobs = [("qkv", m) for m in range(24)] + [("out", m) for m in range(8)]
        for gi, (kind, m) in enumerate(prep_jobs):
            lslot = (gi % 3) * 1024
            cslot = (gi % 2) * 1024
            if gi >= 3:
                W("sync", "VD", vcast_done[gi - 3])
            src_rows = (wqkv_d if kind == "qkv" else wout_d)[m * 128:(m + 1) * 128, :]
            lt = dma("sync", "LW%d" % (gi % 3), wldf[:, lslot:lslot + 1024], src_rows)
            W("vector", "LW%d" % (gi % 3), lt)
            if gi >= 2:
                W("vector", "TD", ttr_done[gi - 2])
            vcast_done[gi] = OP("vector", lambda e, ls=lslot, cs=cslot: e.tensor_copy(
                wcst[:, cs:cs + 1024], wldf[:, ls:ls + 1024]), "VD")
            W("tensor", "VD", vcast_done[gi])
            if gi >= 1:
                W("tensor", "SD", evac_done[gi - 1])
            for k in range(DC):
                ttr_done[gi] = OP("tensor", lambda e, cs=cslot, kk=k: e.transpose(
                    Bp[:, kk * 128:kk * 128 + 128],
                    wcst[:, cs + kk * 128:cs + kk * 128 + 128], ident[:]),
                    "TD" if k == DC - 1 else None)
            W("scalar", "TD", ttr_done[gi])
            if kind == "qkv":
                for k in range(DC):
                    evac_done[gi] = OP("scalar", lambda e, mm=m, kk=k: e.activation(
                        wqkvT[:, kk * 3072 + mm * 128: kk * 3072 + mm * 128 + 128],
                        Bp[:, kk * 128:kk * 128 + 128], AF.Identity,
                        scale=gsb[:, kk:kk + 1]), "SD" if k == DC - 1 else None)
            else:
                evac_done[gi] = OP("scalar", lambda e, mm=m: e.copy(
                    woutT[:].rearrange("p (k x) -> p k x", x=1024)
                    [:, :, mm * 128:mm * 128 + 128],
                    Bp[:].rearrange("p (k x) -> p k x", x=128)), "SD")
        bp_free_sd = evac_done[len(prep_jobs) - 1]


        if max_phase >= 1:
            rope_done = {}
            rope_bank = {}
            qke_sd = {}
            vevac_done = {}
            zfm_done = {}
            qkv_done = {}
            exp_done = {}
            mask_done = {}
            av_done = {}
            x2_store_st = {}
            sg_store_st = 0
            woutln_vd = 0
            norm_vd = 0
            rec_done = 0
            ao_done = 0
            prev_norm_vd = 0
            last_rmm_td = 0
            rs_row_st = {}
            rsum32_vd = 0

            z_done_map = {}
            ld_seq_of = {}
            seq_par_read = {0: 0, 1: 0}

            def emit_AB(un):
                    bb, ss = un // 4, un % 4
                    sh = (un % 2) * 4096
                    if un == 0:
                        W("sync", "VD", CV("VD"))
                        W("sync", "TD", CV("TD"))
                    W("sync", "VD", seq_par_read[un % 2])
                    ld_seq = dma("sync", "LSEQ",
                                 seqsb[:, sh:sh + 4096].rearrange("p (t d) -> p t d", d=1024),
                                 seq_d[bb, ss * 512:(ss + 1) * 512, :]
                                 .rearrange("(t p) d -> p t d", p=128))
                    ld_seq_of[un] = ld_seq
                    if un:
                        W("vector", "SD", z_done_map[(un - 1, NT - 1)])
                        DR("vector")
                        DR("scalar")
                    bv1 = None
                    for t in range(NT):
                        c0 = 32 + t * 8
                        sq = seqsb[:, sh + t * 1024: sh + (t + 1) * 1024]
                        W("vector", "LSEQ", ld_seq)
                        if t:
                            DR("vector")
                        OP("vector", lambda e, sq=sq: e.bn_stats(st6[:, 0:6], sq[:, 0:512]))
                        OP("vector", lambda e, sq=sq: e.bn_stats(st6[:, 6:12], sq[:, 512:1024]))
                        DR("vector")
                        OP("vector", lambda e, t=t: e.bn_aggr(mvt4[:, 2 * t:2 * t + 2], st6[:]))
                        DR("vector")
                        OP("vector", lambda e, t=t, c0=c0: e.tensor_scalar(
                            cw[:, c0:c0 + 1], mvt4[:, 2 * t + 1:2 * t + 2],
                            EPS_LN, None, ALU.add))
                        DR("vector")
                        OP("vector", lambda e, c0=c0: e.reciprocal(cw[:, c0 + 1:c0 + 2],
                                                                   cw[:, c0:c0 + 1]))
                        DR("vector")
                        OP("vector", lambda e, t=t, c0=c0: e.tensor_tensor(
                            cw[:, c0 + 4:c0 + 5], mvt4[:, 2 * t + 1:2 * t + 2],
                            cw[:, c0 + 1:c0 + 2], ALU.mult))
                        DR("vector")
                        OP("vector", lambda e, c0=c0: e.tensor_scalar(
                            cw[:, c0 + 5:c0 + 6], cw[:, c0 + 4:c0 + 5],
                            EPS_RMS, None, ALU.add))
                        DR("vector")
                        bv1 = OP("vector", lambda e, c0=c0: e.reciprocal(
                            cw[:, c0 + 6:c0 + 7], cw[:, c0 + 5:c0 + 6]),
                            "VD" if t == NT - 1 else None)
                    W("scalar", "VD", bv1)
                    for t in range(NT):
                        c0 = 32 + t * 8
                        bs1 = OP("scalar", lambda e, c0=c0: e.activation(
                            cw[:, c0 + 2:c0 + 3], cw[:, c0 + 1:c0 + 2], AF.Sqrt))
                        bs1 = OP("scalar", lambda e, c0=c0: e.activation(
                            cw[:, c0 + 7:c0 + 8], cw[:, c0 + 6:c0 + 7], AF.Sqrt),
                            "SD" if t == NT - 1 else None)
                    W("vector", "SD", bs1)
                    bv2 = None
                    for t in range(NT):
                        c0 = 32 + t * 8
                        OP("vector", lambda e, c0=c0: e.tensor_tensor(
                            cw[:, c0 + 3:c0 + 4], cw[:, c0 + 2:c0 + 3],
                            cw[:, c0 + 7:c0 + 8], ALU.mult))
                        DR("vector")
                        bv2 = OP("vector", lambda e, t=t, c0=c0: e.tensor_scalar(
                            cw[:, c0 + 4:c0 + 5], mvt4[:, 2 * t:2 * t + 1],
                            cw[:, c0 + 3:c0 + 4], -1.0, ALU.mult, ALU.mult),
                            "VD" if t == NT - 1 else None)
                        if t != NT - 1:
                            DR("vector")
                    W("scalar", "VD", bv2)
                    for t in range(NT):
                        c0 = 32 + t * 8
                        sq = seqsb[:, sh + t * 1024: sh + (t + 1) * 1024]
                        if un and t == 0:
                            W("scalar", "TD", qkv_done[(un - 1, NT - 1)])
                        z_done_map[(un, t)] = OP("scalar", lambda e, t=t, sq=sq, c0=c0: e.activation(
                            zbf[:, t * 1024:(t + 1) * 1024], sq, AF.Identity,
                            scale=cw[:, c0 + 3:c0 + 4], bias=cw[:, c0 + 4:c0 + 5]), "SD")

            emit_AB(0)
            for u in range(16):
                b, s = u // 4, u % 4
                ld_seq = ld_seq_of[u]
                for t in range(NT):
                    gt = s * 4 + t
                    W("tensor", "SD", z_done_map[(u, t)])
                    W("tensor", "SD", bp_free_sd)
                    if u and t == 0:
                        W("tensor", "TD", qkv_done[(u - 1, NT - 1)])
                    for k in range(DC):
                        ztr = OP("tensor", lambda e, t=t, kk=k: e.transpose(
                            Bp[:, kk * 128:kk * 128 + 128],
                            zbf[:, t * 1024 + kk * 128: t * 1024 + kk * 128 + 128],
                            ident[:]), "TD" if k == DC - 1 else None)
                    W("scalar", "TD", ztr)
                    zfm_done[t] = OP("scalar", lambda e, t=t: e.copy(
                        zfm[:].rearrange("p (k x) -> p k x", x=512)[:, :, t * 128:t * 128 + 128],
                        Bp[:].rearrange("p (k x) -> p k x", x=128)), "SD")
                    bp_free_sd = zfm_done[t]
                    W("tensor", "SD", zfm_done[t])
                    pt = (u, t - 1) if t else ((u - 1, NT - 1) if u else None)
                    if u and t == 0:
                        W("tensor", "VD", woutln_vd)
                    for k in range(DC):
                        for jj in range(6):
                            if k == 0 and pt is not None:
                                if jj < 4:
                                    W("tensor", "SD", qke_sd[(pt[0], pt[1], jj)])
                                elif jj == 4:
                                    W("tensor", "SD", vevac_done[pt])
                            qkv_done[(u, t)] = OP("tensor", lambda e, t=t, kk=k, j=jj: e.matmul(
                                A[j][:],
                                zfm[:, kk * 512 + t * 128: kk * 512 + t * 128 + 128],
                                wqkvT[:, kk * 3072 + j * 512: kk * 3072 + (j + 1) * 512],
                                start=(kk == 0), stop=(kk == DC - 1)),
                                "TD" if (k == DC - 1 and jj == 5) else None)
                    W("scalar", "TD", qkv_done[(u, t)])
                    for bank in range(4):
                        if (u, t) != (0, 0):
                            W("scalar", "VD", rope_bank.get(
                                (u, t - 1, bank) if t else (u - 1, NT - 1, bank), 0))
                        qke_sd[(u, t, bank)] = OP("scalar", lambda e, b=bank: e.copy(
                            qke[:, b * 512:(b + 1) * 512], A[b][:]), "SD")
                    if u or t:
                        DR("vector")
                    for bank in range(4):
                        base = t * 2048 + bank * 512
                        W("vector", "SD", qke_sd[(u, t, bank)])
                        Eap = qke[:, bank * 512:(bank + 1) * 512].rearrange(
                            "p (h x) -> p h x", x=64)[:, :, 0:32]
                        Oap = qke[:, bank * 512:(bank + 1) * 512].rearrange(
                            "p (h x) -> p h x", x=64)[:, :, 32:64]
                        cosap = costab[:, gt * 32:(gt + 1) * 32] \
                            .rearrange("p (o x) -> p o x", o=1).broadcast_to([128, 8, 32])
                        sinap = sintab[:, gt * 32:(gt + 1) * 32] \
                            .rearrange("p (o x) -> p o x", o=1).broadcast_to([128, 8, 32])
                        r1v = rta[:].rearrange("p (h x) -> p h x", x=32)
                        r2v = rtb[:].rearrange("p (h x) -> p h x", x=32)
                        outE = qksb[:, base:base + 512].rearrange(
                            "p (h x) -> p h x", x=64)[:, :, 0:32]
                        outO = qksb[:, base:base + 512].rearrange(
                            "p (h x) -> p h x", x=64)[:, :, 32:64]
                        OP("vector", lambda e, a=Eap, c=cosap, o=r1v: e.tensor_tensor(o, a, c, ALU.mult))
                        OP("vector", lambda e, a=Oap, c=sinap, o=r2v: e.tensor_tensor(o, a, c, ALU.mult))
                        DR("vector")
                        OP("vector", lambda e, o=outE, x=r1v, y=r2v: e.tensor_tensor(o, x, y, ALU.subtract))
                        DR("vector")
                        OP("vector", lambda e, a=Oap, c=cosap, o=r1v: e.tensor_tensor(o, a, c, ALU.mult))
                        OP("vector", lambda e, a=Eap, c=sinap, o=r2v: e.tensor_tensor(o, a, c, ALU.mult))
                        DR("vector")
                        rope_done[(u, t)] = OP("vector", lambda e, o=outO, x=r1v, y=r2v:
                                               e.tensor_tensor(o, x, y, ALU.add), "VD")
                        rope_bank[(u, t, bank)] = rope_done[(u, t)]
                        if bank != 3:
                            DR("vector")
                    W("scalar", "TD", qkv_done[(u, t)])
                    for half in range(2):
                        dstv = vst[:, t * 1040 + half * 8 * 65: t * 1040 + (half * 8 + 8) * 65] \
                            .rearrange("p (h c) -> p h c", c=65)[:, :, 0:64]
                        vevac_done[(u, t)] = OP("scalar", lambda e, hh=half, dv=dstv: e.copy(
                            dv, A[4 + hh][:].rearrange("p (h x) -> p h x", x=64)),
                            "SD" if half == 1 else None)
                qkfm_sd = None
                for t in range(NT):
                    for part, dst in ((0, qfm), (1, kfm)):
                        W("tensor", "VD", rope_done[(u, t)])
                        W("tensor", "SD", bp_free_sd)
                        for c in range(DC):
                            ftr = OP("tensor", lambda e, t=t, pp=part, cc=c: e.transpose(
                                Bp[:, cc * 128:cc * 128 + 128],
                                qksb[:, t * 2048 + pp * 1024 + cc * 128:
                                     t * 2048 + pp * 1024 + cc * 128 + 128],
                                ident[:]), "TD" if c == DC - 1 else None)
                        W("scalar", "TD", ftr)
                        qkfm_sd = OP("scalar", lambda e, t=t, dd=dst: e.copy(
                            dd[:].rearrange("p (c x) -> p c x", x=512)[:, :, t * 128:t * 128 + 128],
                            Bp[:].rearrange("p (c x) -> p c x", x=128)), "SD")
                        bp_free_sd = qkfm_sd
                sc_all = {}
                for h in range(H + 1):
                    if h < H:
                        par, ch, pb = h % 2, h // 2, (h % 2) * 64
                        qc = ch * 512
                        _off = [0, 512, 896, 1152, 1280]
                        ebs = [par * 1792 + _off[jc] for jc in range(5)]
                        sc_done = {}
                        if h >= 1:
                            for jc in range(5):
                                W("tensor", "SD", exp_done[(u, h - 1, jc)])
                        else:
                            W("tensor", "SD", qkfm_sd)
                            if u == 0:
                                W("tensor", "GD", gld_pm)
                        sc_done[4] = OP("tensor", lambda e, pb=pb, h=h, qc=qc: e.matmul(
                            A[4][0:16, :], pmkf[pb:pb + 64, h * 16:(h + 1) * 16],
                            qfm[pb:pb + 64, qc:qc + 512], start=True, stop=True), "TD")
                        for jc in range(4):
                            sc_done[jc] = OP("tensor", lambda e, pb=pb, qc=qc, jc=jc: e.matmul(
                                A[jc][:, jc * 128:512],
                                kfm[pb:pb + 64, qc + jc * 128: qc + jc * 128 + 128],
                                qfm[pb:pb + 64, qc + jc * 128: qc + 512],
                                start=True, stop=True), "TD")
                        sc_all[h] = sc_done
                        for jc in (4, 0, 1, 2, 3):
                            W("scalar", "TD", sc_done[jc])
                            if h >= 2:
                                W("scalar", "TD", av_done[(u, h - 2)])
                                if jc < 4:
                                    W("scalar", "VD", mask_done[(u, h - 2, jc)])
                            elif u and h < 2:
                                W("scalar", "TD", av_done[(u - 1, h + 14)])
                                if jc < 4:
                                    W("scalar", "VD", mask_done[(u - 1, h + 14, jc)])
                            if jc == 4:
                                exp_done[(u, h, 4)] = OP("scalar", lambda e, b4=ebs[4]: e.activation(
                                    expt[0:16, b4:b4 + 512], A[4][0:16, :],
                                    AF.Exp, scale=0.125), "SD")
                            else:
                                wdt = 512 - jc * 128
                                exp_done[(u, h, jc)] = OP("scalar",
                                    lambda e, jc=jc, bj=ebs[jc], w=wdt: e.activation(
                                        expt[:, bj: bj + w],
                                        A[jc][:, jc * 128:512], AF.Exp, scale=0.125), "SD")
                        for jc in range(4):
                            W("vector", "SD", exp_done[(u, h, jc)])
                            dg = expt[:, ebs[jc]: ebs[jc] + 128]
                            mask_done[(u, h, jc)] = OP("vector", lambda e, d=dg: e.tensor_tensor(
                                d, d, mask[:], ALU.mult), "VD")
                    if h >= 1:
                        hh2 = h - 1
                        par2 = hh2 % 2
                        _off = [0, 512, 896, 1152, 1280]
                        eb2 = [par2 * 1792 + _off[jc] for jc in range(5)]
                        W("tensor", "SD", exp_done[(u, hh2, 4)])
                        for jc in range(4):
                            W("tensor", "SD", exp_done[(u, hh2, jc)])
                            W("tensor", "VD", mask_done[(u, hh2, jc)])
                        if hh2 >= 1 or u:
                            W("tensor", "SD", rec_done)
                            W("tensor", "SD", ao_done)
                        OP("tensor", lambda e, h=hh2, b4=eb2[4]: e.matmul(
                            A[5][0:65, :], pmvst[0:16, h * 65:(h + 1) * 65],
                            expt[0:16, b4:b4 + 512], start=True, stop=False))
                        for jc in range(4):
                            wdt = 512 - jc * 128
                            av_done[(u, hh2)] = OP("tensor",
                                lambda e, h=hh2, jc=jc, bj=eb2[jc], w=wdt: e.matmul(
                                    A[5][0:65, jc * 128:512],
                                    vst[:, jc * 1040 + h * 65: jc * 1040 + h * 65 + 65],
                                    expt[:, bj: bj + w],
                                    start=False, stop=(jc == 3)), "TD" if jc == 3 else None)
                        W("scalar", "TD", av_done[(u, hh2)])
                        if hh2 >= 2:
                            W("scalar", "GR%d" % (hh2 % 2), rs_row_st[(u, hh2 - 2)])
                        elif u and hh2 < 2:
                            W("scalar", "GR%d" % (hh2 % 2), rs_row_st[(u - 1, hh2 + 14)])
                        rec_done = OP("scalar", lambda e, h=hh2: e.copy(
                            sumsb[0:1, (h % 2) * 512:(h % 2) * 512 + 512],
                            A[5][64:65, :]), "SD")
                        W("gpsimd", "SD", rec_done)
                        rs_row_st[(u, hh2)] = dma("gpsimd", "GR%d" % (hh2 % 2),
                                                  rs_d[hh2:hh2 + 1, :],
                                                  sumsb[0:1, (hh2 % 2) * 512:(hh2 % 2) * 512 + 512])
                        if hh2 == 0:
                            W("scalar", "VD", prev_norm_vd)
                        ao_done = OP("scalar", lambda e, h=hh2, pb2=(hh2 % 2) * 64, ch2=hh2 // 2: e.copy(
                            aoun[pb2:pb2 + 64, ch2 * 512:(ch2 + 1) * 512], A[5][0:64, :]), "SD")
                rxp_vd = {}
                if u >= 1:
                    W("sync", "TD", last_rmm_td)
                W("sync", "GR0", CV("GR0"))
                W("sync", "GR1", CV("GR1"))
                if u >= 1:
                    W("sync", "VD", rsum32_vd)
                ld_rs = dma("sync", "LRS", rsum32[:], rs_d)
                W("vector", "LRS", ld_rs)
                DR("vector")

                if u >= 1:
                    W("vector", "TD", last_rmm_td)

                def _recb(e):
                    with nc.allow_low_precision(reason="softmax normalizer as bf16 matmul operand"):
                        return e.reciprocal(rsum[:], rsum32[:])

                rsum32_vd = OP("vector", _recb, "VD")
                for p in range(DC):
                    W("tensor", "VD", rsum32_vd)
                    if p >= 1:
                        W("tensor", "VD", rxp_vd[p - 1])
                    elif u:
                        W("tensor", "VD", woutln_vd)
                    rtd = OP("tensor", lambda e, p=p: e.matmul(
                        Cp[:], indsel[0:16, p * 128:(p + 1) * 128], rsum[0:16, :],
                        start=True, stop=True), "TD")
                    last_rmm_td = rtd
                    W("vector", "TD", rtd)
                    rxp_vd[p] = OP("vector", lambda e, p=p: e.tensor_copy(
                        rxp[:, p * 512:(p + 1) * 512], Cp[:]), "VD")
                DR("vector")
                W("vector", "SD", ao_done)
                for p in range(DC):
                    norm_vd = OP("vector", lambda e, p=p: e.tensor_tensor(
                        aoun[:, p * 512:(p + 1) * 512], aoun[:, p * 512:(p + 1) * 512],
                        rxp[:, p * 512:(p + 1) * 512], ALU.mult), "VD")
                prev_norm_vd = norm_vd
                if u + 1 < 16:
                    emit_AB(u + 1)
                wout_vd_par = {}
                for t in range(NT):
                    bw = (t % 2) * 2
                    W("tensor", "VD", norm_vd)
                    if t >= 2:
                        W("tensor", "VD", wout_vd_par[t - 2])
                    for k in range(DC):
                        for jj in range(2):
                            wmm = OP("tensor", lambda e, t=t, kk=k, j=jj, bw=bw: e.matmul(
                                A[bw + j][:],
                                aoun[:, kk * 512 + t * 128: kk * 512 + t * 128 + 128],
                                woutT[:, kk * 1024 + j * 512: kk * 1024 + (j + 1) * 512],
                                start=(kk == 0), stop=(kk == DC - 1)),
                                "TD" if (k == DC - 1 and jj == 1) else None)
                    W("vector", "TD", wmm)
                    DR("vector")
                    DR("scalar")
                    OP("vector", lambda e, bw=bw: e.bn_stats(st6[:, 0:6], A[bw][:]))
                    OP("vector", lambda e, bw=bw: e.bn_stats(st6[:, 6:12], A[bw + 1][:]))
                    DR("vector")
                    OP("vector", lambda e: e.bn_aggr(mvt[:], st6[:]))
                    DR("vector")
                    OP("vector", lambda e: e.tensor_scalar(cw[:, 10:11], mvt[:, 1:2],
                                                           EPS_LN, None, ALU.add))
                    DR("vector")
                    vv = OP("vector", lambda e: e.reciprocal(cw[:, 11:12], cw[:, 10:11]), "VD")
                    W("scalar", "VD", vv)
                    sv = OP("scalar", lambda e: e.activation(cw[:, 12:13], cw[:, 11:12], AF.Sqrt), "SD")
                    W("vector", "SD", sv)
                    for half in range(2):
                        OP("vector", lambda e, hh=half, bw=bw: e.tensor_scalar(
                            x1t[:, hh * 512:(hh + 1) * 512], A[bw + hh][:],
                            mvt[:, 0:1], cw[:, 12:13], ALU.subtract, ALU.mult))
                    DR("vector")
                    OP("vector", lambda e, t=t, shu=(u % 2) * 4096: e.tensor_tensor(
                        x1t[:], x1t[:],
                        seqsb[:, shu + t * 1024: shu + (t + 1) * 1024], ALU.add))
                    DR("vector")
                    OP("vector", lambda e: e.bn_stats(st6[:, 0:6], x1t[:, 0:512]))
                    OP("vector", lambda e: e.bn_stats(st6[:, 6:12], x1t[:, 512:1024]))
                    DR("vector")
                    OP("vector", lambda e: e.bn_aggr(mvt[:], st6[:]))
                    DR("vector")
                    OP("vector", lambda e: e.tensor_scalar(cw[:, 14:15], mvt[:, 1:2],
                                                           EPS_LN, None, ALU.add))
                    DR("vector")
                    vv = OP("vector", lambda e: e.reciprocal(cw[:, 15:16], cw[:, 14:15]), "VD")
                    seq_par_read[u % 2] = vv
                    W("scalar", "VD", vv)
                    sv = OP("scalar", lambda e: e.activation(cw[:, 16:17], cw[:, 15:16], AF.Sqrt))
                    sv = OP("scalar", lambda e: e.activation(cw[:, 17:18], cw[:, 14:15], AF.Sqrt), "SD", 1)
                    W("vector", "SD", sv)
                    slot = 0
                    if (u * 4 + t - 1) >= 0:
                        W("vector", "SX%d" % ((t - 1) % 2), x2_store_st.get(u * 4 + t - 1, 0))
                    OP("vector", lambda e, sl=slot: e.tensor_scalar(
                        x2t[:, sl:sl + 1024], x1t[:], mvt[:, 0:1], cw[:, 16:17],
                        ALU.subtract, ALU.mult))
                    DR("vector")
                    OP("vector", lambda e, u=u, t=t, sl=slot: e.tensor_reduce(
                        mx[:, MX_X + u * 4 + t: MX_X + u * 4 + t + 1],
                        x2t[:, sl:sl + 1024], axis=AX.X, op=ALU.max,
                        apply_absolute_value=True))
                    OP("vector", lambda e, t=t: e.tensor_copy(sgst[:, t * 2:t * 2 + 1], cw[:, 17:18]))
                    woutln_vd = OP("vector", lambda e, t=t: e.tensor_copy(
                        sgst[:, t * 2 + 1:t * 2 + 2], mvt[:, 0:1]), "VD")
                    wout_vd_par[t] = woutln_vd
                    W("scalar", "VD", woutln_vd)
                    x2_store_st[u * 4 + t] = dma("scalar", "SX%d" % (t % 2),
                                                 x2_d[u * 4 + t], x2t[:, slot:slot + 1024])
                W("scalar", "VD", woutln_vd)
                if u >= 1:
                    pass
                sg_store_st = dma("scalar", "SSG", sig_d[u], sgst[:])
                if u < 15:
                    W("vector", "SSG", sg_store_st)

        if max_phase >= 2:
            DR("vector")
            OP("vector", lambda e: e.tensor_reduce(cw[:, 18:19], mx[:, MX_X:MX_X + 64],
                                                   axis=AX.X, op=ALU.max))
            DR("vector")
            mxv = OP("vector", lambda e: e.tensor_copy(cw[:, 18:19], cw[:, 18:19]), "VD")

            cc_state = {"btd": 0}

            def collective(idx, col_ap, after_vd, out_col_ap):
                W("tensor", "VD", after_vd)
                ttd = OP("tensor", lambda e, c=col_ap: e.transpose(
                    Cp[0:1, 0:128], c, identf[:]), "TD")
                W("vector", "TD", ttd)
                DR("vector")
                mv_ = OP("vector", lambda e: e.reduce_max(
                    mcol8[0:1, 0:1], Cp[0:1, 0:128], axis=AX.X), "VD")
                W("gpsimd", "VD", mv_)
                W("gpsimd", "TD", cc_state["btd"])
                gd = dma("gpsimd", "GD", cci[idx], mcol8[:])
                W("gpsimd", "GD", gd)
                ccv = OP("gpsimd", lambda e, i=idx: e.collective_compute(
                    "AllReduce", ALU.max, replica_groups=[list(range(n_cores))],
                    ins=[cci[i].opt()], outs=[cco[i].opt()]), "CC")
                W("gpsimd", "CC", ccv)
                gd = dma("gpsimd", "GD", mg8[:], cco[idx])
                W("tensor", "GD", gd)
                btd = OP("tensor", lambda e: e.matmul(
                    Cp[0:128, 0:1], onesrow[0:1, 0:128], mg8[0:1, 0:1],
                    start=True, stop=True), "TD")
                cc_state["btd"] = btd
                W("vector", "TD", btd)
                vv = OP("vector", lambda e, o=out_col_ap: e.tensor_copy(
                    o, Cp[0:128, 0:1]), "VD")
                return vv

            collective(0, cw[:, 18:19], mxv, cw[:, 19:20])
            DR("vector")
            OP("vector", lambda e: e.reciprocal(cw[:, 20:21], cw[:, 19:20]))
            DR("vector")
            OP("vector", lambda e: e.tensor_scalar(scl[:, SXI:SXI + 1], cw[:, 20:21],
                                                   32767.0, CLIP, ALU.mult, ALU.min))
            DR("vector")
            sc_x_vd = OP("vector", lambda e: e.reciprocal(scl[:, SX:SX + 1],
                                                          scl[:, SXI:SXI + 1]), "VD")

            def barrier():
                vals = dict(cnt)
                for e in ENGINES:
                    for k, v in vals.items():
                        W(e, k, v)
            barrier()

        if max_phase >= 2.2:
            xq_sd = {}
            xq_td = {}
            xq_vd_hist = {}
            NTT_RUN = NTT if max_phase >= 2.5 else 8
            for i in range(NTT_RUN):
                slot = (i % 2) * 1024
                if i >= 2:
                    W("sync", "VD", xq_vd_hist[i - 2])
                ldt = dma("sync", "LX%d" % (i % 2), xin[:, slot:slot + 1024],
                          x2_d[i][:, :])
                W("vector", "LX%d" % (i % 2), ldt)
                W("vector", "VD", sc_x_vd)
                if i >= 1:
                    W("vector", "TD", xq_td[i - 1])
                    DR("vector")
                OP("vector", lambda e, sl=slot: e.tensor_scalar(
                    xt1[:], xin[:, sl:sl + 1024], scl[:, SXI:SXI + 1], None, ALU.mult))
                DR("vector")
                OP("vector", lambda e: e.tensor_scalar(xr2[:], xt1[:], MAGIC, MAGIC,
                                                       ALU.add, ALU.subtract))
                DR("vector")
                xq_vd = OP("vector", lambda e: e.tensor_scalar(
                    xqtm[:], xr2[:], scl[:, SX:SX + 1], None, ALU.mult), "VD")
                xq_vd_hist[i] = xq_vd
                W("tensor", "VD", xq_vd)
                W("tensor", "SD", bp_free_sd)
                for k in range(DC):
                    xq_td[i] = OP("tensor", lambda e, kk=k: e.transpose(
                        Bp[:, kk * 128:kk * 128 + 128],
                        xqtm[:, kk * 128:kk * 128 + 128], ident[:]),
                        "TD" if k == DC - 1 else None)
                W("scalar", "TD", xq_td[i])
                xq_sd[i] = OP("scalar", lambda e, i=i: e.copy(
                    xqfm[:].rearrange("p (k x) -> p k x", x=8192)[:, :, i * 128:i * 128 + 128],
                    Bp[:].rearrange("p (k x) -> p k x", x=128)), "SD")
                bp_free_sd = xq_sd[i]
            xq_all_sd = xq_sd[NTT_RUN - 1]

            bank_vd = [0, 0, 0, 0]
            bank_sd = [0, 0, 0, 0]
            stg_st = [0, 0, 0, 0]
            wq1_done = {}
            wtr_sd_of = {}
            last_mm_td = 0
            mm_last_of = {}
            CC2_RUN = 48 if max_phase >= 2.75 else (4 if max_phase >= 2.6 else 0)

            def srcw(cc2):
                m, wg = cc2 // 2, cc2 % 2
                return (wup_d if wg == 0 else wgate_d)[m * 128:(m + 1) * 128, :]

            def emit_dequant(cc2):
                slot = (cc2 % 2) * 1024
                W("vector", "LWS%d" % (cc2 % 2), ld_of[cc2])
                if cc2 >= 1:
                    W("vector", "TD", wtr_td_of[cc2 - 1])
                DR("vector")
                OP("vector", lambda e, sl=slot: e.tensor_reduce(
                    mxb32[:], wstr[:, sl:sl + 1024].rearrange("p (a b) -> p a b", b=32),
                    axis=AX.X, op=ALU.max, apply_absolute_value=True))
                DR("vector")
                OP("vector", lambda e: e.reciprocal(rmxb[:], mxb32[:]))
                DR("vector")
                OP("vector", lambda e: e.tensor_scalar(sinvb[:], rmxb[:], 7.0,
                                                       CLIP, ALU.mult, ALU.min))
                DR("vector")
                OP("vector", lambda e: e.reciprocal(sblk[:], sinvb[:]))
                DR("vector")
                wq1_done[cc2] = OP("vector", lambda e, sl=slot: e.tensor_tensor(
                    wq1[:].rearrange("p (a b) -> p a b", b=32),
                    wstr[:, sl:sl + 1024].rearrange("p (a b) -> p a b", b=32),
                    sinvb[:].rearrange("p (a b) -> p a b", b=1).broadcast_to([128, 32, 32]),
                    ALU.mult), "VD")
                DR("vector")
                OP("vector", lambda e: e.tensor_scalar(wr2[:], wq1[:], MAGIC, MAGIC,
                                                       ALU.add, ALU.subtract))
                DR("vector")
                wdq_vd = OP("vector", lambda e: e.tensor_tensor(
                    wdqb[:].rearrange("p (a b) -> p a b", b=32),
                    wr2[:].rearrange("p (a b) -> p a b", b=32),
                    sblk[:].rearrange("p (a b) -> p a b", b=1).broadcast_to([128, 32, 32]),
                    ALU.mult), "VD")
                W("tensor", "VD", wdq_vd)
                W("tensor", "SD", bp_state[0])
                for k in range(DC):
                    wtr_td = OP("tensor", lambda e, kk=k: e.transpose(
                        Bp[:, kk * 128:kk * 128 + 128],
                        wdqb[:, kk * 128:kk * 128 + 128], ident[:]),
                        "TD" if k == DC - 1 else None)
                wtr_td_of[cc2] = wtr_td
                W("scalar", "TD", wtr_td)
                if cc2 >= 2:
                    W("scalar", "TD", mm_last_of[cc2 - 2])
                wslot = (cc2 % 2) * 1024
                wtr_sd_of[cc2] = OP("scalar", lambda e, ws=wslot: e.copy(
                    wtr[:, ws:ws + 1024], Bp[:]), "SD")
                bp_state[0] = wtr_sd_of[cc2]

            bp_state = [bp_free_sd]
            wtr_td_of = {}
            ld_of = {}
            if CC2_RUN:
                ld_of[0] = dma("sync", "LWS0", wstr[:, 0:1024], srcw(0))
                emit_dequant(0)
            for cc2 in range(CC2_RUN):
                m, wg = cc2 // 2, cc2 % 2
                wdst = u_d if wg == 0 else gg_d
                mxbase = MX_U if wg == 0 else MX_G
                wslot = (cc2 % 2) * 1024
                for gq in range(NG):
                    bk = gq % 4
                    if gq == 3 and cc2 + 1 < CC2_RUN:
                        W("sync", "VD", wq1_done.get(cc2 - 1, 0))
                        ld_of[cc2 + 1] = dma("sync", "LWS%d" % ((cc2 + 1) % 2),
                                             wstr[:, ((cc2 + 1) % 2) * 1024:
                                                  ((cc2 + 1) % 2) * 1024 + 1024],
                                             srcw(cc2 + 1))
                        emit_dequant(cc2 + 1)
                    W("tensor", "SD", wtr_sd_of[cc2])
                    W("tensor", "VD", bank_vd[bk])
                    W("tensor", "SD", bank_sd[bk])
                    for k in range(DC):
                        mmtd = OP("tensor", lambda e, kk=k, gq=gq, bk=bk, ws=wslot: e.matmul(
                            A[bk][:], wtr[:, ws + kk * 128: ws + kk * 128 + 128],
                            xqfm[:, kk * 8192 + gq * 512: kk * 8192 + (gq + 1) * 512],
                            start=(kk == 0), stop=(kk == DC - 1)),
                            "TD" if k == DC - 1 else None)
                    last_mm_td = mmtd
                    mm_last_of[cc2] = mmtd
                    W("vector", "TD", mmtd)
                    bank_vd[bk] = OP("vector", lambda e, bk=bk, col=mxbase + m * 16 + gq:
                                     e.tensor_reduce(mx[:, col:col + 1], A[bk][:],
                                                     axis=AX.X, op=ALU.max,
                                                     apply_absolute_value=True), "VD")
                    W("scalar", "TD", mmtd)
                    W("scalar", "VD", bank_vd[bk])
                    W("scalar", "SG%d" % bk, stg_st[bk])
                    bank_sd[bk] = OP("scalar", lambda e, bk=bk: e.copy(
                        stg[:, bk * 512:(bk + 1) * 512], A[bk][:]), "SD")
                    W("scalar", "SD", bank_sd[bk])
                    stg_st[bk] = dma("scalar", "SG%d" % bk, wdst[m, gq],
                                     stg[:, bk * 512:(bk + 1) * 512])
            bp_free_sd = bp_state[0]

            W("sync", "VD", CV("VD"))
            W("sync", "TD", CV("TD"))
            wdn_vd = 0
            wdn_td = 0
            WDN_RUN = 16 if max_phase >= 2.9 else (1 if max_phase >= 2.8 else 0)
            for r2 in range(WDN_RUN):
                r, hh = r2 // 2, r2 % 2
                if r2 >= 1:
                    W("sync", "VD", wdn_vd)
                ldw = dma("sync", "LWD", wdnin[:],
                          wdown_d[r * 128:(r + 1) * 128, hh * 1536:(hh + 1) * 1536])
                W("vector", "LWD", ldw)
                if r2 >= 1:
                    W("vector", "TD", wdn_td)
                DR("vector")
                OP("vector", lambda e: e.tensor_reduce(
                    mxb48[:], wdnin[:].rearrange("p (a b) -> p a b", b=32),
                    axis=AX.X, op=ALU.max, apply_absolute_value=True))
                DR("vector")
                OP("vector", lambda e: e.reciprocal(rmx48[:], mxb48[:]))
                DR("vector")
                OP("vector", lambda e: e.tensor_scalar(siv48[:], rmx48[:], 7.0, CLIP,
                                                       ALU.mult, ALU.min))
                DR("vector")
                OP("vector", lambda e: e.reciprocal(sbk48[:], siv48[:]))
                DR("vector")
                OP("vector", lambda e: e.tensor_tensor(
                    wdn1[:].rearrange("p (a b) -> p a b", b=32),
                    wdnin[:].rearrange("p (a b) -> p a b", b=32),
                    siv48[:].rearrange("p (a b) -> p a b", b=1).broadcast_to([128, 48, 32]),
                    ALU.mult))
                DR("vector")
                OP("vector", lambda e: e.tensor_scalar(wdnr[:], wdn1[:], MAGIC, MAGIC,
                                                       ALU.add, ALU.subtract))
                DR("vector")
                wdn_vd = OP("vector", lambda e: e.tensor_tensor(
                    wdnb[:].rearrange("p (a b) -> p a b", b=32),
                    wdnr[:].rearrange("p (a b) -> p a b", b=32),
                    sbk48[:].rearrange("p (a b) -> p a b", b=1).broadcast_to([128, 48, 32]),
                    ALU.mult), "VD")
                W("tensor", "VD", wdn_vd)
                for bb in range(2):
                    nblk = 8 if bb == 0 else 4
                    W("tensor", "SD", bp_free_sd)
                    for k in range(nblk):
                        mblk = bb * 8 + k
                        wdn_td = OP("tensor", lambda e, kk=k, mb=mblk: e.transpose(
                            Bp[:, kk * 128:kk * 128 + 128],
                            wdnb[:, mb * 128:mb * 128 + 128], ident[:]),
                            "TD" if k == nblk - 1 else None)
                    W("scalar", "TD", wdn_td)
                    mg0 = hh * 12 + bb * 8
                    bp_free_sd = OP("scalar", lambda e, mg0=mg0, r=r, nb=nblk: e.copy(
                        wdT[:].rearrange("p (mm x) -> p mm x", x=1024)
                        [:, mg0:mg0 + nb, r * 128:r * 128 + 128],
                        Bp[:, 0:nb * 128].rearrange("p (k x) -> p k x", x=128)), "SD")

        if max_phase >= 3.2:
            DR("vector")
            OP("vector", lambda e: e.tensor_reduce(cw[:, 18:19], mx[:, MX_U:MX_U + 384],
                                                   axis=AX.X, op=ALU.max))
            mug = OP("vector", lambda e: e.tensor_reduce(cw[:, 19:20], mx[:, MX_G:MX_G + 384],
                                                         axis=AX.X, op=ALU.max), "VD")
            DR("vector")
            if max_phase >= 3.4:
                collective(1, cw[:, 18:19], mug, cw[:, 20:21])
            if max_phase >= 3.6:
                collective(2, cw[:, 19:20], CV("VD"), cw[:, 21:22])
            if max_phase >= 3.8:
                DR("vector")
                OP("vector", lambda e: e.reciprocal(cw[:, 22:23], cw[:, 20:21]))
                OP("vector", lambda e: e.reciprocal(cw[:, 23:24], cw[:, 21:22]))
                DR("vector")
                OP("vector", lambda e: e.tensor_scalar(scl[:, SUI:SUI + 1], cw[:, 22:23],
                                                       32767.0, CLIP, ALU.mult, ALU.min))
                OP("vector", lambda e: e.tensor_scalar(scl[:, SGI:SGI + 1], cw[:, 23:24],
                                                       32767.0, CLIP, ALU.mult, ALU.min))
                DR("vector")
                OP("vector", lambda e: e.reciprocal(scl[:, SU:SU + 1], scl[:, SUI:SUI + 1]))
                OP("vector", lambda e: e.reciprocal(scl[:, SG:SG + 1], scl[:, SGI:SGI + 1]))
                DR("vector")
                sc_ug_vd = OP("vector", lambda e: e.tensor_scalar(
                    scl[:, SGQ:SGQ + 1], scl[:, SG:SG + 1], float(2.0 ** -16), None,
                    ALU.mult), "VD")

        if max_phase >= 5:
            barrier()

            p3_st = {}
            p3v1 = {}
            p3v2 = {}
            a3sd3 = {}
            for ci in range(97):
                if ci < 96:
                    m, q4 = ci // 4, ci % 4
                    hf = (ci % 2) * 2048
                    if ci >= 2:
                        W("sync", "VD", p3v1[ci - 2])
                    ldg = dma("sync", "LG%d" % (ci % 2), gin[:, hf:hf + 2048]
                              .rearrange("p (a b) -> p a b", b=512),
                              gg_d[m, q4 * 4:(q4 + 1) * 4].rearrange("a p b -> p a b"))
                    W("vector", "LG%d" % (ci % 2), ldg)
                    W("vector", "VD", sc_ug_vd)
                    if ci >= 2:
                        W("vector", "SD", a3sd3[ci - 2])
                    DR("vector")
                    OP("vector", lambda e, h=hf: e.tensor_scalar(
                        tb1[:, h:h + 2048], gin[:, h:h + 2048],
                        scl[:, SGI:SGI + 1], None, ALU.mult))
                    DR("vector")
                    p3v1[ci] = OP("vector", lambda e, h=hf: e.tensor_scalar(
                        rb3[:, h:h + 2048], tb1[:, h:h + 2048], MAGIC, MAGIC,
                        ALU.add, ALU.subtract), "VD")
                    W("scalar", "VD", p3v1[ci])
                    DR("scalar")
                    OP("scalar", lambda e, h=hf: e.activation(
                        sigb[:, h:h + 2048], rb3[:, h:h + 2048], AF.Sigmoid,
                        scale=scl[:, SG:SG + 1]))
                    DR("scalar")
                    OP("scalar", lambda e, h=hf: e.activation(
                        tb1[:, h:h + 2048], sigb[:, h:h + 2048], AF.Identity,
                        scale=65536.0, bias=cmag[:, 0:1]))
                    DR("scalar")
                    a3sd3[ci] = OP("scalar", lambda e, h=hf: e.activation(
                        a3b[:, h:h + 2048], tb1[:, h:h + 2048], AF.Identity,
                        bias=cmag[:, 1:2]), "SD")
                if ci >= 1:
                    j = ci - 1
                    mj, q4j = j // 4, j % 4
                    hj = (j % 2) * 2048
                    W("vector", "SD", a3sd3[j])
                    OP("vector", lambda e, h=hj: e.tensor_scalar(
                        c2b[:, h:h + 2048], a3b[:, h:h + 2048], 65535.0,
                        scl[:, SGQ:SGQ + 1], ALU.min, ALU.mult))
                    DR("vector")
                    W("vector", "STB%d" % (j % 2), p3_st.get(j - 2, 0))
                    OP("vector", lambda e, h=hj: e.tensor_tensor(
                        tob[:, h:h + 2048], rb3[:, h:h + 2048],
                        c2b[:, h:h + 2048], ALU.mult))
                    DR("vector")
                    p3v2[j] = OP("vector", lambda e, j=j, h=hj: e.tensor_reduce(
                        mx[:, MX_T + j:MX_T + j + 1], tob[:, h:h + 2048],
                        axis=AX.X, op=ALU.max, apply_absolute_value=True), "VD")
                    W("scalar", "VD", p3v2[j])
                    p3_st[j] = dma("scalar", "STB%d" % (j % 2),
                                   t_d[mj, q4j * 4:(q4j + 1) * 4].rearrange("a p b -> p a b"),
                                   tob[:, hj:hj + 2048].rearrange("p (a b) -> p a b", b=512))

            DR("vector")
            mt = OP("vector", lambda e: e.tensor_reduce(cw[:, 18:19], mx[:, MX_T:MX_T + 96],
                                                        axis=AX.X, op=ALU.max), "VD")
            DR("vector")
            collective(3, cw[:, 18:19], mt, cw[:, 19:20])
            DR("vector")
            OP("vector", lambda e: e.reciprocal(cw[:, 20:21], cw[:, 19:20]))
            DR("vector")
            OP("vector", lambda e: e.tensor_scalar(scl[:, STI:STI + 1], cw[:, 20:21],
                                                   32767.0, CLIP, ALU.mult, ALU.min))
            DR("vector")
            OP("vector", lambda e: e.reciprocal(cw[:, 21:22], scl[:, STI:STI + 1]))
            DR("vector")
            sc_t_vd = OP("vector", lambda e: e.tensor_tensor(
                scl[:, STU:STU + 1], cw[:, 21:22], scl[:, SU:SU + 1], ALU.mult), "VD")

        if max_phase >= 6:
            barrier()

            p4_st = {}
            p4v1 = {}
            p4v2 = {}
            rusd4 = {}
            for ci in range(97):
                if ci < 96:
                    m, q4 = ci // 4, ci % 4
                    hf = (ci % 2) * 2048
                    if ci >= 2:
                        W("sync", "VD", p4v1[ci - 2])
                        W("sync", "SD", rusd4[ci - 2])
                    ldt = dma("sync", "LG%d" % (ci % 2), gin[:, hf:hf + 2048]
                              .rearrange("p (a b) -> p a b", b=512),
                              t_d[m, q4 * 4:(q4 + 1) * 4].rearrange("a p b -> p a b"))
                    ldu = dma("sync", "LU%d" % (ci % 2), uin[:, hf:hf + 2048]
                              .rearrange("p (a b) -> p a b", b=512),
                              u_d[m, q4 * 4:(q4 + 1) * 4].rearrange("a p b -> p a b"))
                    W("vector", "LG%d" % (ci % 2), ldt)
                    W("vector", "VD", sc_t_vd)
                    if ci >= 2:
                        W("vector", "SD", rusd4[ci - 2])
                    DR("vector")
                    OP("vector", lambda e, h=hf: e.tensor_scalar(
                        tb1[:, h:h + 2048], gin[:, h:h + 2048],
                        scl[:, STI:STI + 1], None, ALU.mult))
                    DR("vector")
                    OP("vector", lambda e, h=hf: e.tensor_scalar(
                        rb3[:, h:h + 2048], tb1[:, h:h + 2048], MAGIC, MAGIC,
                        ALU.add, ALU.subtract))
                    DR("vector")
                    p4v1[ci] = OP("vector", lambda e, h=hf: e.tensor_scalar(
                        c2b[:, h:h + 2048], rb3[:, h:h + 2048],
                        scl[:, STU:STU + 1], None, ALU.mult), "VD")
                    W("scalar", "LU%d" % (ci % 2), ldu)
                    W("scalar", "VD", p4v1[ci])
                    DR("scalar")
                    OP("scalar", lambda e, h=hf: e.activation(
                        tb1[:, h:h + 2048], uin[:, h:h + 2048], AF.Identity,
                        scale=scl[:, SUI:SUI + 1], bias=cmag[:, 0:1]))
                    DR("scalar")
                    rusd4[ci] = OP("scalar", lambda e, h=hf: e.activation(
                        a3b[:, h:h + 2048], tb1[:, h:h + 2048], AF.Identity,
                        bias=cmag[:, 1:2]), "SD")
                if ci >= 1:
                    j = ci - 1
                    mj, q4j = j // 4, j % 4
                    hj = (j % 2) * 2048
                    W("vector", "SD", rusd4[j])
                    W("vector", "STB%d" % (j % 2), p4_st.get(j - 2, 0))
                    OP("vector", lambda e, h=hj: e.tensor_tensor(
                        tob[:, h:h + 2048], c2b[:, h:h + 2048],
                        a3b[:, h:h + 2048], ALU.mult))
                    DR("vector")
                    p4v2[j] = OP("vector", lambda e, j=j, h=hj: e.tensor_reduce(
                        mx[:, MX_O + j:MX_O + j + 1], tob[:, h:h + 2048],
                        axis=AX.X, op=ALU.max, apply_absolute_value=True), "VD")
                    W("scalar", "VD", p4v2[j])
                    p4_st[j] = dma("scalar", "STB%d" % (j % 2),
                                   o_d[mj, q4j * 4:(q4j + 1) * 4].rearrange("a p b -> p a b"),
                                   tob[:, hj:hj + 2048].rearrange("p (a b) -> p a b", b=512))

            DR("vector")
            mo = OP("vector", lambda e: e.tensor_reduce(cw[:, 18:19], mx[:, MX_O:MX_O + 96],
                                                        axis=AX.X, op=ALU.max), "VD")
            DR("vector")
            collective(4, cw[:, 18:19], mo, cw[:, 19:20])
            DR("vector")
            OP("vector", lambda e: e.reciprocal(cw[:, 20:21], cw[:, 19:20]))
            DR("vector")
            OP("vector", lambda e: e.tensor_scalar(scl[:, SOI:SOI + 1], cw[:, 20:21],
                                                   32767.0, CLIP, ALU.mult, ALU.min))
            DR("vector")
            sc_o_vd = OP("vector", lambda e: e.reciprocal(scl[:, SO:SO + 1],
                                                          scl[:, SOI:SOI + 1]), "VD")

        if max_phase >= 7:
            barrier()

            dn_td = {}
            out_st = {}
            ln_vd = 0
            ln_vd_par = {}
            x2g_vd_par = {}
            x2g_vd = 0
            oq_vd_hist = {}
            oq_sd_hist = {}
            oq_last_vd = {}
            ld_oq = {}

            def emit_oq_prep(g, mlist):
                half = (g % 2) * 12288
                for m in mlist:
                    slot = (m % 8) * 512
                    if (g, m - 8) in oq_sd_hist:
                        W("sync", "SD", oq_sd_hist[(g, m - 8)])
                    elif (g - 1, 16 + m) in oq_sd_hist:
                        W("sync", "SD", oq_sd_hist[(g - 1, 16 + m)])
                    ldo = dma("sync", "LO%d" % (m % 8), gin[:, slot:slot + 512],
                              o_d[m, g])
                    W("scalar", "LO%d" % (m % 8), ldo)
                    W("scalar", "VD", sc_o_vd)
                    DR("scalar")
                    OP("scalar", lambda e, sl=slot: e.activation(
                        ot1[:], gin[:, sl:sl + 512], AF.Identity,
                        scale=scl[:, SOI:SOI + 1], bias=cmag[:, 0:1]))
                    DR("scalar")
                    if (g, m - 2) in oq_vd_hist:
                        W("scalar", "VD", oq_vd_hist[(g, m - 2)])
                    elif (g - 1, 22 + m) in oq_vd_hist:
                        W("scalar", "VD", oq_vd_hist[(g - 1, 22 + m)])
                    o2s = (m % 2) * 512
                    osd = OP("scalar", lambda e, o2s=o2s: e.activation(
                        ot2[:, o2s:o2s + 512], ot1[:], AF.Identity, bias=cmag[:, 1:2]), "SD")
                    W("vector", "SD", osd)
                    if g >= 2 and m == 0:
                        W("vector", "TD", dn_td[(g - 2, NT - 1)])
                    DR("vector")
                    oq_sd_hist[(g, m)] = osd
                    oq_vd_hist[(g, m)] = OP("vector", lambda e, m=m, hh=half, o2s=o2s: e.tensor_scalar(
                        oqb[:, hh + m * 512: hh + (m + 1) * 512], ot2[:, o2s:o2s + 512],
                        scl[:, SO:SO + 1], None, ALU.mult), "VD")
                oq_last_vd[g] = oq_vd_hist[(g, mlist[-1])] if mlist else oq_last_vd.get(g, 0)

            emit_oq_prep(0, list(range(FC)))
            for gq in range(NG):
                half = (gq % 2) * 12288
                xgh = (gq % 2) * 4096
                if gq >= 2:
                    W("sync", "VD", x2g_vd_par[gq - 2])
                ldx = dma("sync", "LXG",
                          x2g[:, xgh:xgh + 4096].rearrange("p (t d) -> p t d", d=1024),
                          x2_d[gq * 4:(gq + 1) * 4].rearrange("t p d -> p t d"))
                lds = dma("sync", "LXG", sg4b[:, (gq % 2) * 8:(gq % 2) * 8 + 8], sig_d[gq])
                for tt in range(NT):
                    bw = (tt % 2) * 2
                    W("tensor", "VD", oq_last_vd[gq])
                    kprev = (gq, tt - 2) if tt >= 2 else (gq - 1, tt + 2)
                    if kprev in ln_vd_par:
                        W("tensor", "VD", ln_vd_par[kprev])
                    for m in range(FC):
                        for jj in range(2):
                            dn_td[(gq, tt)] = OP("tensor", lambda e, m=m, tt=tt, j=jj, hh=half, bw=bw:
                                                 e.matmul(
                                A[bw + j][:], oqb[:, hh + m * 512 + tt * 128:
                                             hh + m * 512 + tt * 128 + 128],
                                wdT[:, m * 1024 + j * 512: m * 1024 + (j + 1) * 512],
                                start=(m == 0), stop=(m == FC - 1)),
                                "TD" if (m == FC - 1 and jj == 1) else None)
                    W("vector", "TD", dn_td[(gq, tt)])
                    W("vector", "LXG", lds)
                    DR("vector")
                    DR("scalar")
                    OP("vector", lambda e, bw=bw: e.bn_stats(st6[:, 0:6], A[bw][:]))
                    OP("vector", lambda e, bw=bw: e.bn_stats(st6[:, 6:12], A[bw + 1][:]))
                    DR("vector")
                    OP("vector", lambda e: e.bn_aggr(mvt[:], st6[:]))
                    DR("vector")
                    OP("vector", lambda e: e.tensor_scalar(cw[:, 10:11], mvt[:, 1:2],
                                                           EPS_LN, None, ALU.add))
                    DR("vector")
                    vv = OP("vector", lambda e: e.reciprocal(cw[:, 11:12], cw[:, 10:11]), "VD")
                    W("scalar", "VD", vv)
                    sv = OP("scalar", lambda e: e.activation(cw[:, 12:13], cw[:, 11:12],
                                                             AF.Sqrt), "SD")
                    W("vector", "SD", sv)
                    for half2 in range(2):
                        OP("vector", lambda e, hh=half2, bw=bw: e.tensor_scalar(
                            lnb[:, hh * 512:(hh + 1) * 512], A[bw + hh][:],
                            mvt[:, 0:1], cw[:, 12:13], ALU.subtract, ALU.mult))
                    OP("vector", lambda e, tt=tt, xgh=xgh, sb=(gq % 2) * 8: e.tensor_scalar(
                        x1b[:], x2g[:, xgh + tt * 1024: xgh + (tt + 1) * 1024],
                        sg4b[:, sb + tt * 2:sb + tt * 2 + 1],
                        sg4b[:, sb + tt * 2 + 1:sb + tt * 2 + 2],
                        ALU.mult, ALU.add))
                    DR("vector")
                    oslot = (tt % 2) * 1024
                    prev_o = (gq, tt - 2) if tt >= 2 else (gq - 1, tt + 2)
                    W("vector", "SY%d" % (tt % 2), out_st.get(prev_o, 0))
                    x2g_vd = ln_vd = OP("vector", lambda e, osl=oslot: e.tensor_tensor(
                        outb[:, osl:osl + 1024], lnb[:], x1b[:], ALU.add), "VD")
                    ln_vd_par[(gq, tt)] = ln_vd
                    x2g_vd_par[gq] = ln_vd
                    W("scalar", "VD", ln_vd)
                    tok0 = gq * 512 + tt * 128
                    out_st[(gq, tt)] = dma("scalar", "SY%d" % (tt % 2),
                                           y_d[tok0 // 2048,
                                               tok0 % 2048: tok0 % 2048 + 128, :],
                                           outb[:, oslot:oslot + 1024])
                    if gq + 1 < NG and tt < 2:
                        emit_oq_prep(gq + 1, list(range(tt * 12, tt * 12 + 12)))

        for k, v in dict(cnt).items():
            W("sync", k, v)

        waited = {}
        for e in ENGINES:
            for item in prog[e]:
                if item[0] == "w":
                    waited.setdefault(item[1], set()).add(item[2])
        dma_sems = {s for s in cnt if s not in ("TD", "VD", "SD", "PD", "CC")}
        run_cnt = {}
        for e in ENGINES:
            newops = []
            pend = {}
            for item in prog[e]:
                if item[0] == "o":
                    _, fn, inc, n = item
                    if inc is not None and inc not in dma_sems:
                        v = run_cnt.get(inc, 0) + n
                        run_cnt[inc] = v
                        if v in waited.get(inc, ()):
                            newops.append(("o", fn, inc, n + pend.pop(inc, 0)))
                        else:
                            pend[inc] = pend.get(inc, 0) + n
                            newops.append(("o", fn, None, 0))
                        continue
                newops.append(item)
            assert not pend or all(v == 0 for v in pend.values()) or True
            prog[e] = newops

        def runner(eng_name):
            def run(eng):
                lastw = {}
                for item in prog[eng_name]:
                    if item[0] == "w":
                        if lastw.get(item[1], -1) >= item[2]:
                            continue
                        lastw[item[1]] = item[2]
                        eng.wait_ge(sems[item[1]], item[2])
                    elif item[0] == "d":
                        eng.drain()
                    else:
                        _, fn, inc, n = item
                        instr = fn(eng)
                        if inc is not None:
                            instr.then_inc(sems[inc], n)
            return run

        blk.sync(runner("sync"))
        blk.tensor(runner("tensor"))
        blk.vector(runner("vector"))
        blk.scalar(runner("scalar"))
        blk.gpsimd(runner("gpsimd"))

    return nc


def make_in_maps(inputs, n_cores=8):
    consts = host_constants()
    perm = qk_row_perm()
    wqkv = np.ascontiguousarray(inputs["L0_attn_w_qkv"][perm])
    g8 = np.ascontiguousarray(inputs["L0_attn_g"].reshape(DC, 128).T)
    pm = inputs["L0_attn_pm"]
    pmkf = np.empty((128, 256), dtype=np.float32)
    for h in range(H):
        blkv = pm[0, h].T
        pmkf[0:64, h * 16:(h + 1) * 16] = blkv
        pmkf[64:128, h * 16:(h + 1) * 16] = blkv
    per_head = np.concatenate([np.arange(0, DH, 2), np.arange(1, DH, 2)])
    pmkf_p = np.empty_like(pmkf)
    pmkf_p[0:64] = pmkf[0:64][per_head]
    pmkf_p[64:128] = pmkf[64:128][per_head]
    pmvst = np.zeros((16, 1040), dtype=np.float32)
    for h in range(H):
        pmvst[:, h * 65: h * 65 + 64] = pm[1, h]
        pmvst[:, h * 65 + 64] = 1.0
    seq = inputs["L0_attn_seq"]
    maps = []
    for c in range(n_cores):
        maps.append({
            "seq": np.ascontiguousarray(seq[c * NB:(c + 1) * NB]),
            "wqkv": wqkv,
            "wout": np.ascontiguousarray(inputs["L0_attn_w_out"]),
            "wup": np.ascontiguousarray(inputs["L0_ffn_w_up"]),
            "wgate": np.ascontiguousarray(inputs["L0_ffn_w_gate"]),
            "wdown": np.ascontiguousarray(inputs["L0_ffn_w_down"]),
            "gvec": g8,
            "pmkf": pmkf_p,
            "pmvst": pmvst,
            "identc": consts["ident"],
            "identf": np.eye(128, dtype=np.float32),
            "maskmul": consts["maskmul"],
            "costab": consts["costab"],
            "sintab": consts["sintab"],
            "indsel": consts["indsel"],
        })
    return maps


def kernel(**inputs):
    n_cores = 8
    maps = make_in_maps(inputs, n_cores)
    nc = build_nc(n_cores)
    res = run_bass_kernel_spmd(nc, maps, core_ids=list(range(n_cores)))
    out = np.concatenate([res.results[c]["y"] for c in range(n_cores)], axis=0)
    return out.astype(np.float32)


# version 16 (final)
# speedup vs baseline: 1.1116x; 1.0148x over previous
import contextlib
import numpy as np
import ml_dtypes
import concourse.bass as bass
import concourse.mybir as mybir
import bass_rust
from concourse import library_config
from concourse.bass_utils import run_bass_kernel_spmd

F32 = mybir.dt.float32
BF16 = mybir.dt.bfloat16
AF = mybir.ActivationFunctionType
ALU = mybir.AluOpType
AX = mybir.AxisListType
RED = bass_rust.ReduceOp

NB, NSEG, NT = 4, 4, 4
D, DC, H, DH = 1024, 8, 16, 64
FF, FC = 3072, 24
NTT, NG = 64, 16
MAGIC = float(np.float32(1.5 * 2 ** 23))
EPS_LN = 1e-5
EPS_RMS = 1.1920929e-07
CLIP = 1e12

ENGINES = ("sync", "tensor", "vector", "scalar", "gpsimd")


def host_constants():
    c = {}
    c["ident"] = np.eye(128, dtype=ml_dtypes.bfloat16)
    j = np.arange(128)[:, None]
    i = np.arange(128)[None, :]
    c["maskmul"] = (j <= i).astype(ml_dtypes.bfloat16)
    pos = np.arange(2048, dtype=np.float64)
    inv = 1.0 / 10000.0 ** (np.arange(0, DH, 2, dtype=np.float64) / DH)
    ang = pos[:, None] * inv[None, :]
    c["costab"] = np.cos(ang).astype(ml_dtypes.bfloat16).reshape(16, 128, 32) \
        .transpose(1, 0, 2).reshape(128, 512).copy()
    c["sintab"] = np.sin(ang).astype(ml_dtypes.bfloat16).reshape(16, 128, 32) \
        .transpose(1, 0, 2).reshape(128, 512).copy()
    ind = np.zeros((16, 1024), dtype=ml_dtypes.bfloat16)
    for h in range(16):
        ind[h, (h // 2) * 128 + (h % 2) * 64: (h // 2) * 128 + (h % 2) * 64 + 64] = 1.0
    c["indsel"] = ind
    return c


def qk_row_perm():
    per_head = np.concatenate([np.arange(0, DH, 2), np.arange(1, DH, 2)])
    one = np.concatenate([h * DH + per_head for h in range(H)])
    return np.concatenate([one, 1024 + one, 2048 + np.arange(1024)])


def build_nc(n_cores=8, max_phase=7, stub_cc=False, p2opts='dmae'):
    nc = bass.Bass("TRN2", target_bir_lowering=False)

    seq_d = nc.dram_tensor("seq", [NB, 2048, D], F32, kind="ExternalInput").ap()
    wqkv_d = nc.dram_tensor("wqkv", [3 * D, D], F32, kind="ExternalInput").ap()
    wout_d = nc.dram_tensor("wout", [D, D], F32, kind="ExternalInput").ap()
    wup_d = nc.dram_tensor("wup", [FF, D], F32, kind="ExternalInput").ap()
    wgate_d = nc.dram_tensor("wgate", [FF, D], F32, kind="ExternalInput").ap()
    wdown_d = nc.dram_tensor("wdown", [D, FF], F32, kind="ExternalInput").ap()
    g_d = nc.dram_tensor("gvec", [128, DC], F32, kind="ExternalInput").ap()
    pmkf_d = nc.dram_tensor("pmkf", [128, 256], F32, kind="ExternalInput").ap()
    pmvst_d = nc.dram_tensor("pmvst", [16, 1040], F32, kind="ExternalInput").ap()
    ident_d = nc.dram_tensor("identc", [128, 128], BF16, kind="ExternalInput").ap()
    identf_d = nc.dram_tensor("identf", [128, 128], F32, kind="ExternalInput").ap()
    mask_d = nc.dram_tensor("maskmul", [128, 128], BF16, kind="ExternalInput").ap()
    cos_d = nc.dram_tensor("costab", [128, 512], BF16, kind="ExternalInput").ap()
    sin_d = nc.dram_tensor("sintab", [128, 512], BF16, kind="ExternalInput").ap()
    ind_d = nc.dram_tensor("indsel", [16, 1024], BF16, kind="ExternalInput").ap()
    y_d = nc.dram_tensor("y", [NB, 2048, D], F32, kind="ExternalOutput").ap()

    x2_d = nc.dram_tensor("X2s", [NTT, 128, D], F32).ap()
    sig_d = nc.dram_tensor("SIGs", [NG, 128, 8], F32).ap()
    u_d = nc.dram_tensor("Usc", [FC, NG, 128, 512], BF16).ap()
    gg_d = nc.dram_tensor("Gsc", [FC, NG, 128, 512], BF16).ap()
    t_d = nc.dram_tensor("Tsc", [FC, NG, 128, 512], BF16).ap()
    o_d = nc.dram_tensor("Osc", [FC, NG, 128, 512], BF16).ap()
    cci = [nc.dram_tensor(f"cci{k}", [1, 8], F32).ap() for k in range(5)]
    cco = [nc.dram_tensor(f"cco{k}", [1, 8], F32).ap() for k in range(5)]
    rs_d = nc.dram_tensor("RSb", [16, 512], F32).ap()

    prog = {e: [] for e in ENGINES}
    cnt = {}

    def W(eng, semname, val):
        if val and val > 0:
            prog[eng].append(("w", semname, val))

    def OP(eng, fn, inc=None, n=1):
        tot = None
        if inc is not None:
            cnt[inc] = cnt.get(inc, 0) + n
            tot = cnt[inc]
        prog[eng].append(("o", fn, inc, n))
        return tot

    def DR(eng):
        prog[eng].append(("d",))

    def CV(semname):
        return cnt.get(semname, 0)

    ctx = contextlib.ExitStack()
    sbt = lambda name, shape, dt: ctx.enter_context(nc.sbuf_tensor(name, shape, dt))
    pst = lambda name, shape, dt: ctx.enter_context(nc.psum_tensor(name, shape, dt))

    with ctx:
        ident = sbt("identS", [128, 128], BF16)
        identf = sbt("identfS", [128, 128], F32)
        onesrow = sbt("onesrowS", [1, 128], F32)
        mcol8 = sbt("mcol8S", [1, 8], F32)
        mg8 = sbt("mg8S", [1, 8], F32)
        mask = sbt("maskS", [128, 128], BF16)
        indsel = sbt("indselS", [16, 1024], BF16)
        gsb = sbt("gsbS", [128, DC], F32)
        costab = sbt("costabS", [128, 512], BF16)
        sintab = sbt("sintabS", [128, 512], BF16)
        mx = sbt("mxS", [128, 1024], F32)
        MX_X, MX_U, MX_G, MX_T, MX_O = 0, 64, 448, 832, 928
        scl = sbt("sclS", [128, 16], F32)
        SXI, SX, SUI, SU, SGI, SG, SGQ, STI, STU, SOI, SO = range(11)
        cw = sbt("cwS", [128, 64], F32)
        st6 = sbt("st6S", [128, 12], F32)
        mvt = sbt("mvtS", [128, 2], F32)
        mvt4 = sbt("mvt4S", [128, 8], F32)
        sgst = sbt("sgstS", [128, 8], F32)
        cmag = sbt("cmagS", [128, 2], F32)

        A = [pst(f"psA{i}", [128, 512], F32) for i in range(6)]
        Bp = pst("psB", [128, 1024], BF16)
        Cp = pst("psC", [128, 512], F32)

        sems = {}
        for s in ("LC", "LW0", "LW1", "LW2", "LSEQ", "LX0", "LX1", "LWS0", "LWS1",
                  "LWD", "LG0", "LG1", "LU0", "LU1", "LXG", "LRS", "LRS0", "LRS1",
                  "LO0", "LO1", "LO2", "LO3", "LO4", "LO5", "LO6", "LO7", "GR0", "GR1",
                  "GD", "TD", "VD", "SD", "PD", "CC",
                  "SX0", "SX1", "SSG", "SG0", "SG1", "SG2", "SG3",
                  "STB0", "STB1", "SY0", "SY1"):
            sems[s] = ctx.enter_context(nc.semaphore(s))

        p1 = contextlib.ExitStack()
        s1 = lambda name, shape, dt: p1.enter_context(nc.sbuf_tensor(name, shape, dt))
        wqkvT = s1("wqkvT", [128, 24576], BF16)
        woutT = s1("woutT", [128, 8192], BF16)
        pmkf = s1("pmkfS", [128, 256], BF16)
        pmvst = s1("pmvstS", [16, 1040], BF16)
        pstage = contextlib.ExitStack()
        wldf = pstage.enter_context(nc.sbuf_tensor("wldf", [128, 3072], F32))
        wcst = pstage.enter_context(nc.sbuf_tensor("wcst", [128, 2048], BF16))
        pstage.close()
        seqsb = s1("seqsb", [128, 8192], F32)
        zbf = s1("zbf", [128, 4096], BF16)
        zfm = s1("zfm", [128, 4096], BF16)
        qksb = s1("qksb", [128, 8192], BF16)
        qfm = s1("qfm", [128, 4096], BF16)
        kfm = s1("kfm", [128, 4096], BF16)
        vst = s1("vst", [128, 4160], BF16)
        expt = s1("expt", [128, 3584], BF16)
        aoun = s1("aoun", [128, 4096], BF16)
        rsum = s1("rsum", [16, 512], BF16)
        rsum32 = s1("rsum32", [16, 512], F32)
        sumsb = s1("sumsb", [1, 1024], F32)
        rxp = s1("rxp", [128, 4096], BF16)
        x1t = s1("x1t", [128, 1024], F32)
        x2t = s1("x2t", [128, 1024], F32)
        rta = s1("rta", [128, 256], BF16)
        qke = s1("qke", [128, 2048], BF16)
        rtb = s1("rtb", [128, 256], BF16)
        p1.close()

        p2 = contextlib.ExitStack()
        s2 = lambda name, shape, dt: p2.enter_context(nc.sbuf_tensor(name, shape, dt))
        xqfm = s2("xqfm", [128, 65536], BF16)
        wstr = s2("wstr", [128, 2048], F32)
        mxb32 = s2("mxb32", [128, 32], F32)
        rmxb = s2("rmxb", [128, 32], F32)
        sinvb = s2("sinvb", [128, 32], F32)
        sblk = s2("sblkS", [128, 32], F32)
        wq1 = s2("wq1", [128, 1024], F32)
        wr2 = s2("wr2", [128, 1024], F32)
        wdqb = s2("wdqb", [128, 1024], BF16)
        wtr = s2("wtrS", [128, 2048], BF16)
        stg = s2("stgS", [128, 2048], BF16)
        p2a = contextlib.ExitStack()
        xin = p2a.enter_context(nc.sbuf_tensor("xinS", [128, 2048], F32))
        xt1 = p2a.enter_context(nc.sbuf_tensor("xt1S", [128, 1024], F32))
        xr2 = p2a.enter_context(nc.sbuf_tensor("xr2S", [128, 1024], F32))
        xqtm = p2a.enter_context(nc.sbuf_tensor("xqtm", [128, 1024], BF16))
        p2a.close()
        p2b = contextlib.ExitStack()
        wdnin = p2b.enter_context(nc.sbuf_tensor("wdnin", [128, 1536], F32))
        wdn1 = p2b.enter_context(nc.sbuf_tensor("wdn1", [128, 1536], F32))
        wdnr = p2b.enter_context(nc.sbuf_tensor("wdnr", [128, 1536], F32))
        wdnb = p2b.enter_context(nc.sbuf_tensor("wdnb", [128, 1536], BF16))
        mxb48 = p2b.enter_context(nc.sbuf_tensor("mxb48", [128, 48], F32))
        rmx48 = p2b.enter_context(nc.sbuf_tensor("rmx48", [128, 48], F32))
        siv48 = p2b.enter_context(nc.sbuf_tensor("siv48", [128, 48], F32))
        sbk48 = p2b.enter_context(nc.sbuf_tensor("sbk48", [128, 48], F32))
        p2b.close()
        p2.close()

        p2c = contextlib.ExitStack()
        s2c = lambda name, shape, dt: p2c.enter_context(nc.sbuf_tensor(name, shape, dt))
        wdT = s2c("wdT", [128, 24576], BF16)
        gin = s2c("ginS", [128, 4096], BF16)
        uin = s2c("uinS", [128, 4096], BF16)
        p34 = contextlib.ExitStack()
        s34 = lambda name, shape, dt: p34.enter_context(nc.sbuf_tensor(name, shape, dt))
        tb1 = s34("tb1S", [128, 4096], F32)
        rb3 = s34("rb3S", [128, 4096], F32)
        sigb = s34("sigbS", [128, 4096], F32)
        a3b = s34("a3bS", [128, 4096], F32)
        c2b = s34("c2bS", [128, 4096], F32)
        tob = s34("tobS", [128, 4096], BF16)
        p34.close()
        p4b = contextlib.ExitStack()
        s4b = lambda name, shape, dt: p4b.enter_context(nc.sbuf_tensor(name, shape, dt))
        oqb = s4b("oqbS", [128, 24576], BF16)
        x2g = s4b("x2gS", [128, 8192], F32)
        lnb = s4b("lnbS", [128, 1024], F32)
        x1b = s4b("x1bS", [128, 1024], F32)
        outb = s4b("outbS", [128, 2048], F32)
        ot1 = s4b("ot1S", [128, 512], F32)
        ot2 = s4b("ot2S", [128, 1024], F32)
        sg4b = s4b("sg4bS", [128, 16], F32)
        p4b.close()
        p2c.close()

        blk = ctx.enter_context(nc.Block())

        def dma(eng, sem, dst, src):
            return OP(eng, lambda e, d=dst, s=src: e.dma_start(out=d, in_=s), sem, 16)

        ld_const = 0
        for dst, srcap in ((ident[:], ident_d), (identf[:], identf_d),
                           (mask[:], mask_d), (indsel[:], ind_d),
                           (gsb[:], g_d),
                           (costab[:], cos_d), (sintab[:], sin_d)):
            ld_const = dma("sync", "LC", dst, srcap)
        W("tensor", "LC", ld_const)
        W("vector", "LC", ld_const)
        W("scalar", "LC", ld_const)
        gld_pm = dma("gpsimd", "GD", pmkf[:], pmkf_d)
        gld_pm = dma("gpsimd", "GD", pmvst[:], pmvst_d)
        OP("vector", lambda e: e.memset(onesrow[:], 1.0))
        OP("vector", lambda e: e.memset(mcol8[:], 0.0))
        OP("vector", lambda e: e.memset(cmag[:, 0:1], MAGIC))
        OP("vector", lambda e: e.memset(cmag[:, 1:2], -MAGIC))
        vst_ones = OP("vector", lambda e: e.memset(
            vst[:].rearrange("p (t h c) -> p t h c", h=16, c=65)[:, :, :, 64:65], 1.0), "VD")

        vcast_done = {}
        ttr_done = {}
        evac_done = {}
        prep_jobs = [("qkv", m) for m in range(24)] + [("out", m) for m in range(8)]
        for gi, (kind, m) in enumerate(prep_jobs):
            lslot = (gi % 3) * 1024
            cslot = (gi % 2) * 1024
            if gi >= 3:
                W("sync", "VD", vcast_done[gi - 3])
            src_rows = (wqkv_d if kind == "qkv" else wout_d)[m * 128:(m + 1) * 128, :]
            lt = dma("sync", "LW%d" % (gi % 3), wldf[:, lslot:lslot + 1024], src_rows)
            W("vector", "LW%d" % (gi % 3), lt)
            if gi >= 2:
                W("vector", "TD", ttr_done[gi - 2])
            vcast_done[gi] = OP("vector", lambda e, ls=lslot, cs=cslot: e.tensor_copy(
                wcst[:, cs:cs + 1024], wldf[:, ls:ls + 1024]), "VD")
            W("tensor", "VD", vcast_done[gi])
            if gi >= 1:
                W("tensor", "SD", evac_done[gi - 1])
            for k in range(DC):
                ttr_done[gi] = OP("tensor", lambda e, cs=cslot, kk=k: e.transpose(
                    Bp[:, kk * 128:kk * 128 + 128],
                    wcst[:, cs + kk * 128:cs + kk * 128 + 128], ident[:]),
                    "TD" if k == DC - 1 else None)
            W("scalar", "TD", ttr_done[gi])
            if kind == "qkv":
                for k in range(DC):
                    evac_done[gi] = OP("scalar", lambda e, mm=m, kk=k: e.activation(
                        wqkvT[:, kk * 3072 + mm * 128: kk * 3072 + mm * 128 + 128],
                        Bp[:, kk * 128:kk * 128 + 128], AF.Identity,
                        scale=gsb[:, kk:kk + 1]), "SD" if k == DC - 1 else None)
            else:
                evac_done[gi] = OP("scalar", lambda e, mm=m: e.copy(
                    woutT[:].rearrange("p (k x) -> p k x", x=1024)
                    [:, :, mm * 128:mm * 128 + 128],
                    Bp[:].rearrange("p (k x) -> p k x", x=128)), "SD")
        bp_free_sd = evac_done[len(prep_jobs) - 1]


        if max_phase >= 1:
            rope_done = {}
            rope_bank = {}
            qke_sd = {}
            vevac_done = {}
            zfm_done = {}
            qkv_done = {}
            exp_done = {}
            mask_done = {}
            av_done = {}
            x2_store_st = {}
            sg_store_st = 0
            woutln_vd = 0
            norm_vd = 0
            rec_done = 0
            ao_done = 0
            prev_norm_vd = 0
            last_rmm_td = 0
            rs_row_st = {}
            rsum32_vd = 0

            z_done_map = {}
            ld_seq_of = {}
            seq_par_read = {0: 0, 1: 0}

            def emit_AB(un):
                    bb, ss = un // 4, un % 4
                    sh = (un % 2) * 4096
                    if un == 0:
                        W("sync", "VD", CV("VD"))
                        W("sync", "TD", CV("TD"))
                    W("sync", "VD", seq_par_read[un % 2])
                    ld_seq = dma("sync", "LSEQ",
                                 seqsb[:, sh:sh + 4096].rearrange("p (t d) -> p t d", d=1024),
                                 seq_d[bb, ss * 512:(ss + 1) * 512, :]
                                 .rearrange("(t p) d -> p t d", p=128))
                    ld_seq_of[un] = ld_seq
                    if un:
                        W("vector", "SD", z_done_map[(un - 1, NT - 1)])
                        DR("vector")
                        DR("scalar")
                    bv1 = None
                    for t in range(NT):
                        c0 = 32 + t * 8
                        sq = seqsb[:, sh + t * 1024: sh + (t + 1) * 1024]
                        W("vector", "LSEQ", ld_seq)
                        if t:
                            DR("vector")
                        OP("vector", lambda e, sq=sq: e.bn_stats(st6[:, 0:6], sq[:, 0:512]))
                        OP("vector", lambda e, sq=sq: e.bn_stats(st6[:, 6:12], sq[:, 512:1024]))
                        DR("vector")
                        OP("vector", lambda e, t=t: e.bn_aggr(mvt4[:, 2 * t:2 * t + 2], st6[:]))
                        DR("vector")
                        OP("vector", lambda e, t=t, c0=c0: e.tensor_scalar(
                            cw[:, c0:c0 + 1], mvt4[:, 2 * t + 1:2 * t + 2],
                            EPS_LN, None, ALU.add))
                        DR("vector")
                        OP("vector", lambda e, c0=c0: e.reciprocal(cw[:, c0 + 1:c0 + 2],
                                                                   cw[:, c0:c0 + 1]))
                        DR("vector")
                        OP("vector", lambda e, t=t, c0=c0: e.tensor_tensor(
                            cw[:, c0 + 4:c0 + 5], mvt4[:, 2 * t + 1:2 * t + 2],
                            cw[:, c0 + 1:c0 + 2], ALU.mult))
                        DR("vector")
                        OP("vector", lambda e, c0=c0: e.tensor_scalar(
                            cw[:, c0 + 5:c0 + 6], cw[:, c0 + 4:c0 + 5],
                            EPS_RMS, None, ALU.add))
                        DR("vector")
                        bv1 = OP("vector", lambda e, c0=c0: e.reciprocal(
                            cw[:, c0 + 6:c0 + 7], cw[:, c0 + 5:c0 + 6]),
                            "VD" if t == NT - 1 else None)
                    W("scalar", "VD", bv1)
                    for t in range(NT):
                        c0 = 32 + t * 8
                        bs1 = OP("scalar", lambda e, c0=c0: e.activation(
                            cw[:, c0 + 2:c0 + 3], cw[:, c0 + 1:c0 + 2], AF.Sqrt))
                        bs1 = OP("scalar", lambda e, c0=c0: e.activation(
                            cw[:, c0 + 7:c0 + 8], cw[:, c0 + 6:c0 + 7], AF.Sqrt),
                            "SD" if t == NT - 1 else None)
                    W("vector", "SD", bs1)
                    bv2 = None
                    for t in range(NT):
                        c0 = 32 + t * 8
                        OP("vector", lambda e, c0=c0: e.tensor_tensor(
                            cw[:, c0 + 3:c0 + 4], cw[:, c0 + 2:c0 + 3],
                            cw[:, c0 + 7:c0 + 8], ALU.mult))
                        DR("vector")
                        bv2 = OP("vector", lambda e, t=t, c0=c0: e.tensor_scalar(
                            cw[:, c0 + 4:c0 + 5], mvt4[:, 2 * t:2 * t + 1],
                            cw[:, c0 + 3:c0 + 4], -1.0, ALU.mult, ALU.mult),
                            "VD" if t == NT - 1 else None)
                        if t != NT - 1:
                            DR("vector")
                    W("scalar", "VD", bv2)
                    for t in range(NT):
                        c0 = 32 + t * 8
                        sq = seqsb[:, sh + t * 1024: sh + (t + 1) * 1024]
                        if un and t == 0:
                            W("scalar", "TD", qkv_done[(un - 1, NT - 1)])
                        z_done_map[(un, t)] = OP("scalar", lambda e, t=t, sq=sq, c0=c0: e.activation(
                            zbf[:, t * 1024:(t + 1) * 1024], sq, AF.Identity,
                            scale=cw[:, c0 + 3:c0 + 4], bias=cw[:, c0 + 4:c0 + 5]), "SD")

            emit_AB(0)
            for u in range(16):
                b, s = u // 4, u % 4
                ld_seq = ld_seq_of[u]
                for t in range(NT):
                    gt = s * 4 + t
                    W("tensor", "SD", z_done_map[(u, t)])
                    W("tensor", "SD", bp_free_sd)
                    if u and t == 0:
                        W("tensor", "TD", qkv_done[(u - 1, NT - 1)])
                    for k in range(DC):
                        ztr = OP("tensor", lambda e, t=t, kk=k: e.transpose(
                            Bp[:, kk * 128:kk * 128 + 128],
                            zbf[:, t * 1024 + kk * 128: t * 1024 + kk * 128 + 128],
                            ident[:]), "TD" if k == DC - 1 else None)
                    W("scalar", "TD", ztr)
                    zfm_done[t] = OP("scalar", lambda e, t=t: e.copy(
                        zfm[:].rearrange("p (k x) -> p k x", x=512)[:, :, t * 128:t * 128 + 128],
                        Bp[:].rearrange("p (k x) -> p k x", x=128)), "SD")
                    bp_free_sd = zfm_done[t]
                    W("tensor", "SD", zfm_done[t])
                    pt = (u, t - 1) if t else ((u - 1, NT - 1) if u else None)
                    if u and t == 0:
                        W("tensor", "VD", woutln_vd)
                    for k in range(DC):
                        for jj in range(6):
                            if k == 0 and pt is not None:
                                if jj < 4:
                                    W("tensor", "SD", qke_sd[(pt[0], pt[1], jj)])
                                elif jj == 4:
                                    W("tensor", "SD", vevac_done[pt])
                            qkv_done[(u, t)] = OP("tensor", lambda e, t=t, kk=k, j=jj: e.matmul(
                                A[j][:],
                                zfm[:, kk * 512 + t * 128: kk * 512 + t * 128 + 128],
                                wqkvT[:, kk * 3072 + j * 512: kk * 3072 + (j + 1) * 512],
                                start=(kk == 0), stop=(kk == DC - 1)),
                                "TD" if (k == DC - 1 and jj == 5) else None)
                    W("scalar", "TD", qkv_done[(u, t)])
                    for bank in range(4):
                        if (u, t) != (0, 0):
                            W("scalar", "VD", rope_bank.get(
                                (u, t - 1, bank) if t else (u - 1, NT - 1, bank), 0))
                        qke_sd[(u, t, bank)] = OP("scalar", lambda e, b=bank: e.copy(
                            qke[:, b * 512:(b + 1) * 512], A[b][:]), "SD")
                    if u or t:
                        DR("vector")
                    for bank in range(4):
                        base = t * 2048 + bank * 512
                        W("vector", "SD", qke_sd[(u, t, bank)])
                        Eap = qke[:, bank * 512:(bank + 1) * 512].rearrange(
                            "p (h x) -> p h x", x=64)[:, :, 0:32]
                        Oap = qke[:, bank * 512:(bank + 1) * 512].rearrange(
                            "p (h x) -> p h x", x=64)[:, :, 32:64]
                        cosap = costab[:, gt * 32:(gt + 1) * 32] \
                            .rearrange("p (o x) -> p o x", o=1).broadcast_to([128, 8, 32])
                        sinap = sintab[:, gt * 32:(gt + 1) * 32] \
                            .rearrange("p (o x) -> p o x", o=1).broadcast_to([128, 8, 32])
                        r1v = rta[:].rearrange("p (h x) -> p h x", x=32)
                        r2v = rtb[:].rearrange("p (h x) -> p h x", x=32)
                        outE = qksb[:, base:base + 512].rearrange(
                            "p (h x) -> p h x", x=64)[:, :, 0:32]
                        outO = qksb[:, base:base + 512].rearrange(
                            "p (h x) -> p h x", x=64)[:, :, 32:64]
                        OP("vector", lambda e, a=Eap, c=cosap, o=r1v: e.tensor_tensor(o, a, c, ALU.mult))
                        OP("vector", lambda e, a=Oap, c=sinap, o=r2v: e.tensor_tensor(o, a, c, ALU.mult))
                        DR("vector")
                        OP("vector", lambda e, o=outE, x=r1v, y=r2v: e.tensor_tensor(o, x, y, ALU.subtract))
                        DR("vector")
                        OP("vector", lambda e, a=Oap, c=cosap, o=r1v: e.tensor_tensor(o, a, c, ALU.mult))
                        OP("vector", lambda e, a=Eap, c=sinap, o=r2v: e.tensor_tensor(o, a, c, ALU.mult))
                        DR("vector")
                        rope_done[(u, t)] = OP("vector", lambda e, o=outO, x=r1v, y=r2v:
                                               e.tensor_tensor(o, x, y, ALU.add), "VD")
                        rope_bank[(u, t, bank)] = rope_done[(u, t)]
                        if bank != 3:
                            DR("vector")
                    W("scalar", "TD", qkv_done[(u, t)])
                    for half in range(2):
                        dstv = vst[:, t * 1040 + half * 8 * 65: t * 1040 + (half * 8 + 8) * 65] \
                            .rearrange("p (h c) -> p h c", c=65)[:, :, 0:64]
                        vevac_done[(u, t)] = OP("scalar", lambda e, hh=half, dv=dstv: e.copy(
                            dv, A[4 + hh][:].rearrange("p (h x) -> p h x", x=64)),
                            "SD" if half == 1 else None)
                qkfm_sd = None
                for t in range(NT):
                    for part, dst in ((0, qfm), (1, kfm)):
                        W("tensor", "VD", rope_done[(u, t)])
                        W("tensor", "SD", bp_free_sd)
                        for c in range(DC):
                            ftr = OP("tensor", lambda e, t=t, pp=part, cc=c: e.transpose(
                                Bp[:, cc * 128:cc * 128 + 128],
                                qksb[:, t * 2048 + pp * 1024 + cc * 128:
                                     t * 2048 + pp * 1024 + cc * 128 + 128],
                                ident[:]), "TD" if c == DC - 1 else None)
                        W("scalar", "TD", ftr)
                        qkfm_sd = OP("scalar", lambda e, t=t, dd=dst: e.copy(
                            dd[:].rearrange("p (c x) -> p c x", x=512)[:, :, t * 128:t * 128 + 128],
                            Bp[:].rearrange("p (c x) -> p c x", x=128)), "SD")
                        bp_free_sd = qkfm_sd
                sc_all = {}
                for h in range(H + 1):
                    if h < H:
                        par, ch, pb = h % 2, h // 2, (h % 2) * 64
                        qc = ch * 512
                        _off = [0, 512, 896, 1152, 1280]
                        ebs = [par * 1792 + _off[jc] for jc in range(5)]
                        sc_done = {}
                        if h >= 1:
                            for jc in range(5):
                                W("tensor", "SD", exp_done[(u, h - 1, jc)])
                        else:
                            W("tensor", "SD", qkfm_sd)
                            if u == 0:
                                W("tensor", "GD", gld_pm)
                        sc_done[4] = OP("tensor", lambda e, pb=pb, h=h, qc=qc: e.matmul(
                            A[4][0:16, :], pmkf[pb:pb + 64, h * 16:(h + 1) * 16],
                            qfm[pb:pb + 64, qc:qc + 512], start=True, stop=True), "TD")
                        for jc in range(4):
                            sc_done[jc] = OP("tensor", lambda e, pb=pb, qc=qc, jc=jc: e.matmul(
                                A[jc][:, jc * 128:512],
                                kfm[pb:pb + 64, qc + jc * 128: qc + jc * 128 + 128],
                                qfm[pb:pb + 64, qc + jc * 128: qc + 512],
                                start=True, stop=True), "TD")
                        sc_all[h] = sc_done
                        for jc in (4, 0, 1, 2, 3):
                            W("scalar", "TD", sc_done[jc])
                            if h >= 2:
                                W("scalar", "TD", av_done[(u, h - 2)])
                                if jc < 4:
                                    W("scalar", "VD", mask_done[(u, h - 2, jc)])
                            elif u and h < 2:
                                W("scalar", "TD", av_done[(u - 1, h + 14)])
                                if jc < 4:
                                    W("scalar", "VD", mask_done[(u - 1, h + 14, jc)])
                            if jc == 4:
                                exp_done[(u, h, 4)] = OP("scalar", lambda e, b4=ebs[4]: e.activation(
                                    expt[0:16, b4:b4 + 512], A[4][0:16, :],
                                    AF.Exp, scale=0.125), "SD")
                            else:
                                wdt = 512 - jc * 128
                                exp_done[(u, h, jc)] = OP("scalar",
                                    lambda e, jc=jc, bj=ebs[jc], w=wdt: e.activation(
                                        expt[:, bj: bj + w],
                                        A[jc][:, jc * 128:512], AF.Exp, scale=0.125), "SD")
                        for jc in range(4):
                            W("vector", "SD", exp_done[(u, h, jc)])
                            dg = expt[:, ebs[jc]: ebs[jc] + 128]
                            mask_done[(u, h, jc)] = OP("vector", lambda e, d=dg: e.tensor_tensor(
                                d, d, mask[:], ALU.mult), "VD")
                    if h >= 1:
                        hh2 = h - 1
                        par2 = hh2 % 2
                        _off = [0, 512, 896, 1152, 1280]
                        eb2 = [par2 * 1792 + _off[jc] for jc in range(5)]
                        W("tensor", "SD", exp_done[(u, hh2, 4)])
                        for jc in range(4):
                            W("tensor", "SD", exp_done[(u, hh2, jc)])
                            W("tensor", "VD", mask_done[(u, hh2, jc)])
                        if hh2 >= 1 or u:
                            W("tensor", "SD", rec_done)
                            W("tensor", "SD", ao_done)
                        OP("tensor", lambda e, h=hh2, b4=eb2[4]: e.matmul(
                            A[5][0:65, :], pmvst[0:16, h * 65:(h + 1) * 65],
                            expt[0:16, b4:b4 + 512], start=True, stop=False))
                        for jc in range(4):
                            wdt = 512 - jc * 128
                            av_done[(u, hh2)] = OP("tensor",
                                lambda e, h=hh2, jc=jc, bj=eb2[jc], w=wdt: e.matmul(
                                    A[5][0:65, jc * 128:512],
                                    vst[:, jc * 1040 + h * 65: jc * 1040 + h * 65 + 65],
                                    expt[:, bj: bj + w],
                                    start=False, stop=(jc == 3)), "TD" if jc == 3 else None)
                        W("scalar", "TD", av_done[(u, hh2)])
                        if hh2 >= 2:
                            W("scalar", "GR%d" % (hh2 % 2), rs_row_st[(u, hh2 - 2)])
                        elif u and hh2 < 2:
                            W("scalar", "GR%d" % (hh2 % 2), rs_row_st[(u - 1, hh2 + 14)])
                        rec_done = OP("scalar", lambda e, h=hh2: e.copy(
                            sumsb[0:1, (h % 2) * 512:(h % 2) * 512 + 512],
                            A[5][64:65, :]), "SD")
                        W("gpsimd", "SD", rec_done)
                        rs_row_st[(u, hh2)] = dma("gpsimd", "GR%d" % (hh2 % 2),
                                                  rs_d[hh2:hh2 + 1, :],
                                                  sumsb[0:1, (hh2 % 2) * 512:(hh2 % 2) * 512 + 512])
                        if hh2 == 0:
                            W("scalar", "VD", prev_norm_vd)
                        ao_done = OP("scalar", lambda e, h=hh2, pb2=(hh2 % 2) * 64, ch2=hh2 // 2: e.copy(
                            aoun[pb2:pb2 + 64, ch2 * 512:(ch2 + 1) * 512], A[5][0:64, :]), "SD")
                rxp_vd = {}
                if u >= 1:
                    W("sync", "TD", last_rmm_td)
                W("sync", "GR0", CV("GR0"))
                W("sync", "GR1", CV("GR1"))
                if u >= 1:
                    W("sync", "VD", rsum32_vd)
                ld_rs = dma("sync", "LRS", rsum32[:], rs_d)
                W("vector", "LRS", ld_rs)
                DR("vector")

                if u >= 1:
                    W("vector", "TD", last_rmm_td)

                def _recb(e):
                    with nc.allow_low_precision(reason="softmax normalizer as bf16 matmul operand"):
                        return e.reciprocal(rsum[:], rsum32[:])

                rsum32_vd = OP("vector", _recb, "VD")
                for p in range(DC):
                    W("tensor", "VD", rsum32_vd)
                    if p >= 4:
                        W("tensor", "VD", rxp_vd[p - 4])
                    elif u and p == 0:
                        W("tensor", "VD", woutln_vd)
                    rtd = OP("tensor", lambda e, p=p: e.matmul(
                        A[p % 4][:], indsel[0:16, p * 128:(p + 1) * 128], rsum[0:16, :],
                        start=True, stop=True), "TD")
                    last_rmm_td = rtd
                    W("vector", "TD", rtd)
                    rxp_vd[p] = OP("vector", lambda e, p=p: e.tensor_copy(
                        rxp[:, p * 512:(p + 1) * 512], A[p % 4][:]), "VD")
                DR("vector")
                W("vector", "SD", ao_done)
                for p in range(DC):
                    norm_vd = OP("vector", lambda e, p=p: e.tensor_tensor(
                        aoun[:, p * 512:(p + 1) * 512], aoun[:, p * 512:(p + 1) * 512],
                        rxp[:, p * 512:(p + 1) * 512], ALU.mult), "VD")
                prev_norm_vd = norm_vd
                if u + 1 < 16:
                    emit_AB(u + 1)
                wout_vd_par = {}
                for t in range(NT):
                    bw = (t % 2) * 2
                    W("tensor", "VD", norm_vd)
                    if t >= 2:
                        W("tensor", "VD", wout_vd_par[t - 2])
                    for k in range(DC):
                        for jj in range(2):
                            wmm = OP("tensor", lambda e, t=t, kk=k, j=jj, bw=bw: e.matmul(
                                A[bw + j][:],
                                aoun[:, kk * 512 + t * 128: kk * 512 + t * 128 + 128],
                                woutT[:, kk * 1024 + j * 512: kk * 1024 + (j + 1) * 512],
                                start=(kk == 0), stop=(kk == DC - 1)),
                                "TD" if (k == DC - 1 and jj == 1) else None)
                    W("vector", "TD", wmm)
                    DR("vector")
                    DR("scalar")
                    OP("vector", lambda e, bw=bw: e.bn_stats(st6[:, 0:6], A[bw][:]))
                    OP("vector", lambda e, bw=bw: e.bn_stats(st6[:, 6:12], A[bw + 1][:]))
                    DR("vector")
                    OP("vector", lambda e: e.bn_aggr(mvt[:], st6[:]))
                    DR("vector")
                    OP("vector", lambda e: e.tensor_scalar(cw[:, 10:11], mvt[:, 1:2],
                                                           EPS_LN, None, ALU.add))
                    DR("vector")
                    vv = OP("vector", lambda e: e.reciprocal(cw[:, 11:12], cw[:, 10:11]), "VD")
                    W("scalar", "VD", vv)
                    sv = OP("scalar", lambda e: e.activation(cw[:, 12:13], cw[:, 11:12], AF.Sqrt), "SD")
                    W("vector", "SD", sv)
                    for half in range(2):
                        OP("vector", lambda e, hh=half, bw=bw: e.tensor_scalar(
                            x1t[:, hh * 512:(hh + 1) * 512], A[bw + hh][:],
                            mvt[:, 0:1], cw[:, 12:13], ALU.subtract, ALU.mult))
                    DR("vector")
                    OP("vector", lambda e, t=t, shu=(u % 2) * 4096: e.tensor_tensor(
                        x1t[:], x1t[:],
                        seqsb[:, shu + t * 1024: shu + (t + 1) * 1024], ALU.add))
                    DR("vector")
                    OP("vector", lambda e: e.bn_stats(st6[:, 0:6], x1t[:, 0:512]))
                    OP("vector", lambda e: e.bn_stats(st6[:, 6:12], x1t[:, 512:1024]))
                    DR("vector")
                    OP("vector", lambda e: e.bn_aggr(mvt[:], st6[:]))
                    DR("vector")
                    OP("vector", lambda e: e.tensor_scalar(cw[:, 14:15], mvt[:, 1:2],
                                                           EPS_LN, None, ALU.add))
                    DR("vector")
                    vv = OP("vector", lambda e: e.reciprocal(cw[:, 15:16], cw[:, 14:15]), "VD")
                    seq_par_read[u % 2] = vv
                    W("scalar", "VD", vv)
                    sv = OP("scalar", lambda e: e.activation(cw[:, 16:17], cw[:, 15:16], AF.Sqrt))
                    sv = OP("scalar", lambda e: e.activation(cw[:, 17:18], cw[:, 14:15], AF.Sqrt), "SD", 1)
                    W("vector", "SD", sv)
                    slot = 0
                    if (u * 4 + t - 1) >= 0:
                        W("vector", "SX%d" % ((t - 1) % 2), x2_store_st.get(u * 4 + t - 1, 0))
                    OP("vector", lambda e, sl=slot: e.tensor_scalar(
                        x2t[:, sl:sl + 1024], x1t[:], mvt[:, 0:1], cw[:, 16:17],
                        ALU.subtract, ALU.mult))
                    DR("vector")
                    OP("vector", lambda e, u=u, t=t, sl=slot: e.tensor_reduce(
                        mx[:, MX_X + u * 4 + t: MX_X + u * 4 + t + 1],
                        x2t[:, sl:sl + 1024], axis=AX.X, op=ALU.max,
                        apply_absolute_value=True))
                    OP("vector", lambda e, t=t: e.tensor_copy(sgst[:, t * 2:t * 2 + 1], cw[:, 17:18]))
                    woutln_vd = OP("vector", lambda e, t=t: e.tensor_copy(
                        sgst[:, t * 2 + 1:t * 2 + 2], mvt[:, 0:1]), "VD")
                    wout_vd_par[t] = woutln_vd
                    W("scalar", "VD", woutln_vd)
                    x2_store_st[u * 4 + t] = dma("scalar", "SX%d" % (t % 2),
                                                 x2_d[u * 4 + t], x2t[:, slot:slot + 1024])
                W("scalar", "VD", woutln_vd)
                if u >= 1:
                    pass
                sg_store_st = dma("scalar", "SSG", sig_d[u], sgst[:])
                if u < 15:
                    W("vector", "SSG", sg_store_st)

        if max_phase >= 2:
            DR("vector")
            OP("vector", lambda e: e.tensor_reduce(cw[:, 18:19], mx[:, MX_X:MX_X + 64],
                                                   axis=AX.X, op=ALU.max))
            DR("vector")
            mxv = OP("vector", lambda e: e.tensor_copy(cw[:, 18:19], cw[:, 18:19]), "VD")

            cc_state = {"btd": 0}

            def collective(idx, col_ap, after_vd, out_col_ap):
                W("tensor", "VD", after_vd)
                ttd = OP("tensor", lambda e, c=col_ap: e.transpose(
                    Cp[0:1, 0:128], c, identf[:]), "TD")
                W("vector", "TD", ttd)
                DR("vector")
                mv_ = OP("vector", lambda e: e.reduce_max(
                    mcol8[0:1, 0:1], Cp[0:1, 0:128], axis=AX.X), "VD")
                W("gpsimd", "VD", mv_)
                W("gpsimd", "TD", cc_state["btd"])
                gd = dma("gpsimd", "GD", cci[idx], mcol8[:])
                W("gpsimd", "GD", gd)
                ccv = OP("gpsimd", lambda e, i=idx: e.collective_compute(
                    "AllReduce", ALU.max, replica_groups=[list(range(n_cores))],
                    ins=[cci[i].opt()], outs=[cco[i].opt()]), "CC")
                W("gpsimd", "CC", ccv)
                gd = dma("gpsimd", "GD", mg8[:], cco[idx])
                W("tensor", "GD", gd)
                btd = OP("tensor", lambda e: e.matmul(
                    Cp[0:128, 0:1], onesrow[0:1, 0:128], mg8[0:1, 0:1],
                    start=True, stop=True), "TD")
                cc_state["btd"] = btd
                W("vector", "TD", btd)
                vv = OP("vector", lambda e, o=out_col_ap: e.tensor_copy(
                    o, Cp[0:128, 0:1]), "VD")
                return vv

            collective(0, cw[:, 18:19], mxv, cw[:, 19:20])
            DR("vector")
            OP("vector", lambda e: e.reciprocal(cw[:, 20:21], cw[:, 19:20]))
            DR("vector")
            OP("vector", lambda e: e.tensor_scalar(scl[:, SXI:SXI + 1], cw[:, 20:21],
                                                   32767.0, CLIP, ALU.mult, ALU.min))
            DR("vector")
            sc_x_vd = OP("vector", lambda e: e.reciprocal(scl[:, SX:SX + 1],
                                                          scl[:, SXI:SXI + 1]), "VD")

            def barrier():
                vals = dict(cnt)
                for e in ENGINES:
                    for k, v in vals.items():
                        W(e, k, v)
            barrier()

        if max_phase >= 2.2:
            xq_sd = {}
            xq_td = {}
            xq_vd_hist = {}
            NTT_RUN = NTT if max_phase >= 2.5 else 8
            for i in range(NTT_RUN):
                slot = (i % 2) * 1024
                if i >= 2:
                    W("sync", "VD", xq_vd_hist[i - 2])
                ldt = dma("sync", "LX%d" % (i % 2), xin[:, slot:slot + 1024],
                          x2_d[i][:, :])
                W("vector", "LX%d" % (i % 2), ldt)
                W("vector", "VD", sc_x_vd)
                if i >= 1:
                    W("vector", "TD", xq_td[i - 1])
                    DR("vector")
                OP("vector", lambda e, sl=slot: e.tensor_scalar(
                    xt1[:], xin[:, sl:sl + 1024], scl[:, SXI:SXI + 1], None, ALU.mult))
                DR("vector")
                OP("vector", lambda e: e.tensor_scalar(xr2[:], xt1[:], MAGIC, MAGIC,
                                                       ALU.add, ALU.subtract))
                DR("vector")
                xq_vd = OP("vector", lambda e: e.tensor_scalar(
                    xqtm[:], xr2[:], scl[:, SX:SX + 1], None, ALU.mult), "VD")
                xq_vd_hist[i] = xq_vd
                W("tensor", "VD", xq_vd)
                W("tensor", "SD", bp_free_sd)
                for k in range(DC):
                    xq_td[i] = OP("tensor", lambda e, kk=k: e.transpose(
                        Bp[:, kk * 128:kk * 128 + 128],
                        xqtm[:, kk * 128:kk * 128 + 128], ident[:]),
                        "TD" if k == DC - 1 else None)
                W("scalar", "TD", xq_td[i])
                xq_sd[i] = OP("scalar", lambda e, i=i: e.copy(
                    xqfm[:].rearrange("p (k x) -> p k x", x=8192)[:, :, i * 128:i * 128 + 128],
                    Bp[:].rearrange("p (k x) -> p k x", x=128)), "SD")
                bp_free_sd = xq_sd[i]
            xq_all_sd = xq_sd[NTT_RUN - 1]

            bank_vd = [0, 0, 0, 0]
            bank_sd = [0, 0, 0, 0]
            stg_st = [0, 0, 0, 0]
            wq1_done = {}
            wtr_sd_of = {}
            last_mm_td = 0
            mm_last_of = {}
            CC2_RUN = 48 if max_phase >= 2.75 else (4 if max_phase >= 2.6 else 0)

            def srcw(cc2):
                m, wg = cc2 // 2, cc2 % 2
                return (wup_d if wg == 0 else wgate_d)[m * 128:(m + 1) * 128, :]

            def emit_dequant(cc2):
                slot = (cc2 % 2) * 1024
                W("vector", "LWS%d" % (cc2 % 2), ld_of[cc2])
                if cc2 >= 1:
                    W("vector", "TD", wtr_td_of[cc2 - 1])
                DR("vector")
                OP("vector", lambda e, sl=slot: e.tensor_reduce(
                    mxb32[:], wstr[:, sl:sl + 1024].rearrange("p (a b) -> p a b", b=32),
                    axis=AX.X, op=ALU.max, apply_absolute_value=True))
                DR("vector")
                OP("vector", lambda e: e.reciprocal(rmxb[:], mxb32[:]))
                DR("vector")
                OP("vector", lambda e: e.tensor_scalar(sinvb[:], rmxb[:], 7.0,
                                                       CLIP, ALU.mult, ALU.min))
                DR("vector")
                OP("vector", lambda e: e.reciprocal(sblk[:], sinvb[:]))
                DR("vector")
                wq1_done[cc2] = OP("vector", lambda e, sl=slot: e.tensor_tensor(
                    wq1[:].rearrange("p (a b) -> p a b", b=32),
                    wstr[:, sl:sl + 1024].rearrange("p (a b) -> p a b", b=32),
                    sinvb[:].rearrange("p (a b) -> p a b", b=1).broadcast_to([128, 32, 32]),
                    ALU.mult), "VD")
                DR("vector")
                OP("vector", lambda e: e.tensor_scalar(wr2[:], wq1[:], MAGIC, MAGIC,
                                                       ALU.add, ALU.subtract))
                DR("vector")
                wdq_vd = OP("vector", lambda e: e.tensor_tensor(
                    wdqb[:].rearrange("p (a b) -> p a b", b=32),
                    wr2[:].rearrange("p (a b) -> p a b", b=32),
                    sblk[:].rearrange("p (a b) -> p a b", b=1).broadcast_to([128, 32, 32]),
                    ALU.mult), "VD")
                W("tensor", "VD", wdq_vd)
                W("tensor", "SD", bp_state[0])
                for k in range(DC):
                    wtr_td = OP("tensor", lambda e, kk=k: e.transpose(
                        Bp[:, kk * 128:kk * 128 + 128],
                        wdqb[:, kk * 128:kk * 128 + 128], ident[:]),
                        "TD" if k == DC - 1 else None)
                wtr_td_of[cc2] = wtr_td
                W("scalar", "TD", wtr_td)
                if cc2 >= 2:
                    W("scalar", "TD", mm_last_of[cc2 - 2])
                wslot = (cc2 % 2) * 1024
                wtr_sd_of[cc2] = OP("scalar", lambda e, ws=wslot: e.copy(
                    wtr[:, ws:ws + 1024], Bp[:]), "SD")
                bp_state[0] = wtr_sd_of[cc2]

            bp_state = [bp_free_sd]
            wtr_td_of = {}
            ld_of = {}
            if CC2_RUN:
                ld_of[0] = dma("sync", "LWS0", wstr[:, 0:1024], srcw(0))
                emit_dequant(0)
            for cc2 in range(CC2_RUN):
                m, wg = cc2 // 2, cc2 % 2
                wdst = u_d if wg == 0 else gg_d
                mxbase = MX_U if wg == 0 else MX_G
                wslot = (cc2 % 2) * 1024
                for gq in range(NG):
                    bk = gq % 4
                    if gq == 3 and cc2 + 1 < CC2_RUN:
                        W("sync", "VD", wq1_done.get(cc2 - 1, 0))
                        ld_of[cc2 + 1] = dma("sync", "LWS%d" % ((cc2 + 1) % 2),
                                             wstr[:, ((cc2 + 1) % 2) * 1024:
                                                  ((cc2 + 1) % 2) * 1024 + 1024],
                                             srcw(cc2 + 1))
                        emit_dequant(cc2 + 1)
                    W("tensor", "SD", wtr_sd_of[cc2])
                    W("tensor", "VD", bank_vd[bk])
                    W("tensor", "SD", bank_sd[bk])
                    for k in range(DC):
                        mmtd = OP("tensor", lambda e, kk=k, gq=gq, bk=bk, ws=wslot: e.matmul(
                            A[bk][:], wtr[:, ws + kk * 128: ws + kk * 128 + 128],
                            xqfm[:, kk * 8192 + gq * 512: kk * 8192 + (gq + 1) * 512],
                            start=(kk == 0), stop=(kk == DC - 1)),
                            "TD" if k == DC - 1 else None)
                    last_mm_td = mmtd
                    mm_last_of[cc2] = mmtd
                    W("vector", "TD", mmtd)
                    bank_vd[bk] = OP("vector", lambda e, bk=bk, col=mxbase + m * 16 + gq:
                                     e.tensor_reduce(mx[:, col:col + 1], A[bk][:],
                                                     axis=AX.X, op=ALU.max,
                                                     apply_absolute_value=True), "VD")
                    W("scalar", "TD", mmtd)
                    W("scalar", "VD", bank_vd[bk])
                    W("scalar", "SG%d" % bk, stg_st[bk])
                    bank_sd[bk] = OP("scalar", lambda e, bk=bk: e.copy(
                        stg[:, bk * 512:(bk + 1) * 512], A[bk][:]), "SD")
                    W("scalar", "SD", bank_sd[bk])
                    stg_st[bk] = dma("scalar", "SG%d" % bk, wdst[m, gq],
                                     stg[:, bk * 512:(bk + 1) * 512])
            bp_free_sd = bp_state[0]

            W("sync", "VD", CV("VD"))
            W("sync", "TD", CV("TD"))
            wdn_vd = 0
            wdn_td = 0
            WDN_RUN = 16 if max_phase >= 2.9 else (1 if max_phase >= 2.8 else 0)
            for r2 in range(WDN_RUN):
                r, hh = r2 // 2, r2 % 2
                if r2 >= 1:
                    W("sync", "VD", wdn_vd)
                ldw = dma("sync", "LWD", wdnin[:],
                          wdown_d[r * 128:(r + 1) * 128, hh * 1536:(hh + 1) * 1536])
                W("vector", "LWD", ldw)
                if r2 >= 1:
                    W("vector", "TD", wdn_td)
                DR("vector")
                OP("vector", lambda e: e.tensor_reduce(
                    mxb48[:], wdnin[:].rearrange("p (a b) -> p a b", b=32),
                    axis=AX.X, op=ALU.max, apply_absolute_value=True))
                DR("vector")
                OP("vector", lambda e: e.reciprocal(rmx48[:], mxb48[:]))
                DR("vector")
                OP("vector", lambda e: e.tensor_scalar(siv48[:], rmx48[:], 7.0, CLIP,
                                                       ALU.mult, ALU.min))
                DR("vector")
                OP("vector", lambda e: e.reciprocal(sbk48[:], siv48[:]))
                DR("vector")
                OP("vector", lambda e: e.tensor_tensor(
                    wdn1[:].rearrange("p (a b) -> p a b", b=32),
                    wdnin[:].rearrange("p (a b) -> p a b", b=32),
                    siv48[:].rearrange("p (a b) -> p a b", b=1).broadcast_to([128, 48, 32]),
                    ALU.mult))
                DR("vector")
                OP("vector", lambda e: e.tensor_scalar(wdnr[:], wdn1[:], MAGIC, MAGIC,
                                                       ALU.add, ALU.subtract))
                DR("vector")
                wdn_vd = OP("vector", lambda e: e.tensor_tensor(
                    wdnb[:].rearrange("p (a b) -> p a b", b=32),
                    wdnr[:].rearrange("p (a b) -> p a b", b=32),
                    sbk48[:].rearrange("p (a b) -> p a b", b=1).broadcast_to([128, 48, 32]),
                    ALU.mult), "VD")
                W("tensor", "VD", wdn_vd)
                for bb in range(2):
                    nblk = 8 if bb == 0 else 4
                    W("tensor", "SD", bp_free_sd)
                    for k in range(nblk):
                        mblk = bb * 8 + k
                        wdn_td = OP("tensor", lambda e, kk=k, mb=mblk: e.transpose(
                            Bp[:, kk * 128:kk * 128 + 128],
                            wdnb[:, mb * 128:mb * 128 + 128], ident[:]),
                            "TD" if k == nblk - 1 else None)
                    W("scalar", "TD", wdn_td)
                    mg0 = hh * 12 + bb * 8
                    bp_free_sd = OP("scalar", lambda e, mg0=mg0, r=r, nb=nblk: e.copy(
                        wdT[:].rearrange("p (mm x) -> p mm x", x=1024)
                        [:, mg0:mg0 + nb, r * 128:r * 128 + 128],
                        Bp[:, 0:nb * 128].rearrange("p (k x) -> p k x", x=128)), "SD")

        if max_phase >= 3.2:
            DR("vector")
            OP("vector", lambda e: e.tensor_reduce(cw[:, 18:19], mx[:, MX_U:MX_U + 384],
                                                   axis=AX.X, op=ALU.max))
            mug = OP("vector", lambda e: e.tensor_reduce(cw[:, 19:20], mx[:, MX_G:MX_G + 384],
                                                         axis=AX.X, op=ALU.max), "VD")
            DR("vector")
            if max_phase >= 3.4:
                collective(1, cw[:, 18:19], mug, cw[:, 20:21])
            if max_phase >= 3.6:
                collective(2, cw[:, 19:20], CV("VD"), cw[:, 21:22])
            if max_phase >= 3.8:
                DR("vector")
                OP("vector", lambda e: e.reciprocal(cw[:, 22:23], cw[:, 20:21]))
                OP("vector", lambda e: e.reciprocal(cw[:, 23:24], cw[:, 21:22]))
                DR("vector")
                OP("vector", lambda e: e.tensor_scalar(scl[:, SUI:SUI + 1], cw[:, 22:23],
                                                       32767.0, CLIP, ALU.mult, ALU.min))
                OP("vector", lambda e: e.tensor_scalar(scl[:, SGI:SGI + 1], cw[:, 23:24],
                                                       32767.0, CLIP, ALU.mult, ALU.min))
                DR("vector")
                OP("vector", lambda e: e.reciprocal(scl[:, SU:SU + 1], scl[:, SUI:SUI + 1]))
                OP("vector", lambda e: e.reciprocal(scl[:, SG:SG + 1], scl[:, SGI:SGI + 1]))
                DR("vector")
                sc_ug_vd = OP("vector", lambda e: e.tensor_scalar(
                    scl[:, SGQ:SGQ + 1], scl[:, SG:SG + 1], float(2.0 ** -16), None,
                    ALU.mult), "VD")

        if max_phase >= 5:
            barrier()

            p3_st = {}
            p3v1 = {}
            p3v2 = {}
            a3sd3 = {}
            for ci in range(97):
                if ci < 96:
                    m, q4 = ci // 4, ci % 4
                    hf = (ci % 2) * 2048
                    if ci >= 2:
                        W("sync", "VD", p3v1[ci - 2])
                    ldg = dma("sync", "LG%d" % (ci % 2), gin[:, hf:hf + 2048]
                              .rearrange("p (a b) -> p a b", b=512),
                              gg_d[m, q4 * 4:(q4 + 1) * 4].rearrange("a p b -> p a b"))
                    W("vector", "LG%d" % (ci % 2), ldg)
                    W("vector", "VD", sc_ug_vd)
                    if ci >= 2:
                        W("vector", "SD", a3sd3[ci - 2])
                    DR("vector")
                    OP("vector", lambda e, h=hf: e.tensor_scalar(
                        tb1[:, h:h + 2048], gin[:, h:h + 2048],
                        scl[:, SGI:SGI + 1], None, ALU.mult))
                    DR("vector")
                    p3v1[ci] = OP("vector", lambda e, h=hf: e.tensor_scalar(
                        rb3[:, h:h + 2048], tb1[:, h:h + 2048], MAGIC, MAGIC,
                        ALU.add, ALU.subtract), "VD")
                    W("scalar", "VD", p3v1[ci])
                    DR("scalar")
                    OP("scalar", lambda e, h=hf: e.activation(
                        sigb[:, h:h + 2048], rb3[:, h:h + 2048], AF.Sigmoid,
                        scale=scl[:, SG:SG + 1]))
                    DR("scalar")
                    OP("scalar", lambda e, h=hf: e.activation(
                        tb1[:, h:h + 2048], sigb[:, h:h + 2048], AF.Identity,
                        scale=65536.0, bias=cmag[:, 0:1]))
                    DR("scalar")
                    a3sd3[ci] = OP("scalar", lambda e, h=hf: e.activation(
                        a3b[:, h:h + 2048], tb1[:, h:h + 2048], AF.Identity,
                        bias=cmag[:, 1:2]), "SD")
                if ci >= 1:
                    j = ci - 1
                    mj, q4j = j // 4, j % 4
                    hj = (j % 2) * 2048
                    W("vector", "SD", a3sd3[j])
                    OP("vector", lambda e, h=hj: e.tensor_scalar(
                        c2b[:, h:h + 2048], a3b[:, h:h + 2048], 65535.0,
                        scl[:, SGQ:SGQ + 1], ALU.min, ALU.mult))
                    DR("vector")
                    W("vector", "STB%d" % (j % 2), p3_st.get(j - 2, 0))
                    OP("vector", lambda e, h=hj: e.tensor_tensor(
                        tob[:, h:h + 2048], rb3[:, h:h + 2048],
                        c2b[:, h:h + 2048], ALU.mult))
                    DR("vector")
                    p3v2[j] = OP("vector", lambda e, j=j, h=hj: e.tensor_reduce(
                        mx[:, MX_T + j:MX_T + j + 1], tob[:, h:h + 2048],
                        axis=AX.X, op=ALU.max, apply_absolute_value=True), "VD")
                    W("scalar", "VD", p3v2[j])
                    p3_st[j] = dma("scalar", "STB%d" % (j % 2),
                                   t_d[mj, q4j * 4:(q4j + 1) * 4].rearrange("a p b -> p a b"),
                                   tob[:, hj:hj + 2048].rearrange("p (a b) -> p a b", b=512))

            DR("vector")
            mt = OP("vector", lambda e: e.tensor_reduce(cw[:, 18:19], mx[:, MX_T:MX_T + 96],
                                                        axis=AX.X, op=ALU.max), "VD")
            DR("vector")
            collective(3, cw[:, 18:19], mt, cw[:, 19:20])
            DR("vector")
            OP("vector", lambda e: e.reciprocal(cw[:, 20:21], cw[:, 19:20]))
            DR("vector")
            OP("vector", lambda e: e.tensor_scalar(scl[:, STI:STI + 1], cw[:, 20:21],
                                                   32767.0, CLIP, ALU.mult, ALU.min))
            DR("vector")
            OP("vector", lambda e: e.reciprocal(cw[:, 21:22], scl[:, STI:STI + 1]))
            DR("vector")
            sc_t_vd = OP("vector", lambda e: e.tensor_tensor(
                scl[:, STU:STU + 1], cw[:, 21:22], scl[:, SU:SU + 1], ALU.mult), "VD")

        if max_phase >= 6:
            barrier()

            p4_st = {}
            p4v1 = {}
            p4v2 = {}
            rusd4 = {}
            for ci in range(97):
                if ci < 96:
                    m, q4 = ci // 4, ci % 4
                    hf = (ci % 2) * 2048
                    if ci >= 2:
                        W("sync", "VD", p4v1[ci - 2])
                        W("sync", "SD", rusd4[ci - 2])
                    ldt = dma("sync", "LG%d" % (ci % 2), gin[:, hf:hf + 2048]
                              .rearrange("p (a b) -> p a b", b=512),
                              t_d[m, q4 * 4:(q4 + 1) * 4].rearrange("a p b -> p a b"))
                    ldu = dma("sync", "LU%d" % (ci % 2), uin[:, hf:hf + 2048]
                              .rearrange("p (a b) -> p a b", b=512),
                              u_d[m, q4 * 4:(q4 + 1) * 4].rearrange("a p b -> p a b"))
                    W("vector", "LG%d" % (ci % 2), ldt)
                    W("vector", "VD", sc_t_vd)
                    if ci >= 2:
                        W("vector", "SD", rusd4[ci - 2])
                    DR("vector")
                    OP("vector", lambda e, h=hf: e.tensor_scalar(
                        tb1[:, h:h + 2048], gin[:, h:h + 2048],
                        scl[:, STI:STI + 1], None, ALU.mult))
                    DR("vector")
                    OP("vector", lambda e, h=hf: e.tensor_scalar(
                        rb3[:, h:h + 2048], tb1[:, h:h + 2048], MAGIC, MAGIC,
                        ALU.add, ALU.subtract))
                    DR("vector")
                    p4v1[ci] = OP("vector", lambda e, h=hf: e.tensor_scalar(
                        c2b[:, h:h + 2048], rb3[:, h:h + 2048],
                        scl[:, STU:STU + 1], None, ALU.mult), "VD")
                    W("scalar", "LU%d" % (ci % 2), ldu)
                    W("scalar", "VD", p4v1[ci])
                    DR("scalar")
                    OP("scalar", lambda e, h=hf: e.activation(
                        tb1[:, h:h + 2048], uin[:, h:h + 2048], AF.Identity,
                        scale=scl[:, SUI:SUI + 1], bias=cmag[:, 0:1]))
                    DR("scalar")
                    rusd4[ci] = OP("scalar", lambda e, h=hf: e.activation(
                        a3b[:, h:h + 2048], tb1[:, h:h + 2048], AF.Identity,
                        bias=cmag[:, 1:2]), "SD")
                if ci >= 1:
                    j = ci - 1
                    mj, q4j = j // 4, j % 4
                    hj = (j % 2) * 2048
                    W("vector", "SD", rusd4[j])
                    W("vector", "STB%d" % (j % 2), p4_st.get(j - 2, 0))
                    OP("vector", lambda e, h=hj: e.tensor_tensor(
                        tob[:, h:h + 2048], c2b[:, h:h + 2048],
                        a3b[:, h:h + 2048], ALU.mult))
                    DR("vector")
                    p4v2[j] = OP("vector", lambda e, j=j, h=hj: e.tensor_reduce(
                        mx[:, MX_O + j:MX_O + j + 1], tob[:, h:h + 2048],
                        axis=AX.X, op=ALU.max, apply_absolute_value=True), "VD")
                    W("scalar", "VD", p4v2[j])
                    p4_st[j] = dma("scalar", "STB%d" % (j % 2),
                                   o_d[mj, q4j * 4:(q4j + 1) * 4].rearrange("a p b -> p a b"),
                                   tob[:, hj:hj + 2048].rearrange("p (a b) -> p a b", b=512))

            DR("vector")
            mo = OP("vector", lambda e: e.tensor_reduce(cw[:, 18:19], mx[:, MX_O:MX_O + 96],
                                                        axis=AX.X, op=ALU.max), "VD")
            DR("vector")
            collective(4, cw[:, 18:19], mo, cw[:, 19:20])
            DR("vector")
            OP("vector", lambda e: e.reciprocal(cw[:, 20:21], cw[:, 19:20]))
            DR("vector")
            OP("vector", lambda e: e.tensor_scalar(scl[:, SOI:SOI + 1], cw[:, 20:21],
                                                   32767.0, CLIP, ALU.mult, ALU.min))
            DR("vector")
            sc_o_vd = OP("vector", lambda e: e.reciprocal(scl[:, SO:SO + 1],
                                                          scl[:, SOI:SOI + 1]), "VD")

        if max_phase >= 7:
            barrier()

            dn_td = {}
            out_st = {}
            ln_vd = 0
            ln_vd_par = {}
            x2g_vd_par = {}
            x2g_vd = 0
            oq_vd_hist = {}
            oq_sd_hist = {}
            oq_last_vd = {}
            ld_oq = {}

            def emit_oq_prep(g, mlist):
                half = (g % 2) * 12288
                for m in mlist:
                    slot = (m % 8) * 512
                    if (g, m - 8) in oq_sd_hist:
                        W("sync", "SD", oq_sd_hist[(g, m - 8)])
                    elif (g - 1, 16 + m) in oq_sd_hist:
                        W("sync", "SD", oq_sd_hist[(g - 1, 16 + m)])
                    ldo = dma("sync", "LO%d" % (m % 8), gin[:, slot:slot + 512],
                              o_d[m, g])
                    W("scalar", "LO%d" % (m % 8), ldo)
                    W("scalar", "VD", sc_o_vd)
                    DR("scalar")
                    OP("scalar", lambda e, sl=slot: e.activation(
                        ot1[:], gin[:, sl:sl + 512], AF.Identity,
                        scale=scl[:, SOI:SOI + 1], bias=cmag[:, 0:1]))
                    DR("scalar")
                    if (g, m - 2) in oq_vd_hist:
                        W("scalar", "VD", oq_vd_hist[(g, m - 2)])
                    elif (g - 1, 22 + m) in oq_vd_hist:
                        W("scalar", "VD", oq_vd_hist[(g - 1, 22 + m)])
                    o2s = (m % 2) * 512
                    osd = OP("scalar", lambda e, o2s=o2s: e.activation(
                        ot2[:, o2s:o2s + 512], ot1[:], AF.Identity, bias=cmag[:, 1:2]), "SD")
                    W("vector", "SD", osd)
                    if g >= 2 and m == 0:
                        W("vector", "TD", dn_td[(g - 2, NT - 1)])
                    DR("vector")
                    oq_sd_hist[(g, m)] = osd
                    oq_vd_hist[(g, m)] = OP("vector", lambda e, m=m, hh=half, o2s=o2s: e.tensor_scalar(
                        oqb[:, hh + m * 512: hh + (m + 1) * 512], ot2[:, o2s:o2s + 512],
                        scl[:, SO:SO + 1], None, ALU.mult), "VD")
                oq_last_vd[g] = oq_vd_hist[(g, mlist[-1])] if mlist else oq_last_vd.get(g, 0)

            emit_oq_prep(0, list(range(FC)))
            for gq in range(NG):
                half = (gq % 2) * 12288
                xgh = (gq % 2) * 4096
                if gq >= 2:
                    W("sync", "VD", x2g_vd_par[gq - 2])
                ldx = dma("sync", "LXG",
                          x2g[:, xgh:xgh + 4096].rearrange("p (t d) -> p t d", d=1024),
                          x2_d[gq * 4:(gq + 1) * 4].rearrange("t p d -> p t d"))
                lds = dma("sync", "LXG", sg4b[:, (gq % 2) * 8:(gq % 2) * 8 + 8], sig_d[gq])
                for tt in range(NT):
                    bw = (tt % 2) * 2
                    W("tensor", "VD", oq_last_vd[gq])
                    kprev = (gq, tt - 2) if tt >= 2 else (gq - 1, tt + 2)
                    if kprev in ln_vd_par:
                        W("tensor", "VD", ln_vd_par[kprev])
                    for m in range(FC):
                        for jj in range(2):
                            dn_td[(gq, tt)] = OP("tensor", lambda e, m=m, tt=tt, j=jj, hh=half, bw=bw:
                                                 e.matmul(
                                A[bw + j][:], oqb[:, hh + m * 512 + tt * 128:
                                             hh + m * 512 + tt * 128 + 128],
                                wdT[:, m * 1024 + j * 512: m * 1024 + (j + 1) * 512],
                                start=(m == 0), stop=(m == FC - 1)),
                                "TD" if (m == FC - 1 and jj == 1) else None)
                    W("vector", "TD", dn_td[(gq, tt)])
                    W("vector", "LXG", lds)
                    DR("vector")
                    DR("scalar")
                    OP("vector", lambda e, bw=bw: e.bn_stats(st6[:, 0:6], A[bw][:]))
                    OP("vector", lambda e, bw=bw: e.bn_stats(st6[:, 6:12], A[bw + 1][:]))
                    DR("vector")
                    OP("vector", lambda e: e.bn_aggr(mvt[:], st6[:]))
                    DR("vector")
                    OP("vector", lambda e: e.tensor_scalar(cw[:, 10:11], mvt[:, 1:2],
                                                           EPS_LN, None, ALU.add))
                    DR("vector")
                    vv = OP("vector", lambda e: e.reciprocal(cw[:, 11:12], cw[:, 10:11]), "VD")
                    W("scalar", "VD", vv)
                    sv = OP("scalar", lambda e: e.activation(cw[:, 12:13], cw[:, 11:12],
                                                             AF.Sqrt), "SD")
                    W("vector", "SD", sv)
                    for half2 in range(2):
                        OP("vector", lambda e, hh=half2, bw=bw: e.tensor_scalar(
                            lnb[:, hh * 512:(hh + 1) * 512], A[bw + hh][:],
                            mvt[:, 0:1], cw[:, 12:13], ALU.subtract, ALU.mult))
                    OP("vector", lambda e, tt=tt, xgh=xgh, sb=(gq % 2) * 8: e.tensor_scalar(
                        x1b[:], x2g[:, xgh + tt * 1024: xgh + (tt + 1) * 1024],
                        sg4b[:, sb + tt * 2:sb + tt * 2 + 1],
                        sg4b[:, sb + tt * 2 + 1:sb + tt * 2 + 2],
                        ALU.mult, ALU.add))
                    DR("vector")
                    oslot = (tt % 2) * 1024
                    prev_o = (gq, tt - 2) if tt >= 2 else (gq - 1, tt + 2)
                    W("vector", "SY%d" % (tt % 2), out_st.get(prev_o, 0))
                    x2g_vd = ln_vd = OP("vector", lambda e, osl=oslot: e.tensor_tensor(
                        outb[:, osl:osl + 1024], lnb[:], x1b[:], ALU.add), "VD")
                    ln_vd_par[(gq, tt)] = ln_vd
                    x2g_vd_par[gq] = ln_vd
                    W("scalar", "VD", ln_vd)
                    tok0 = gq * 512 + tt * 128
                    out_st[(gq, tt)] = dma("scalar", "SY%d" % (tt % 2),
                                           y_d[tok0 // 2048,
                                               tok0 % 2048: tok0 % 2048 + 128, :],
                                           outb[:, oslot:oslot + 1024])
                    if gq + 1 < NG and tt < 2:
                        emit_oq_prep(gq + 1, list(range(tt * 12, tt * 12 + 12)))

        for k, v in dict(cnt).items():
            W("sync", k, v)

        waited = {}
        for e in ENGINES:
            for item in prog[e]:
                if item[0] == "w":
                    waited.setdefault(item[1], set()).add(item[2])
        dma_sems = {s for s in cnt if s not in ("TD", "VD", "SD", "PD", "CC")}
        run_cnt = {}
        for e in ENGINES:
            newops = []
            pend = {}
            for item in prog[e]:
                if item[0] == "o":
                    _, fn, inc, n = item
                    if inc is not None and inc not in dma_sems:
                        v = run_cnt.get(inc, 0) + n
                        run_cnt[inc] = v
                        if v in waited.get(inc, ()):
                            newops.append(("o", fn, inc, n + pend.pop(inc, 0)))
                        else:
                            pend[inc] = pend.get(inc, 0) + n
                            newops.append(("o", fn, None, 0))
                        continue
                newops.append(item)
            assert not pend or all(v == 0 for v in pend.values()) or True
            prog[e] = newops

        def runner(eng_name):
            def run(eng):
                lastw = {}
                for item in prog[eng_name]:
                    if item[0] == "w":
                        if lastw.get(item[1], -1) >= item[2]:
                            continue
                        lastw[item[1]] = item[2]
                        eng.wait_ge(sems[item[1]], item[2])
                    elif item[0] == "d":
                        eng.drain()
                    else:
                        _, fn, inc, n = item
                        instr = fn(eng)
                        if inc is not None:
                            instr.then_inc(sems[inc], n)
            return run

        blk.sync(runner("sync"))
        blk.tensor(runner("tensor"))
        blk.vector(runner("vector"))
        blk.scalar(runner("scalar"))
        blk.gpsimd(runner("gpsimd"))

    return nc


def make_in_maps(inputs, n_cores=8):
    consts = host_constants()
    perm = qk_row_perm()
    wqkv = np.ascontiguousarray(inputs["L0_attn_w_qkv"][perm])
    g8 = np.ascontiguousarray(inputs["L0_attn_g"].reshape(DC, 128).T)
    pm = inputs["L0_attn_pm"]
    pmkf = np.empty((128, 256), dtype=np.float32)
    for h in range(H):
        blkv = pm[0, h].T
        pmkf[0:64, h * 16:(h + 1) * 16] = blkv
        pmkf[64:128, h * 16:(h + 1) * 16] = blkv
    per_head = np.concatenate([np.arange(0, DH, 2), np.arange(1, DH, 2)])
    pmkf_p = np.empty_like(pmkf)
    pmkf_p[0:64] = pmkf[0:64][per_head]
    pmkf_p[64:128] = pmkf[64:128][per_head]
    pmvst = np.zeros((16, 1040), dtype=np.float32)
    for h in range(H):
        pmvst[:, h * 65: h * 65 + 64] = pm[1, h]
        pmvst[:, h * 65 + 64] = 1.0
    seq = inputs["L0_attn_seq"]
    maps = []
    for c in range(n_cores):
        maps.append({
            "seq": np.ascontiguousarray(seq[c * NB:(c + 1) * NB]),
            "wqkv": wqkv,
            "wout": np.ascontiguousarray(inputs["L0_attn_w_out"]),
            "wup": np.ascontiguousarray(inputs["L0_ffn_w_up"]),
            "wgate": np.ascontiguousarray(inputs["L0_ffn_w_gate"]),
            "wdown": np.ascontiguousarray(inputs["L0_ffn_w_down"]),
            "gvec": g8,
            "pmkf": pmkf_p,
            "pmvst": pmvst,
            "identc": consts["ident"],
            "identf": np.eye(128, dtype=np.float32),
            "maskmul": consts["maskmul"],
            "costab": consts["costab"],
            "sintab": consts["sintab"],
            "indsel": consts["indsel"],
        })
    return maps


def kernel(**inputs):
    n_cores = 8
    maps = make_in_maps(inputs, n_cores)
    nc = build_nc(n_cores)
    res = run_bass_kernel_spmd(nc, maps, core_ids=list(range(n_cores)))
    out = np.concatenate([res.results[c]["y"] for c in range(n_cores)], axis=0)
    return out.astype(np.float32)
